# version 10
# baseline (speedup 1.0000x reference)
import numpy as np
from contextlib import ExitStack
import concourse.bass as bass
import concourse.mybir as mybir

F32 = mybir.dt.float32
BF16 = mybir.dt.bfloat16
AF = mybir.ActivationFunctionType
ALU = mybir.AluOpType
AX = mybir.AxisListType

ENGINES = ("sync", "gpsimd", "tensor", "vector", "scalar")

_AUTOSYNC_OPS = frozenset((
    "tensor_tensor", "tensor_add", "tensor_sub", "tensor_mul", "tensor_max",
    "tensor_scalar", "tensor_single_scalar", "tensor_scalar_add",
    "tensor_scalar_sub", "tensor_scalar_mul", "tensor_scalar_max",
    "tensor_scalar_min", "tensor_relu", "tensor_copy", "tensor_reduce",
    "reduce_max", "reduce_sum", "scalar_tensor_tensor", "tensor_tensor_scan",
    "memset", "memzero", "reciprocal", "stream_shuffle", "select",
    "copy_predicated", "transpose", "bn_stats", "bn_aggr", "max", "max_index",
    "activation", "sign", "square", "sqrt", "mul", "copy", "add", "iota",
    "affine_select", "partition_broadcast", "partition_all_reduce",
))


class Sem:
    __slots__ = ("h", "n", "name")

    def __init__(self, h, name):
        self.h = h
        self.n = 0
        self.name = name

    @property
    def now(self):
        return (self, self.n)


class AutoSyncEngine:

    def __init__(self, eng, sem):
        self._eng = eng
        self._sem = sem
        self._n = 0

    def wait_ge(self, sem, v):
        return self._eng.wait_ge(sem, v)

    def __getattr__(self, name):
        attr = getattr(self._eng, name)
        if name not in _AUTOSYNC_OPS:
            return attr

        def wrapped(*a, **k):
            if self._n > 0:
                self._eng.wait_ge(self._sem, self._n)
            ins = attr(*a, **k)
            ins.then_inc(self._sem, 1)
            self._n += 1
            return ins

        return wrapped


class Prog:
    AUTO = ("vector", "scalar", "gpsimd")

    def __init__(self, nc, es: ExitStack):
        self.nc = nc
        self.es = es
        self.steps = {e: [] for e in ENGINES}
        self.waited = {e: {} for e in ENGINES}
        self.sv = self.sem("auto_v")
        self.ss = self.sem("auto_s")
        self.sg = self.sem("auto_g")
        self._auto = {"vector": self.sv, "scalar": self.ss, "gpsimd": self.sg}

    def sem(self, name):
        return Sem(self.es.enter_context(self.nc.semaphore(name)), name)

    def sbuf(self, name, shape, dtype):
        return self.es.enter_context(self.nc.sbuf_tensor(name, shape, dtype))

    def psum(self, name, shape, dtype=F32):
        return self.es.enter_context(self.nc.psum_tensor(name, shape, dtype))

    def on(self, engine, fn, wait=(), inc=(), n=1):
        w = []
        seen = self.waited[engine]
        for s, v in wait:
            if v > 0 and seen.get(id(s), -1) < v:
                w.append((s, v))
                seen[id(s)] = v
        if engine in self._auto and not inc:
            s = self._auto[engine]
            s.n += n
            self.steps[engine].append((w, fn, (), s.n))
            return s.n
        for s, k in inc:
            s.n += k
        self.steps[engine].append((w, fn, list(inc), None))
        return None

    def dma(self, engine, out, in_, sem, wait=()):
        self.on(engine, lambda e, o=out, i=in_: e.dma_start(out=o, in_=i),
                wait=wait, inc=[(sem, 16)])

    def replay(self, engine, eng):
        if engine in self._auto:
            proxy = AutoSyncEngine(eng, self._auto[engine].h)
            for w, fn, incs, after in self.steps[engine]:
                for s, v in w:
                    eng.wait_ge(s.h, v)
                ins = fn(proxy)
                if after is None:
                    for s, k in incs:
                        ins = ins.then_inc(s.h, k)
                else:
                    assert proxy._n == after, (
                        f"{engine} step declared wrong n: have {proxy._n}, expected {after}")
            return
        for w, fn, incs, _ in self.steps[engine]:
            for s, v in w:
                eng.wait_ge(s.h, v)
            ins = fn(eng)
            for s, k in incs:
                assert ins is not None, f"step on {engine} has inc but returned None"
                ins = ins.then_inc(s.h, k)

    def finalize(self):
        with self.nc.Block() as block:
            @block.sync
            def _(eng):
                self.replay("sync", eng)

            @block.gpsimd
            def _(eng):
                self.replay("gpsimd", eng)

            @block.tensor
            def _(eng):
                self.replay("tensor", eng)

            @block.vector
            def _(eng):
                self.replay("vector", eng)

            @block.scalar
            def _(eng):
                self.replay("scalar", eng)


def make_consts():
    ident = np.eye(128, dtype=np.float32)
    import ml_dtypes
    consts = {
        "c_ident_f32": ident,
        "c_ident_bf": ident.astype(ml_dtypes.bfloat16),
        "c_ident2": np.vstack([np.eye(64), np.eye(64)]).astype(ml_dtypes.bfloat16),
        "c_ones_row": np.ones((1, 128), dtype=np.float32),
        "c_ones_col": np.ones((128, 1), dtype=np.float32),
        "c_pm": np.tile(np.array([[-1.0], [1.0]], np.float32), (64, 1)),
        "c_revidx": np.tile(np.arange(8, 0, -1, dtype=np.float32)[None, :], (128, 1)),
    }
    return consts


SHUF_SWAP = [i ^ 1 for i in range(32)]


"""Fused single-launch megakernel: all 4 blocks in one raw-Bass program.

Per core: X (feature-major residual) stays in SBUF across blocks; weights
stream from HBM; x loaded once, output stored once. Block scratch tensors are
views of one byte arena (reset between blocks, with engine barriers).
"""
import ml_dtypes
from concourse.bass_utils import run_bass_kernel_spmd

U8 = mybir.dt.uint8
T_SEQ = 2048
D = 1024
NG = 8


class Arena:
    def __init__(self, p, name, nbytes):
        self.t = p.sbuf(name, [128, nbytes], U8)
        self.off = 0

    def take(self, shape, dtype):
        bpe = 4 if dtype == F32 else 2
        free = 1
        for s in shape[1:]:
            free *= s
        nb = free * bpe
        off = self.off
        self.off = (self.off + nb + 31) & ~31
        v = self.t[:, off:off + nb].bitcast(dtype)
        if len(shape) == 3:
            v = v.rearrange("p (a b) -> p a b", a=shape[1])
        elif len(shape) == 4:
            v = v.rearrange("p (a b c) -> p a b c", a=shape[1], b=shape[2])
        if shape[0] != 128:
            v = v[0:shape[0]]
        return v

    def reset(self):
        self.off = 0


def build_fused():
    nc = bass.Bass("TRN2", target_bir_lowering=False, debug=False)
    dxin = nc.dram_tensor("xin", [T_SEQ, D], F32, kind="ExternalInput").ap()
    dxout = nc.dram_tensor("xout", [T_SEQ, D], F32, kind="ExternalOutput").ap()
    dw = {}
    for w in ("wqT", "wqTs", "wkT", "wvT", "woT"):
        dw[w] = nc.dram_tensor(w, [D, D], F32, kind="ExternalInput").ap()
    dcols = nc.dram_tensor("cols", [128, 72], F32, kind="ExternalInput").ap()
    drw = nc.dram_tensor("rw", [D, 8], F32, kind="ExternalInput").ap()
    dw1 = nc.dram_tensor("w1", [8, D, 2816], F32, kind="ExternalInput").ap()
    dw2 = nc.dram_tensor("w2", [8, 1408, D], F32, kind="ExternalInput").ap()
    dwin = nc.dram_tensor("winT", [32, D, 128], F32, kind="ExternalInput").ap()
    dwout = nc.dram_tensor("woutT", [NG, 2048, 128], F32, kind="ExternalInput").ap()
    dwx = nc.dram_tensor("wxT", [2048, 96], F32, kind="ExternalInput").ap()
    dwdt = nc.dram_tensor("wdtT", [64, 2048], F32, kind="ExternalInput").ap()
    dalog = nc.dram_tensor("alog", [128, 256], F32, kind="ExternalInput").ap()
    dmcols = nc.dram_tensor("mcols", [128, 144], F32, kind="ExternalInput").ap()
    dgw = nc.dram_tensor("gw", [D, 8], F32, kind="ExternalInput").ap()
    dgb = nc.dram_tensor("gb", [1, 8], F32, kind="ExternalInput").ap()
    dew = nc.dram_tensor("ew", [8, D, D], F32, kind="ExternalInput").ap()
    deb = nc.dram_tensor("eb", [8, D], F32, kind="ExternalInput").ap()
    dconst = {}
    for cn, sh, dt_ in (("c_ident_f32", [128, 128], F32), ("c_ident_bf", [128, 128], BF16),
                        ("c_ones_row", [1, 128], F32), ("c_ones_col", [128, 1], F32),
                        ("c_pm", [128, 1], F32), ("c_revidx", [128, 8], F32),
                        ("c_sel", [8, 1024], F32), ("c_selb", [8, 1024], BF16),
                        ("c_sel16b", [16, 2048], BF16)):
        dconst[cn] = nc.dram_tensor(cn, sh, dt_, kind="ExternalInput").ap()

    es = ExitStack()
    with es:
        p = Prog(nc, es)
        X = p.sbuf("X", [128, NG, T_SEQ], F32)
        T2 = p.sbuf("T2", [128, 1024], F32)
        ident = p.sbuf("ident", [128, 128], F32)
        ident_bf = p.sbuf("ident_bf", [128, 128], BF16)
        ones_row = p.sbuf("ones_row", [1, 128], F32)
        ones_col = p.sbuf("ones_col", [128, 1], F32)
        pm = p.sbuf("pm", [128, 1], F32)
        revidx = p.sbuf("revidx", [128, 8], F32)
        selb = p.sbuf("selb", [8, 1024], BF16)
        cols = p.sbuf("scols", [128, 72], F32)
        ssgn = p.sbuf("ssgn", [128, 8], F32)
        gbs = p.sbuf("gbs", [1, 8], F32)
        r1 = p.sbuf("r1", [1, 512], F32)
        r2 = p.sbuf("r2", [1, 512], F32)
        r3 = p.sbuf("r3", [1, 512], F32)
        lg = p.sbuf("lg", [128, 16, 8], F32)
        nm = p.sbuf("nm", [128, 1], F32)
        rsum = p.sbuf("rsum", [128, 1], F32)
        m1 = p.sbuf("m1", [128, 1], F32)
        sc1 = p.sbuf("sc1", [128, 8], F32)
        sc2 = p.sbuf("sc2", [128, 8], F32)
        AR = Arena(p, "ARENA", 132608)
        import os as _os
        _dbg = _os.environ.get("ARDBG")

        P6 = p.psum("P6", [128, 3072], F32)
        PTf = p.psum("PTf", [128, 512], F32)
        PTb = p.psum("PTb", [128, 1024], BF16)

        ld = p.sem("ld")
        ldg = p.sem("ldg")
        ldw = [p.sem("ldw0"), p.sem("ldw1")]
        w1s = p.sem("w1s")
        w2s = p.sem("w2s")
        lwx = p.sem("lwx")
        lwin = [p.sem("lwin0"), p.sem("lwin1")]
        lwout = [p.sem("lwout0"), p.sem("lwout1")]
        ews = p.sem("ews")
        pe = p.sem("pe")
        ve = p.sv
        se = p.ss
        sg = p.sg
        st = p.sem("st")

        bcol = lambda i, go: cols[:, 8 * i + go:8 * i + go + 1]
        CONVW, CONVB, BDT, DP, LNG, LNB = 0, 64, 80, 96, 112, 120
        EPS1r = cols[0:1, 64:65]

        def barrier():
            targets = [pe.now, ve.now, se.now, ld.now, w1s.now, w2s.now, lwx.now,
                       ldw[0].now, ldw[1].now, lwin[0].now, lwin[1].now,
                       lwout[0].now, ews.now, ldg.now]
            targets = [t for t in targets if t[1] > 0]
            for eng in ("sync", "gpsimd", "tensor", "vector", "scalar"):
                p.on(eng, lambda e: None, wait=list(targets), n=0)

        for cn, t in (("c_ident_f32", ident), ("c_ident_bf", ident_bf),
                      ("c_ones_row", ones_row), ("c_ones_col", ones_col),
                      ("c_pm", pm), ("c_revidx", revidx),
                      ("c_selb", selb)):
            p.dma("sync", t[:], dconst[cn], ld)
        p.dma("sync", cols[:], dcols, ld)
        p.dma("sync", gbs[:], dgb, ld)
        consts_ld = ld.now
        p.on("vector", lambda e: e.tensor_scalar(ssgn[:], cols[:, 40:48], pm[:], None, ALU.mult),
             wait=[consts_ld], n=1)

        xt = T2
        tr_done = {}
        cp_done = {}
        for tt in range(16):
            wl = [consts_ld] if tt < 1 else [(pe, tr_done[tt - 1])]
            p.dma("sync", xt[:], dxin[tt * 128:(tt + 1) * 128, :], ld, wait=wl)
            ldv = ld.now
            for half in range(2):
                def trb(e, tt=tt, half=half):
                    ins = None
                    for g4 in range(4):
                        g = half * 4 + g4
                        ins = e.transpose(PTf[:, g4 * 128:(g4 + 1) * 128],
                                          xt[:, g * 128:(g + 1) * 128], ident[:])
                    return ins
                w = [ldv]
                if tt > 0 or half > 0:
                    w.append((ve, cp_done.get((tt, half - 1), cp_done.get((tt - 1, 1), 0))))
                p.on("tensor", trb, wait=w, inc=[(pe, 1)])
                def cpb(e, tt=tt, half=half):
                    ins = None
                    for g4 in range(4):
                        g = half * 4 + g4
                        ins = e.tensor_copy(X[:, g, tt * 128:(tt + 1) * 128],
                                            PTf[:, g4 * 128:(g4 + 1) * 128])
                    return ins
                p.on("vector", cpb, wait=[pe.now], n=4)
                cp_done[(tt, half)] = ve.n
            tr_done[tt] = pe.n
        x_loaded = ve.now

        def lnchunk(src_g, eps_ap, tc, pa0, gate_wait):
            def mmu(e):
                ins = None
                for g in range(NG):
                    ins = e.matmul(P6[0:1, pa0:pa0 + tc], ones_col[:], src_g(g),
                                   start=(g == 0), stop=(g == NG - 1))
                return ins
            p.on("tensor", mmu, wait=gate_wait, inc=[(pe, 1)])
            p.on("vector", lambda e: e.tensor_scalar(r1[0:1, 0:tc], P6[0:1, pa0:pa0 + tc],
                                                     1.0 / D, None, ALU.mult),
                 wait=[pe.now], n=1)
            mu_ev = ve.now
            sq_done = 0
            for g in range(NG):
                w = list(gate_wait) if g < 2 else [(pe, sq_done)]
                p.on("scalar", lambda e, g=g: e.activation(
                    T2[:, (g % 2) * tc:(g % 2) * tc + tc], src_g(g), AF.Square),
                    wait=w + [mu_ev], n=1)
                p.on("tensor", lambda e, g=g: e.matmul(
                    P6[0:1, pa0 + tc:pa0 + 2 * tc], ones_col[:],
                    T2[:, (g % 2) * tc:(g % 2) * tc + tc],
                    start=(g == 0), stop=(g == NG - 1)),
                     wait=[se.now, mu_ev], inc=[(pe, 1)])
                if g >= 1:
                    sq_done = pe.n - 1
            p.on("vector", lambda e: e.tensor_scalar(r2[0:1, 0:tc], P6[0:1, pa0 + tc:pa0 + 2 * tc],
                                                     1.0 / D, None, ALU.mult),
                 wait=[pe.now], n=1)
            p.on("vector", lambda e: e.tensor_tensor(r3[0:1, 0:tc], r1[0:1, 0:tc], r1[0:1, 0:tc],
                                                     ALU.mult), n=1)
            p.on("vector", lambda e: e.tensor_tensor(r3[0:1, 0:tc], r2[0:1, 0:tc], r3[0:1, 0:tc],
                                                     ALU.subtract), n=1)
            p.on("scalar", lambda e: e.activation(r2[0:1, 0:tc], r3[0:1, 0:tc], AF.Sqrt, bias=eps_ap),
                 wait=[ve.now], n=1)
            p.on("vector", lambda e: e.reciprocal(r3[0:1, 0:tc], r2[0:1, 0:tc]), wait=[se.now], n=1)
            p.on("vector", lambda e: e.scalar_tensor_tensor(r1[0:1, 0:tc], r1[0:1, 0:tc], -1.0,
                                                            r3[0:1, 0:tc], ALU.mult, ALU.mult), n=1)
            rows_done = ve.now
            def mbc(e):
                e.matmul(P6[:, pa0:pa0 + tc], ones_row[:], r3[0:1, 0:tc], start=True, stop=True)
                return e.matmul(P6[:, pa0 + tc:pa0 + 2 * tc], ones_row[:], r1[0:1, 0:tc],
                                start=True, stop=True)
            p.on("tensor", mbc, wait=[rows_done], inc=[(pe, 1)])
            return pe.now

        NCB = 16
        Hg = AR.take([128, NG, 512], BF16)
        wbufH = [AR.take([128, 4, D], BF16), AR.take([128, 4, D], BF16)]
        Qg = AR.take([128, NG, 512], BF16)
        Qs = AR.take([128, NG, 512], BF16)
        Kg = AR.take([128, NG, 512], BF16)
        Vg = AR.take([128, NG, 512], BF16)
        e1a = AR.take([128, 512], F32)
        t1a = AR.take([128, 512], F32)
        swp = AR.take([128, 512], F32)
        Vp = AR.take([128, NCB, 65], BF16)
        Pb = AR.take([128, NCB, 512], BF16)
        o_s = AR.take([64, 512], F32)
        rec_s = AR.take([1, 512], F32)
        oatt = AR.take([128, NG, T_SEQ], BF16)

        slot_free = [ve.n] * 2
        slot_free_se = [se.n] * 2
        slot_i = 0
        PROJ = (("q", "wqT", Qg, 0), ("qs", "wqTs", Qs, 6), ("k", "wkT", Kg, 1), ("v", "wvT", Vg, 2))
        last_exp_for_st = {0: 0, 1: 0}
        last_pv_for_p = {0: 0, 1: 0}
        last_oevac = {0: 0, 1: 0}
        last_scatter = 0
        last_pv_all = 0
        wstep_mm_done = {}
        ln_free_a = 0

        for g4 in range(4):
            tg = slice(512 * g4, 512 * (g4 + 1))
            bc1 = lnchunk(lambda g, tg=tg: X[:, g, tg], EPS1r, 512, 0,
                          [x_loaded, (ve, ln_free_a), (ve, max(slot_free)),
                           (se, max(slot_free_se))])
            def hbt(e, tg=tg):
                ins = None
                for g in range(NG):
                    e.tensor_tensor(e1a[:], X[:, g, tg], P6[:, 0:512], ALU.mult)
                    ins = e.tensor_tensor(Hg[:, g, :], e1a[:], P6[:, 512:1024], ALU.add)
                return ins
            p.on("vector", hbt, wait=[bc1], n=16)
            ln_free_a = ve.n
            h_all = ve.now

            for pi, (nmp, wname, out_t, bidx) in enumerate(PROJ):
                ws = g4 * 4 + pi
                for half in range(2):
                    hstep = 2 * ws + half
                    dma_w = [(pe, wstep_mm_done.get(hstep - 2, 0))] if hstep >= 2 else [consts_ld]
                    for j in range(4):
                        jj = half * 4 + j
                        p.dma("gpsimd", wbufH[hstep % 2][:, j, :],
                              dw[wname][jj * 128:(jj + 1) * 128, :], ldw[hstep % 2],
                              wait=dma_w)
                w_ld = [ldw[0].now, ldw[1].now]
                for go in range(NG):
                    r = slot_i % 2
                    slot_i += 1
                    pv_view = P6[:, 512 * r:512 * (r + 1)]
                    def mproj(e, ws=ws, go=go, pv=pv_view):
                        ins = None
                        for j in range(NG):
                            wb = wbufH[(2 * ws + j // 4) % 2]
                            ins = e.matmul(pv, wb[:, j % 4, go * 128:(go + 1) * 128],
                                           Hg[:, j, :], start=(j == 0), stop=(j == NG - 1))
                        return ins
                    p.on("tensor", mproj,
                         wait=[h_all] + w_ld + [(ve, slot_free[r]), (se, slot_free_se[r])],
                         inc=[(pe, 1)])
                    wstep_mm_done[2 * ws] = pe.n
                    wstep_mm_done[2 * ws + 1] = pe.n
                    mm_done = pe.now
                    if nmp == "v":
                        p.on("scalar", lambda e, go=go, pv=pv_view: e.activation(
                            Vg[:, go, :], pv, AF.Identity, bias=bcol(2, go), scale=1.0),
                            wait=[mm_done], n=1)
                        slot_free_se[r] = se.n
                    else:
                        p.on("scalar", lambda e, go=go, pv=pv_view, b=bidx: e.activation(
                            e1a[:], pv, AF.Identity, bias=bcol(b, go), scale=1.0),
                            wait=[mm_done, ve.now], n=1)
                        slot_free_se[r] = se.n
                        p.on("scalar", lambda e, go=go: e.activation(
                            t1a[:], e1a[:], AF.Identity, bias=0.0, scale=cols[:, 32 + go:33 + go]),
                            n=1)
                        p.on("vector", lambda e: e.stream_shuffle(swp[:], e1a[:], SHUF_SWAP),
                             wait=[se.now], n=1)
                        p.on("vector", lambda e, out=out_t, go=go: e.scalar_tensor_tensor(
                            out[:, go, :], swp[:], ssgn[:, go:go + 1], t1a[:], ALU.mult, ALU.add),
                            n=1)
                    slot_free[r] = ve.n
            qkv_done = (ve.now, se.now)

            for hl in range(4):
                hh = g4 * 4 + hl
                go_out = hh // 2
                r0 = (hh % 2) * 64
                p.on("vector", lambda e: e.memset(Vp[:], 1.0),
                     wait=[qkv_done[0], (pe, last_pv_all)], n=1)
                def vtr(e, hl=hl):
                    ins = None
                    for g in range(NG):
                        ins = e.transpose(PTb[:, 128 * g:128 * (g + 1)],
                                          Vg[:, g, hl * 128:(hl + 1) * 128],
                                          ident_bf[:])
                    return ins
                p.on("tensor", vtr, wait=[qkv_done[1], (ve, ve.n)], inc=[(pe, 1)])
                def vcp(e):
                    ins = None
                    for g in range(NG):
                        e.tensor_copy(Vp[:, 2 * g, 0:64], PTb[:, 128 * g:128 * g + 64])
                        ins = e.tensor_copy(Vp[:, 2 * g + 1, 0:64], PTb[:, 128 * g + 64:128 * (g + 1)])
                    return ins
                p.on("vector", vcp, wait=[pe.now], n=16)
                vp_ready = ve.n

                for qc in range(4):
                    gqc = hh * 4 + qc
                    for cbk in range(NCB):
                        stp = cbk % 2
                        stv = P6[:, 1024 + 512 * stp: 1024 + 512 * (stp + 1)]
                        def smm(e, cbk=cbk, qc=qc, hl=hl, stv=stv):
                            ins = None
                            b = 64 * (cbk % 2)
                            hs = slice(hl * 128, (hl + 1) * 128)
                            stat = Kg[b:b + 64, cbk // 2, hs]
                            for j in range(4):
                                cbq = 4 * qc + j
                                if cbq % 2 == cbk % 2:
                                    mov = Qg[b:b + 64, cbq // 2, hs]
                                elif cbk % 2 == 0:
                                    mov = Qs[0:64, cbq // 2, hs]
                                else:
                                    mov = Qs[64:128, (cbq // 2 + 7) % 8, hs]
                                ins = e.matmul(stv[:, j * 128:(j + 1) * 128],
                                               stat, mov, start=True, stop=True)
                            return ins
                        p.on("tensor", smm,
                             wait=[qkv_done[0], (se, last_exp_for_st[stp])], inc=[(pe, 1)])
                        p.on("scalar", lambda e, cbk=cbk, stv=stv, Pb=Pb: e.activation(
                            Pb[:, cbk, :], stv, AF.Exp, bias=0.0, scale=0.125),
                            wait=[pe.now, (pe, last_pv_for_p[0])], n=1)
                        last_exp_for_st[stp] = se.n
                    exps_done = se.now
                    pslot = gqc % 2
                    ov = P6[0:65, 2048 + 512 * pslot: 2048 + 512 * (pslot + 1)]
                    def pvm(e, ov=ov, Pb=Pb):
                        ins = None
                        for cbk in range(NCB):
                            ins = e.matmul(ov, Vp[:, cbk, :], Pb[:, cbk, :],
                                           start=(cbk == 0), stop=(cbk == NCB - 1))
                        return ins
                    p.on("tensor", pvm,
                         wait=[exps_done, (ve, last_oevac[pslot]), (ve, vp_ready)],
                         inc=[(pe, 1)])
                    last_pv_for_p[0] = pe.n
                    last_pv_all = pe.n
                    pv_done = pe.now
                    p.on("vector", lambda e, ov=ov: e.reciprocal(rec_s[:], ov[64:65, :]),
                         wait=[pv_done], n=1)
                    p.on("vector", lambda e, ov=ov: e.tensor_copy(o_s[:], ov[0:64, :]), n=1)
                    oev = ve.now
                    last_oevac[pslot] = ve.n
                    bcv = PTf[0:64, :]
                    p.on("tensor", lambda e: e.matmul(bcv, ones_row[0:1, 0:64], rec_s[:],
                                                      start=True, stop=True),
                         wait=[oev, (ve, last_scatter)], inc=[(pe, 1)])
                    bc_done = pe.now
                    for j in range(4):
                        cbq = 4 * qc + j
                        def sct(e, j=j, cbq=cbq, go_out=go_out, r0=r0):
                            return e.tensor_tensor(
                                oatt[r0:r0 + 64, go_out, cbq:2048:16],
                                o_s[:, j * 128:(j + 1) * 128],
                                bcv[:, j * 128:(j + 1) * 128], ALU.mult)
                        p.on("vector", sct, wait=[bc_done], n=1)
                    last_scatter = ve.n
        attn_done = ve.now

        for half in range(2):
            hstep = 32 + half
            for j in range(4):
                jj = half * 4 + j
                p.dma("gpsimd", wbufH[hstep % 2][:, j, :],
                      dw["woT"][jj * 128:(jj + 1) * 128, :], ldw[hstep % 2],
                      wait=[(pe, wstep_mm_done[hstep - 2])])
        wo_ld = [ldw[0].now, ldw[1].now]
        for go in range(NG):
            for c in range(4):
                r = slot_i % 2
                slot_i += 1
                pv_view = P6[:, 512 * r:512 * (r + 1)]
                def mwo(e, go=go, c=c, pv=pv_view):
                    ins = None
                    for j in range(NG):
                        wb = wbufH[j // 4]
                        ins = e.matmul(pv, wb[:, j % 4, go * 128:(go + 1) * 128],
                                       oatt[:, j, 512 * c:512 * (c + 1)],
                                       start=(j == 0), stop=(j == NG - 1))
                    return ins
                p.on("tensor", mwo,
                     wait=[attn_done] + wo_ld + [(ve, slot_free[r]), (se, slot_free_se[r])],
                     inc=[(pe, 1)])
                p.on("vector", lambda e, go=go, c=c, pv=pv_view: e.scalar_tensor_tensor(
                    X[:, go, 512 * c:512 * (c + 1)], pv, bcol(3, go),
                    X[:, go, 512 * c:512 * (c + 1)], ALU.add, ALU.add),
                    wait=[pe.now], n=1)
                slot_free[r] = ve.n

        if _dbg: print("attn arena:", AR.off)
        barrier()
        AR.reset()

        NF = 11
        H = AR.take([128, NG, T_SEQ], BF16)
        w1b = AR.take([128, NG, 2816], BF16)
        w2b = AR.take([128, NF, D], BF16)
        rwb = AR.take([128, NG, 8], BF16)
        lp = AR.take([128, 16, 8], F32)
        lpT = AR.take([8, T_SEQ], BF16)
        silu_s = AR.take([128, 512], F32)
        g_all = AR.take([128, NF, 512], BF16)

        e1m = lambda: T2[:, 0:512]
        for j in range(NG):
            p.dma("gpsimd", rwb[:, j, :], drw[j * 128:(j + 1) * 128, :], ldg)
        rw_ld = ldg.now

        ln_free = 0
        H_done = {}
        for c in range(4):
            cs = slice(512 * c, 512 * (c + 1))
            bc = lnchunk(lambda g, cs=cs: X[:, g, cs], EPS1r, 512, 0,
                         [(ve, ln_free)] if c > 0 else [ve.now])
            def hbt0(e, cs=cs):
                ins = None
                for g in range(NG):
                    e.tensor_tensor(e1m(), X[:, g, cs], P6[:, 0:512], ALU.mult)
                    ins = e.tensor_tensor(H[:, g, cs], e1m(), P6[:, 512:1024], ALU.add)
                return ins
            p.on("vector", hbt0, wait=[bc], n=16)
            ln_free = ve.n
        h_all0 = ve.now

        rt_free = 0
        for tc in range(16):
            ts = slice(tc * 128, (tc + 1) * 128)
            def mrt(e, ts=ts):
                ins = None
                for j in range(NG):
                    ins = e.matmul(P6[:, 1024:1032], H[:, j, ts], rwb[:, j, :],
                                   start=(j == 0), stop=(j == NG - 1))
                return ins
            p.on("tensor", mrt, wait=[h_all0, rw_ld, (ve, rt_free)], inc=[(pe, 1)])
            pl_done = pe.now
            t = lg[:, tc, :]
            pr_ = lp[:, tc, :]
            p.on("vector", lambda e, t=t: e.tensor_copy(t, P6[:, 1024:1032]), wait=[pl_done], n=1)
            rt_free = ve.n
            p.on("vector", lambda e, t=t: e.reduce_max(nm[:], t, axis=AX.X, negate=True), n=1)
            p.on("scalar", lambda e, t=t: e.activation(t, t, AF.Exp, bias=nm[:], scale=1.0),
                 wait=[ve.now], n=1)
            p.on("vector", lambda e, t=t: e.reduce_sum(rsum[:], t, axis=AX.X), wait=[se.now], n=1)
            p.on("vector", lambda e: e.reciprocal(rsum[:], rsum[:]), n=1)
            p.on("vector", lambda e, t=t: e.tensor_scalar(t, t, rsum[:], None, ALU.mult), n=1)
            p.on("vector", lambda e, t=t: e.reduce_max(m1[:], t, axis=AX.X), n=1)
            p.on("vector", lambda e, t=t: e.tensor_scalar(sc1[:], t, m1[:], None, ALU.is_ge), n=1)
            p.on("vector", lambda e: e.tensor_tensor(sc1[:], sc1[:], revidx[:], ALU.mult), n=1)
            p.on("vector", lambda e: e.reduce_max(m1[:], sc1[:], axis=AX.X), n=1)
            p.on("vector", lambda e: e.tensor_scalar(sc1[:], sc1[:], m1[:], None, ALU.is_ge), n=1)
            p.on("vector", lambda e, t=t: e.scalar_tensor_tensor(sc2[:], sc1[:], -1e9, t,
                                                                 ALU.mult, ALU.add), n=1)
            p.on("vector", lambda e: e.reduce_max(m1[:], sc2[:], axis=AX.X), n=1)
            p.on("vector", lambda e: e.tensor_scalar(sc2[:], sc2[:], m1[:], None, ALU.is_ge), n=1)
            p.on("vector", lambda e: e.tensor_tensor(sc2[:], sc2[:], revidx[:], ALU.mult), n=1)
            p.on("vector", lambda e: e.reduce_max(m1[:], sc2[:], axis=AX.X), n=1)
            p.on("vector", lambda e: e.tensor_scalar(sc2[:], sc2[:], m1[:], None, ALU.is_ge), n=1)
            p.on("vector", lambda e: e.tensor_tensor(sc1[:], sc1[:], sc2[:], ALU.add), n=1)
            p.on("vector", lambda e, t=t, pr_=pr_: e.tensor_tensor(pr_, t, sc1[:], ALU.mult), n=1)
        lp_done = ve.now

        lpt_cp = 0
        for tc in range(16):
            p.on("tensor", lambda e, tc=tc: e.transpose(PTf[0:8, 0:128], lp[:, tc, :], ident[:]),
                 wait=[lp_done, (ve, lpt_cp)], inc=[(pe, 1)])
            p.on("vector", lambda e, tc=tc: e.tensor_copy(lpT[:, tc * 128:(tc + 1) * 128],
                                                          PTf[0:8, 0:128]),
                 wait=[pe.now], n=1)
            lpt_cp = ve.n
        lpt_done = ve.now

        w1_free = 0
        w2_free = 0
        lpb_free = 0
        y_free_se = 0
        y_free_ve = 0
        g_free = {}
        out_free = {0: 0, 1: 0}
        for ei in range(8):
            for j in range(NG):
                p.dma("gpsimd", w1b[:, j, :], dw1[ei, j * 128:(j + 1) * 128, :], w1s,
                      wait=[(pe, w1_free)] if ei > 0 else [lp_done])
            for f in range(NF):
                p.dma("gpsimd", w2b[:, f, :], dw2[ei, f * 128:(f + 1) * 128, :], w2s,
                      wait=[(pe, w2_free)] if ei > 0 else [lp_done])
            w1_ld = w1s.now
            w2_ld = w2s.now
            for c in range(4):
                cs = slice(c * 512, (c + 1) * 512)
                p.on("tensor", lambda e, ei=ei, cs=cs: e.matmul(
                    P6[:, 2048:2560], selb[:, ei * 128:(ei + 1) * 128], lpT[:, cs],
                    start=True, stop=True),
                    wait=[lpt_done, (ve, lpb_free)], inc=[(pe, 1)])
                lpb_done = pe.now
                for f in range(NF):
                    def my(e, f=f, cs=cs):
                        for j in range(NG):
                            e.matmul(P6[:, 0:512], w1b[:, j, f * 128:(f + 1) * 128],
                                     H[:, j, cs], start=(j == 0), stop=(j == NG - 1))
                        ins = None
                        for j in range(NG):
                            ins = e.matmul(P6[:, 512:1024],
                                           w1b[:, j, 1408 + f * 128:1408 + f * 128 + 128],
                                           H[:, j, cs], start=(j == 0), stop=(j == NG - 1))
                        return ins
                    p.on("tensor", my, wait=[w1_ld, (se, y_free_se), (ve, y_free_ve),
                                             (pe, g_free.get(f, 0))],
                         inc=[(pe, 1)])
                    y_done = pe.now
                    p.on("scalar", lambda e: e.activation(silu_s[:], P6[:, 0:512], AF.Silu),
                         wait=[y_done], n=1)
                    y_free_se = se.n
                    p.on("vector", lambda e, f=f: e.tensor_tensor(
                        g_all[:, f, :], silu_s[:], P6[:, 512:1024], ALU.mult),
                        wait=[se.now], n=1)
                    y_free_ve = ve.n
                    p.on("vector", lambda e, f=f: e.tensor_tensor(
                        g_all[:, f, :], g_all[:, f, :], P6[:, 2048:2560], ALU.mult),
                        wait=[lpb_done], n=1)
                g_done = ve.now
                lpb_free = ve.n
                for go in range(NG):
                    ov = P6[:, 1024 + (go % 2) * 512:1024 + (go % 2) * 512 + 512]
                    def mo(e, go=go, ov=ov):
                        ins = None
                        for f in range(NF):
                            ins = e.matmul(ov, w2b[:, f, go * 128:(go + 1) * 128],
                                           g_all[:, f, :],
                                           start=(f == 0), stop=(f == NF - 1))
                        return ins
                    p.on("tensor", mo, wait=[g_done, w2_ld, (ve, out_free[go % 2])],
                         inc=[(pe, 1)])
                    if go == NG - 1:
                        for f in range(NF):
                            g_free[f] = pe.n
                    p.on("vector", lambda e, go=go, cs=cs, ov=ov: e.tensor_tensor(
                        X[:, go, cs], ov, X[:, go, cs], ALU.add),
                        wait=[pe.now], n=1)
                    out_free[go % 2] = ve.n
            w1_free = pe.n
            w2_free = pe.n

        if _dbg: print("moe0 arena:", AR.off)
        barrier()
        AR.reset()

        B_NI, B_NS, B_TC = 16, 16, 256
        B_NCH = T_SEQ // B_TC
        U1 = AR.take([128, B_NI, B_TC], F32)
        ln1 = U1[:, 0:NG, :]
        h_bfm = U1[:, 8:12, :].bitcast(BF16).rearrange("p a b -> p (a b)").rearrange(
            "p (a b) -> p a b", b=B_TC)
        u_s = U1
        SCR = AR.take([128, B_NI, 260], F32)
        tailb = AR.take([128, B_NI, 4], F32)
        sz = AR.take([128, B_NI, B_TC], BF16)
        dt_s = AR.take([128, B_NI, B_TC], F32)
        u_bf = AR.take([128, B_NI, B_TC], BF16)
        h_n = AR.take([128, B_NI, B_TC], F32)
        dbu = U1
        y_acc = AR.take([128, B_NI, B_TC], F32)
        state = AR.take([128, 256], F32)
        B_s = AR.take([128, 2, B_TC], F32)
        winb = AR.take([128, 2, 8, 128], BF16)
        woutb = AR.take([128, 2, 16, 128], BF16)
        wxb = AR.take([128, B_NI, 96], BF16)
        wdtb = AR.take([64, 2048], BF16)
        dtr_bf = AR.take([64, B_TC], BF16)
        Br = AR.take([16, B_TC], BF16)
        Cr = AR.take([16, B_TC], BF16)
        sel16 = AR.take([16, 2048], BF16)
        A_all = AR.take([128, 256], F32)
        mcols = AR.take([128, 144], F32)

        ccol = lambda base, i: mcols[:, base + i:base + i + 1]
        EPS2r = mcols[0:1, 129:130]
        ONEc = mcols[:, 130:131]
        p.dma("sync", sel16[:], dconst["c_sel16b"], ld)
        p.dma("sync", mcols[:], dmcols, ld)
        p.dma("sync", A_all[:], dalog, ld)
        mld = ld.now
        p.on("scalar", lambda e: e.activation(A_all[:], A_all[:], AF.Exp), wait=[mld], n=1)
        p.on("vector", lambda e: e.tensor_scalar(A_all[:], A_all[:], -1.0, None, ALU.mult),
             wait=[se.now], n=1)

        for i in range(B_NI):
            p.dma("gpsimd", wxb[:, i, :], dwx[i * 128:(i + 1) * 128, :], lwx)
        p.dma("gpsimd", wdtb[:], dwdt, lwx)
        wx_ld = lwx.now
        p.on("vector", lambda e: e.memset(tailb[:], 0.0), n=1)
        p.on("vector", lambda e: e.memset(state[:], 0.0), n=1)

        ln_free = ve.n
        win_mm = {}
        wout_mm = {}
        win_step = 0
        wout_step = 0
        pb_free = {0: 0, 1: 0}
        pb_free_se = {0: 0, 1: 0}
        pc_free = 0
        pd_free_ve = 0
        pd_free_se = 0
        PBv = lambda s: P6[:, 512 + 512 * s:512 + 512 * s + B_TC]
        for c in range(B_NCH):
            cs = slice(c * B_TC, (c + 1) * B_TC)
            bc1 = lnchunk(lambda g, cs=cs: X[:, g, cs], EPS1r, B_TC, 0,
                          [(ve, ln_free)])
            def l1w(e, cs=cs):
                ins = None
                for g in range(NG):
                    e.tensor_tensor(T2[:, 0:B_TC], X[:, g, cs], P6[:, 0:B_TC], ALU.mult)
                    e.tensor_tensor(ln1[:, g], T2[:, 0:B_TC], P6[:, 256:256 + B_TC], ALU.add)
                    ins = e.tensor_tensor(X[:, g, cs], ln1[:, g], X[:, g, cs], ALU.add)
                return ins
            p.on("vector", l1w, wait=[bc1], n=24)
            ln_free = ve.n
            bc2 = lnchunk(lambda g: ln1[:, g], EPS2r, B_TC, 0, [(ve, ln_free)])
            aff_se = {}
            for g in range(NG):
                w2_ = [bc2] if g < 2 else [bc2, (se, aff_se[g - 2])]
                p.on("vector", lambda e, g=g: e.tensor_tensor(
                    T2[:, 0:B_TC] if g % 2 == 0 else T2[:, B_TC:2 * B_TC],
                    ln1[:, g], P6[:, 0:B_TC], ALU.mult), wait=w2_, n=1)
                p.on("vector", lambda e, g=g: e.tensor_tensor(
                    T2[:, 0:B_TC] if g % 2 == 0 else T2[:, B_TC:2 * B_TC],
                    T2[:, 0:B_TC] if g % 2 == 0 else T2[:, B_TC:2 * B_TC],
                    P6[:, 256:256 + B_TC], ALU.add), n=1)
                p.on("scalar", lambda e, g=g: e.activation(
                    h_bfm[:, g, :], T2[:, 0:B_TC] if g % 2 == 0 else T2[:, B_TC:2 * B_TC],
                    AF.Identity, bias=ccol(LNB, g), scale=ccol(LNG, g)),
                    wait=[ve.now], n=1)
                aff_se[g] = se.n
            ln_free = ve.n
            h_done = se.now

            for ei in range(32):
                slot = win_step % 2
                wsem = lwin[slot]
                p.dma("gpsimd", winb[:, slot, :, :],
                      dwin[ei, :, :].rearrange("(j p) k -> p j k", p=128),
                      wsem, wait=[(pe, win_mm.get(win_step - 2, 0))] if win_step >= 2 else [consts_ld])
                win_ld = wsem.now
                pslot = ei % 2
                def mxz2(e, slot=slot, pslot=pslot):
                    ins = None
                    for j in range(NG):
                        ins = e.matmul(PBv(pslot), winb[:, slot, j, :], h_bfm[:, j, :],
                                       start=(j == 0), stop=(j == NG - 1))
                    return ins
                p.on("tensor", mxz2,
                     wait=[h_done, win_ld, (ve, pb_free[pslot]), (se, pb_free_se[pslot])],
                     inc=[(pe, 1)])
                win_mm[win_step] = pe.n
                win_step += 1
                mmd = pe.now
                if ei < 16:
                    p.on("vector", lambda e, ei=ei, pslot=pslot: e.tensor_copy(
                        SCR[:, ei, 4:4 + B_TC], PBv(pslot)), wait=[mmd], n=1)
                    pb_free[pslot] = ve.n
                else:
                    p.on("scalar", lambda e, ei=ei, pslot=pslot: e.activation(
                        sz[:, ei - 16, :], PBv(pslot), AF.Silu), wait=[mmd], n=1)
                    pb_free_se[pslot] = se.n

            for i in range(B_NI):
                p.on("vector", lambda e, i=i: e.tensor_copy(SCR[:, i, 0:4], tailb[:, i, :]), n=1)
            conv_par = {0: 0, 1: 0}
            conv_entry = ve.n
            restores_done = ve.now
            for i in range(B_NI):
                cv = lambda i=i: T2[:, (i % 2) * B_TC:(i % 2) * B_TC + B_TC]
                p.on("scalar", lambda e, i=i: e.activation(
                    cv(i), SCR[:, i, 1:1 + B_TC], AF.Identity,
                    bias=ccol(CONVB, i), scale=ccol(CONVW, i * 4 + 0)),
                    wait=[(ve, conv_par[i % 2] or conv_entry), restores_done], n=1)
                for j in (1, 2, 3):
                    p.on("vector", lambda e, i=i, j=j: e.scalar_tensor_tensor(
                        cv(i), SCR[:, i, 1 + j:1 + j + B_TC], ccol(CONVW, i * 4 + j),
                        cv(i), ALU.mult, ALU.add), wait=[se.now], n=1)
                p.on("scalar", lambda e, i=i: e.activation(u_s[:, i, :], cv(i), AF.Silu),
                     wait=[(ve, ve.n), (pe, win_mm[win_step - 1])], n=1)
                p.on("vector", lambda e, i=i: e.tensor_copy(u_bf[:, i, :], u_s[:, i, :]),
                     wait=[se.now], n=1)
                p.on("vector", lambda e, i=i: e.tensor_copy(tailb[:, i, :], SCR[:, i, B_TC:B_TC + 4]), n=1)
                conv_par[i % 2] = ve.n
            u_done = (ve.now, se.now)

            def mxd(e):
                for i in range(B_NI):
                    e.matmul(P6[0:80, 2048:2048 + B_TC], wxb[:, i, 0:80], u_bf[:, i, :],
                             start=(i == 0), stop=(i == B_NI - 1))
                ins = None
                for i in range(B_NI):
                    ins = e.matmul(P6[0:16, 2304:2304 + B_TC], wxb[:, i, 80:96], u_bf[:, i, :],
                                   start=(i == 0), stop=(i == B_NI - 1))
                return ins
            p.on("tensor", mxd, wait=[u_done[0], wx_ld, (ve, pd_free_ve), (se, pd_free_se)],
                 inc=[(pe, 1)])
            xd_done = pe.now
            p.on("scalar", lambda e: e.activation(dtr_bf[:], P6[0:64, 2048:2048 + B_TC], AF.Identity),
                 wait=[xd_done], n=1)
            pd_free_se = se.n
            p.on("vector", lambda e: e.tensor_copy(Br[:], P6[64:80, 2048:2048 + B_TC]),
                 wait=[xd_done, (se, pd_free_se)], n=1)
            p.on("vector", lambda e: e.tensor_copy(Cr[:], P6[0:16, 2304:2304 + B_TC]), n=1)
            pd_free_ve = ve.n
            xd_ev = (ve.now, se.now)

            for i in range(B_NI):
                pslot = i % 2
                p.on("tensor", lambda e, i=i, pslot=pslot: e.matmul(
                    PBv(pslot), wdtb[:, i * 128:(i + 1) * 128], dtr_bf[:],
                    start=True, stop=True),
                    wait=[xd_ev[1], (ve, pb_free[pslot]), (se, pb_free_se[pslot])],
                    inc=[(pe, 1)])
                mmd = pe.now
                p.on("scalar", lambda e, i=i, pslot=pslot: e.activation(
                    T2[:, 0:B_TC], PBv(pslot), AF.Abs, bias=ccol(BDT, i)),
                    wait=[mmd, (ve, ve.n)], n=1)
                abs_done = se.n
                p.on("scalar", lambda e: e.activation(T2[:, 0:B_TC], T2[:, 0:B_TC], AF.Exp,
                                                      scale=-1.0), n=1)
                p.on("scalar", lambda e: e.activation(T2[:, 0:B_TC], T2[:, 0:B_TC], AF.Ln,
                                                      bias=ONEc), n=1)
                pb_free_se[pslot] = se.n
                p.on("vector", lambda e, i=i, pslot=pslot: e.tensor_scalar(
                    T2[:, B_TC:2 * B_TC], PBv(pslot), ccol(BDT, i), 0.0,
                    ALU.add, ALU.max), wait=[mmd, (se, abs_done)], n=1)
                pb_free[pslot] = ve.n
                p.on("vector", lambda e, i=i: e.tensor_tensor(
                    dt_s[:, i, :], T2[:, B_TC:2 * B_TC], T2[:, 0:B_TC], ALU.add),
                    wait=[se.now], n=1)
            dt_done = ve.now

            p.on("gpsimd", lambda e: e.tensor_tensor(
                SCR[:, :, 0:B_TC], dt_s[:], u_s[:], ALU.mult), wait=[dt_done], n=1)
            du_done = sg.now
            for i in range(B_NI):
                p.on("vector", lambda e, i=i: e.tensor_scalar(
                    y_acc[:, i, :], u_s[:, i, :], ccol(DP, i), None, ALU.mult),
                    wait=[du_done] if i == 0 else [], n=1)
            yinit_done = ve.n
            bs_free = {0: 0, 1: 0}

            pc_freep = {0: pc_free, 1: pc_free}
            scans_done = ve.n
            dbu_g = 0
            for n_ in range(B_NS):
                pcb = 1536 + 1024 * (n_ % 2)
                def mbcn(e, n_=n_, pcb=pcb):
                    e.matmul(P6[:, pcb:pcb + B_TC], sel16[:, n_ * 128:(n_ + 1) * 128], Br[:],
                             start=True, stop=True)
                    return e.matmul(P6[:, pcb + 256:pcb + 256 + B_TC],
                                    sel16[:, n_ * 128:(n_ + 1) * 128],
                                    Cr[:], start=True, stop=True)
                p.on("tensor", mbcn, wait=[xd_ev[0], (ve, pc_freep[n_ % 2])], inc=[(pe, 1)])
                bc_done = pe.now
                p.on("scalar", lambda e, pcb=pcb, n_=n_: e.activation(
                    B_s[:, n_ % 2, :], P6[:, pcb:pcb + B_TC], AF.Identity),
                    wait=[bc_done, (sg, bs_free[n_ % 2])], n=1)
                bsc = se.now
                p.on("gpsimd", lambda e, n_=n_: e.tensor_tensor(
                    dbu[:], SCR[:, :, 0:B_TC],
                    B_s[:, n_ % 2, :].unsqueeze(1).broadcast_to([128, B_NI, B_TC]),
                    ALU.mult),
                    wait=[bsc, (ve, scans_done), (ve, yinit_done)], n=1)
                bs_free[n_ % 2] = sg.n
                dbu_g = sg.n
                p.on("vector", lambda e, n_=n_: e.tensor_tensor(
                    h_n[:], dt_s[:],
                    A_all[:, n_:256:16].unsqueeze(2).broadcast_to([128, B_NI, B_TC]),
                    ALU.mult), wait=[dt_done], n=1)
                p.on("scalar", lambda e: e.activation(
                    h_n[:].rearrange("p a b -> p (a b)"),
                    h_n[:].rearrange("p a b -> p (a b)"), AF.Exp),
                    wait=[ve.now], n=1)
                exp_done = se.now
                for i in range(B_NI):
                    p.on("vector", lambda e, i=i, n_=n_: e.tensor_tensor_scan(
                        h_n[:, i, :], h_n[:, i, :], dbu[:, i, :],
                        state[:, n_ * 16 + i:n_ * 16 + i + 1], ALU.mult, ALU.add),
                        wait=[exp_done, (sg, dbu_g)], n=1)
                scans_done = ve.n
                p.on("vector", lambda e, n_=n_: e.tensor_copy(
                    state[:, n_ * 16:(n_ + 1) * 16], h_n[:, :, B_TC - 1]), n=1)
                p.on("vector", lambda e, pcb=pcb: e.tensor_tensor(
                    h_n[:], h_n[:],
                    P6[:, pcb + 256:pcb + 256 + B_TC].unsqueeze(1).broadcast_to(
                        [128, B_NI, B_TC]),
                    ALU.mult), n=1)
                p.on("vector", lambda e: e.tensor_tensor(y_acc[:], y_acc[:], h_n[:], ALU.add),
                     n=1)
                pc_freep[n_ % 2] = ve.n
            pc_free = ve.n

            for i in range(B_NI):
                p.on("vector", lambda e, i=i: e.tensor_tensor(
                    u_bf[:, i, :], y_acc[:, i, :], sz[:, i, :], ALU.mult), n=1)
            yg_done = ve.now

            for go in range(NG):
                wslot = wout_step % 2
                p.dma("gpsimd", woutb[:, wslot, :, :],
                      dwout[go, :, :].rearrange("(i p) k -> p i k", p=128),
                      lwout[wslot], wait=[(pe, wout_mm.get(wout_step - 2, 0))] if wout_step >= 2 else [consts_ld])
                wo_ldm = lwout[wslot].now
                pslot = go % 2
                def mwom(e, pslot=pslot, wslot=wslot):
                    ins = None
                    for i in range(B_NI):
                        ins = e.matmul(PBv(pslot), woutb[:, wslot, i, :], u_bf[:, i, :],
                                       start=(i == 0), stop=(i == B_NI - 1))
                    return ins
                p.on("tensor", mwom,
                     wait=[yg_done, wo_ldm, (ve, pb_free[pslot]), (se, pb_free_se[pslot])],
                     inc=[(pe, 1)])
                wout_mm[wout_step] = pe.n
                wout_step += 1
                p.on("vector", lambda e, go=go, cs=cs, pslot=pslot: e.tensor_tensor(
                    X[:, go, cs], PBv(pslot), X[:, go, cs], ALU.add),
                    wait=[pe.now], n=1)
                pb_free[pslot] = ve.n

        if _dbg: print("mamba arena:", AR.off)
        barrier()
        AR.reset()

        H1 = AR.take([128, NG, T_SEQ], BF16)
        ewb = AR.take([128, NG, D], BF16)
        gwb = AR.take([128, NG, 8], BF16)
        xe2 = [AR.take([128, NG, 512], BF16), AR.take([128, NG, 512], BF16)]
        cbm = AR.take([128, 16, 8], F32)
        cbT = AR.take([8, T_SEQ], BF16)
        ebs = AR.take([8, D], BF16)

        p.dma("gpsimd", ebs[:], deb, ldg)
        eb_ld = ldg.now
        for j in range(NG):
            p.dma("gpsimd", gwb[:, j, :], dgw[j * 128:(j + 1) * 128, :], ldg)
        gw_ld = ldg.now

        ln_free = ve.n
        for c in range(4):
            cs = slice(512 * c, 512 * (c + 1))
            bc = lnchunk(lambda g, cs=cs: X[:, g, cs], EPS1r, 512, 0, [(ve, ln_free)])
            def hbt1(e, cs=cs):
                ins = None
                for g in range(NG):
                    e.tensor_tensor(T2[:, 0:512], X[:, g, cs], P6[:, 0:512], ALU.mult)
                    ins = e.tensor_tensor(H1[:, g, cs], T2[:, 0:512], P6[:, 512:1024], ALU.add)
                return ins
            p.on("vector", hbt1, wait=[bc], n=16)
            ln_free = ve.n
        h_all1 = ve.now

        p.on("tensor", lambda e: e.matmul(P6[:, 1032:1040], ones_row[:], gbs[:],
                                          start=True, stop=True),
             wait=[consts_ld, h_all1], inc=[(pe, 1)])
        gbb_done = pe.now
        rt_free = 0
        for tc in range(16):
            ts = slice(tc * 128, (tc + 1) * 128)
            def mrt1(e, ts=ts):
                ins = None
                for j in range(NG):
                    ins = e.matmul(P6[:, 1024:1032], H1[:, j, ts], gwb[:, j, :],
                                   start=(j == 0), stop=(j == NG - 1))
                return ins
            p.on("tensor", mrt1, wait=[h_all1, gw_ld, (ve, rt_free)], inc=[(pe, 1)])
            pl_done = pe.now
            t = lg[:, tc, :]
            pr_ = cbm[:, tc, :]
            p.on("vector", lambda e, t=t: e.tensor_copy(t, P6[:, 1024:1032]),
                 wait=[pl_done, gbb_done], n=1)
            p.on("vector", lambda e, t=t: e.tensor_tensor(t, t, P6[:, 1032:1040], ALU.add), n=1)
            rt_free = ve.n
            p.on("vector", lambda e, t=t: e.tensor_scalar_max(t, t, 1e-4), n=1)
            p.on("vector", lambda e, t=t: e.reduce_max(nm[:], t, axis=AX.X, negate=True), n=1)
            p.on("scalar", lambda e, t=t: e.activation(t, t, AF.Exp, bias=nm[:], scale=1.0),
                 wait=[ve.now], n=1)
            p.on("vector", lambda e, t=t: e.reduce_sum(rsum[:], t, axis=AX.X), wait=[se.now], n=1)
            p.on("vector", lambda e: e.reciprocal(rsum[:], rsum[:]), n=1)
            p.on("vector", lambda e, t=t: e.tensor_scalar(t, t, rsum[:], None, ALU.mult), n=1)
            p.on("vector", lambda e, t=t: e.reduce_max(m1[:], t, axis=AX.X), n=1)
            p.on("vector", lambda e, t=t: e.tensor_scalar(sc1[:], t, m1[:], None, ALU.is_ge), n=1)
            p.on("vector", lambda e: e.tensor_tensor(sc1[:], sc1[:], revidx[:], ALU.mult), n=1)
            p.on("vector", lambda e: e.reduce_max(m1[:], sc1[:], axis=AX.X), n=1)
            p.on("vector", lambda e: e.tensor_scalar(sc1[:], sc1[:], m1[:], None, ALU.is_ge), n=1)
            p.on("vector", lambda e, t=t: e.scalar_tensor_tensor(sc2[:], sc1[:], -1e9, t,
                                                                 ALU.mult, ALU.add), n=1)
            p.on("vector", lambda e: e.reduce_max(m1[:], sc2[:], axis=AX.X), n=1)
            p.on("vector", lambda e: e.tensor_scalar(sc2[:], sc2[:], m1[:], None, ALU.is_ge), n=1)
            p.on("vector", lambda e: e.tensor_tensor(sc2[:], sc2[:], revidx[:], ALU.mult), n=1)
            p.on("vector", lambda e: e.reduce_max(m1[:], sc2[:], axis=AX.X), n=1)
            p.on("vector", lambda e: e.tensor_scalar(sc2[:], sc2[:], m1[:], None, ALU.is_ge), n=1)
            p.on("vector", lambda e: e.tensor_tensor(sc1[:], sc1[:], sc2[:], ALU.add), n=1)
            p.on("vector", lambda e, t=t: e.tensor_scalar_max(sc2[:], t, 0.1), n=1)
            p.on("vector", lambda e: e.tensor_tensor(sc2[:], sc2[:], sc1[:], ALU.mult), n=1)
            p.on("vector", lambda e: e.reduce_sum(m1[:], sc2[:], axis=AX.X), n=1)
            p.on("vector", lambda e: e.reciprocal(m1[:], m1[:]), n=1)
            p.on("vector", lambda e, pr_=pr_: e.tensor_scalar(pr_, sc2[:], m1[:], 0.5,
                                                              ALU.mult, ALU.mult), n=1)
        cb_done = ve.now

        cbt_cp = 0
        for tc in range(16):
            p.on("tensor", lambda e, tc=tc: e.transpose(PTf[0:8, 0:128], cbm[:, tc, :], ident[:]),
                 wait=[cb_done, (ve, cbt_cp)], inc=[(pe, 1)])
            p.on("vector", lambda e, tc=tc: e.tensor_copy(cbT[:, tc * 128:(tc + 1) * 128],
                                                          PTf[0:8, 0:128]),
                 wait=[pe.now], n=1)
            cbt_cp = ve.n
        cbt_done = ve.now

        out_free = {0: 0, 1: 0}
        for c in range(4):
            cs = slice(c * 512, (c + 1) * 512)
            pslot = c % 2
            for go in range(NG):
                p.on("tensor", lambda e, cs=cs, go=go, pslot=pslot: e.matmul(
                    P6[:, pslot * 512:pslot * 512 + 512],
                    ebs[:, go * 128:(go + 1) * 128], cbT[:, cs],
                    start=True, stop=True),
                    wait=[cbt_done, eb_ld, (ve, out_free[pslot])], inc=[(pe, 1)])
                p.on("vector", lambda e, cs=cs, go=go, pslot=pslot: e.tensor_tensor(
                    X[:, go, cs], P6[:, pslot * 512:pslot * 512 + 512], X[:, go, cs], ALU.add),
                    wait=[pe.now], n=1)
                out_free[pslot] = ve.n
                pslot = (pslot + 1) % 2

        ew_free = 0
        xe_free_pe = {0: 0, 1: 0}
        plb_free = {0: 0, 1: 0}
        for ei in range(8):
            for j in range(NG):
                p.dma("gpsimd", ewb[:, j, :], dew[ei, j * 128:(j + 1) * 128, :], ews,
                      wait=[(pe, ew_free)] if ei > 0 else [cb_done])
            ew_ld = ews.now
            for c in range(4):
                cs = slice(c * 512, (c + 1) * 512)
                cpar = c % 2
                xe = xe2[cpar]
                plv = P6[:, 1024 + 512 * 0:1536] if True else None
                plo = 1024 if cpar == 0 else 2560
                p.on("tensor", lambda e, ei=ei, cs=cs, plo=plo: e.matmul(
                    P6[:, plo:plo + 512], selb[:, ei * 128:(ei + 1) * 128], cbT[:, cs],
                    start=True, stop=True),
                    wait=[cbt_done, (ve, plb_free[cpar])], inc=[(pe, 1)])
                plb_done = pe.now
                def mxe(e, cs=cs, xe=xe, plo=plo):
                    ins = None
                    for g in range(NG):
                        ins = e.tensor_tensor(xe[:, g, :], H1[:, g, cs], P6[:, plo:plo + 512],
                                              ALU.mult)
                    return ins
                p.on("vector", mxe, wait=[plb_done, (pe, xe_free_pe[cpar])], n=8)
                xe_done = ve.now
                plb_free[cpar] = ve.n
                for go in range(NG):
                    pslot = go % 2
                    def mex(e, go=go, pslot=pslot, xe=xe):
                        ins = None
                        for j in range(NG):
                            ins = e.matmul(P6[:, pslot * 512:pslot * 512 + 512],
                                           ewb[:, j, go * 128:(go + 1) * 128], xe[:, j, :],
                                           start=(j == 0), stop=(j == NG - 1))
                        return ins
                    p.on("tensor", mex, wait=[xe_done, ew_ld, (ve, out_free[pslot])],
                         inc=[(pe, 1)])
                    if go == NG - 1:
                        xe_free_pe[cpar] = pe.n
                    p.on("vector", lambda e, go=go, cs=cs, pslot=pslot: e.tensor_tensor(
                        X[:, go, cs], P6[:, pslot * 512:pslot * 512 + 512], X[:, go, cs],
                        ALU.add),
                        wait=[pe.now], n=1)
                    out_free[pslot] = ve.n
            ew_free = pe.n
        res_done = ve.now

        st_cp = {}
        for tt in range(16):
            for half in range(2):
                def trs(e, tt=tt, half=half):
                    ins = None
                    for g4 in range(4):
                        g = half * 4 + g4
                        ins = e.transpose(PTf[:, g4 * 128:(g4 + 1) * 128],
                                          X[:, g, tt * 128:(tt + 1) * 128], ident[:])
                    return ins
                w = [res_done]
                if (tt, half) != (0, 0):
                    w.append((ve, st_cp.get((tt, half - 1), st_cp.get((tt - 1, 1), 0))))
                if tt >= 1:
                    w.append((st, 16 * tt))
                p.on("tensor", trs, wait=w, inc=[(pe, 1)])
                def cps(e, tt=tt, half=half):
                    ins = None
                    for g4 in range(4):
                        g = half * 4 + g4
                        ins = e.tensor_copy(xt[:, g * 128:(g + 1) * 128],
                                            PTf[:, g4 * 128:(g4 + 1) * 128])
                    return ins
                p.on("vector", cps, wait=[pe.now], n=4)
                st_cp[(tt, half)] = ve.n
            p.dma("sync", dxout[tt * 128:(tt + 1) * 128, :], xt[:, 0:1024], st,
                  wait=[(ve, st_cp[(tt, 1)])])
        p.on("sync", lambda e: None, wait=[st.now])
        p.finalize()
    return nc


def host_fused(inputs, core):
    cs = make_consts()
    x = np.ascontiguousarray(np.asarray(inputs["L0_attn_x"][core], dtype=np.float32))
    cos_row = np.asarray(inputs["L0_attn_cos"])[T_SEQ]
    sin_row = np.asarray(inputs["L0_attn_sin"])[T_SEQ]
    ccolh = np.tile(cos_row, 16).reshape(8, 128).T.astype(np.float32)
    scolh = np.tile(sin_row, 16).reshape(8, 128).T.astype(np.float32)
    colsh = np.zeros((128, 72), np.float32)
    for i, bn in enumerate(("bq", "bk", "bv", "bo")):
        colsh[:, 8 * i:8 * (i + 1)] = np.asarray(inputs["L0_attn_" + bn]).reshape(8, 128).T
    colsh[:, 32:40] = ccolh
    colsh[:, 40:48] = scolh
    colsh[:, 48:56] = np.roll(np.asarray(inputs["L0_attn_bq"]), -64).reshape(8, 128).T
    colsh[:, 64] = 1e-5
    wqT = np.ascontiguousarray(np.asarray(inputs["L0_attn_wq"]).T)
    W_in = np.asarray(inputs["L1_attn_W_in"]).astype(np.float32)
    W_out = np.asarray(inputs["L1_attn_W_out"]).astype(np.float32)
    W_x = np.asarray(inputs["L1_attn_W_x"]).astype(np.float32)
    W_dt = np.asarray(inputs["L1_attn_W_dt"]).astype(np.float32)
    winT_t = np.ascontiguousarray(W_in.T.reshape(1024, 32, 128).transpose(1, 0, 2))
    woutT_t = np.ascontiguousarray(W_out.T.reshape(2048, 8, 128).transpose(1, 0, 2))
    conv_w = np.asarray(inputs["L1_attn_conv_w"]).astype(np.float32)
    mcolsh = np.zeros((128, 144), np.float32)
    cw = conv_w[:, 0, :].reshape(16, 128, 4)
    for i in range(16):
        for j in range(4):
            mcolsh[:, i * 4 + j] = cw[i, :, j]
    mcolsh[:, 64:80] = np.asarray(inputs["L1_attn_conv_b"]).reshape(16, 128).T
    mcolsh[:, 80:96] = np.asarray(inputs["L1_attn_b_dt"]).reshape(16, 128).T
    mcolsh[:, 96:112] = np.asarray(inputs["L1_attn_Dp"]).reshape(16, 128).T
    mcolsh[:, 112:120] = np.asarray(inputs["L1_attn_ln_g"]).reshape(8, 128).T
    mcolsh[:, 120:128] = np.asarray(inputs["L1_attn_ln_b"]).reshape(8, 128).T
    mcolsh[:, 128] = 1e-5
    mcolsh[:, 129] = 1e-6
    mcolsh[:, 130] = 1.0
    alog = np.asarray(inputs["L1_attn_A_log"]).astype(np.float32)
    alog_t = np.zeros((128, 256), np.float32)
    for i in range(16):
        for n in range(16):
            alog_t[:, i * 16 + n] = alog[i * 128:(i + 1) * 128, n]
    sel16h = np.concatenate([np.tile(np.eye(16, dtype=np.float32)[:, n:n + 1], (1, 128))
                             for n in range(16)], axis=1).astype(ml_dtypes.bfloat16)
    selh = np.concatenate([np.tile(np.eye(8, dtype=np.float32)[:, e:e + 1], (1, 128))
                           for e in range(8)], axis=1)
    m = {
        "xin": x,
        "wqT": wqT,
        "wqTs": np.ascontiguousarray(np.roll(wqT, -64, axis=1)),
        "wkT": np.ascontiguousarray(np.asarray(inputs["L0_attn_wk"]).T),
        "wvT": np.ascontiguousarray(np.asarray(inputs["L0_attn_wv"]).T),
        "woT": np.ascontiguousarray(np.asarray(inputs["L0_attn_wo"]).T),
        "cols": colsh,
        "rw": np.asarray(inputs["L0_ffn_router_weight"]).astype(np.float32),
        "w1": np.asarray(inputs["L0_ffn_w1"]).astype(np.float32),
        "w2": np.asarray(inputs["L0_ffn_w2"]).astype(np.float32),
        "winT": winT_t,
        "woutT": woutT_t,
        "wxT": np.ascontiguousarray(W_x.T),
        "wdtT": np.ascontiguousarray(W_dt.T),
        "alog": alog_t,
        "mcols": mcolsh,
        "gw": np.asarray(inputs["L1_ffn_gate_W"]).astype(np.float32),
        "gb": np.asarray(inputs["L1_ffn_gate_b"]).astype(np.float32).reshape(1, 8),
        "ew": np.asarray(inputs["L1_ffn_expert_W"]).astype(np.float32),
        "eb": np.asarray(inputs["L1_ffn_expert_b"]).astype(np.float32),
        "c_ident_f32": cs["c_ident_f32"],
        "c_ident_bf": cs["c_ident_bf"],
        "c_ones_row": cs["c_ones_row"],
        "c_ones_col": cs["c_ones_col"],
        "c_pm": cs["c_pm"],
        "c_revidx": cs["c_revidx"],
        "c_sel": selh,
        "c_selb": selh.astype(ml_dtypes.bfloat16),
        "c_sel16b": sel16h,
    }
    return m




_CACHE = {}


def kernel(**inputs):
    if "nc" not in _CACHE:
        _CACHE["nc"] = build_fused()
    nc = _CACHE["nc"]
    in_maps = [host_fused(inputs, c) for c in range(8)]
    res = run_bass_kernel_spmd(nc, in_maps, core_ids=list(range(8)))
    return np.stack([res.results[c]["xout"] for c in range(8)])


# version 12
# speedup vs baseline: 1.0814x; 1.0814x over previous
import numpy as np
from contextlib import ExitStack
import concourse.bass as bass
import concourse.mybir as mybir

F32 = mybir.dt.float32
BF16 = mybir.dt.bfloat16
AF = mybir.ActivationFunctionType
ALU = mybir.AluOpType
AX = mybir.AxisListType

ENGINES = ("sync", "gpsimd", "tensor", "vector", "scalar")

_AUTOSYNC_OPS = frozenset((
    "tensor_tensor", "tensor_add", "tensor_sub", "tensor_mul", "tensor_max",
    "tensor_scalar", "tensor_single_scalar", "tensor_scalar_add",
    "tensor_scalar_sub", "tensor_scalar_mul", "tensor_scalar_max",
    "tensor_scalar_min", "tensor_relu", "tensor_copy", "tensor_reduce",
    "reduce_max", "reduce_sum", "scalar_tensor_tensor", "tensor_tensor_scan",
    "memset", "memzero", "reciprocal", "stream_shuffle", "select",
    "copy_predicated", "transpose", "bn_stats", "bn_aggr", "max", "max_index",
    "activation", "sign", "square", "sqrt", "mul", "copy", "add", "iota",
    "affine_select", "partition_broadcast", "partition_all_reduce",
))


class Sem:
    __slots__ = ("h", "n", "name")

    def __init__(self, h, name):
        self.h = h
        self.n = 0
        self.name = name

    @property
    def now(self):
        return (self, self.n)


class AutoSyncEngine:

    def __init__(self, eng, sem):
        self._eng = eng
        self._sem = sem
        self._n = 0

    def wait_ge(self, sem, v):
        return self._eng.wait_ge(sem, v)

    def __getattr__(self, name):
        attr = getattr(self._eng, name)
        if name not in _AUTOSYNC_OPS:
            return attr

        def wrapped(*a, **k):
            if self._n > 0:
                self._eng.wait_ge(self._sem, self._n)
            ins = attr(*a, **k)
            ins.then_inc(self._sem, 1)
            self._n += 1
            return ins

        return wrapped


class Prog:
    AUTO = ("vector", "scalar", "gpsimd")

    def __init__(self, nc, es: ExitStack):
        self.nc = nc
        self.es = es
        self.steps = {e: [] for e in ENGINES}
        self.waited = {e: {} for e in ENGINES}
        self.sv = self.sem("auto_v")
        self.ss = self.sem("auto_s")
        self.sg = self.sem("auto_g")
        self._auto = {"vector": self.sv, "scalar": self.ss, "gpsimd": self.sg}

    def sem(self, name):
        return Sem(self.es.enter_context(self.nc.semaphore(name)), name)

    def sbuf(self, name, shape, dtype):
        return self.es.enter_context(self.nc.sbuf_tensor(name, shape, dtype))

    def psum(self, name, shape, dtype=F32):
        return self.es.enter_context(self.nc.psum_tensor(name, shape, dtype))

    def on(self, engine, fn, wait=(), inc=(), n=1):
        w = []
        seen = self.waited[engine]
        for s, v in wait:
            if v > 0 and seen.get(id(s), -1) < v:
                w.append((s, v))
                seen[id(s)] = v
        if engine in self._auto and not inc:
            s = self._auto[engine]
            s.n += n
            self.steps[engine].append((w, fn, (), s.n))
            return s.n
        for s, k in inc:
            s.n += k
        self.steps[engine].append((w, fn, list(inc), None))
        return None

    def dma(self, engine, out, in_, sem, wait=()):
        self.on(engine, lambda e, o=out, i=in_: e.dma_start(out=o, in_=i),
                wait=wait, inc=[(sem, 16)])

    def replay(self, engine, eng):
        if engine in self._auto:
            proxy = AutoSyncEngine(eng, self._auto[engine].h)
            for w, fn, incs, after in self.steps[engine]:
                for s, v in w:
                    eng.wait_ge(s.h, v)
                ins = fn(proxy)
                if after is None:
                    for s, k in incs:
                        ins = ins.then_inc(s.h, k)
                else:
                    assert proxy._n == after, (
                        f"{engine} step declared wrong n: have {proxy._n}, expected {after}")
            return
        for w, fn, incs, _ in self.steps[engine]:
            for s, v in w:
                eng.wait_ge(s.h, v)
            ins = fn(eng)
            for s, k in incs:
                assert ins is not None, f"step on {engine} has inc but returned None"
                ins = ins.then_inc(s.h, k)

    def finalize(self):
        with self.nc.Block() as block:
            @block.sync
            def _(eng):
                self.replay("sync", eng)

            @block.gpsimd
            def _(eng):
                self.replay("gpsimd", eng)

            @block.tensor
            def _(eng):
                self.replay("tensor", eng)

            @block.vector
            def _(eng):
                self.replay("vector", eng)

            @block.scalar
            def _(eng):
                self.replay("scalar", eng)


def make_consts():
    ident = np.eye(128, dtype=np.float32)
    import ml_dtypes
    consts = {
        "c_ident_f32": ident,
        "c_ident_bf": ident.astype(ml_dtypes.bfloat16),
        "c_ident2": np.vstack([np.eye(64), np.eye(64)]).astype(ml_dtypes.bfloat16),
        "c_ones_row": np.ones((1, 128), dtype=np.float32),
        "c_ones_col": np.ones((128, 1), dtype=np.float32),
        "c_pm": np.tile(np.array([[-1.0], [1.0]], np.float32), (64, 1)),
        "c_revidx": np.tile(np.arange(8, 0, -1, dtype=np.float32)[None, :], (128, 1)),
    }
    return consts


SHUF_SWAP = [i ^ 1 for i in range(32)]
import ml_dtypes
from concourse.bass_utils import run_bass_kernel_spmd

U8 = mybir.dt.uint8
T_SEQ = 2048
D = 1024
NG = 8


class Arena:
    def __init__(self, p, name, nbytes):
        self.t = p.sbuf(name, [128, nbytes], U8)
        self.off = 0

    def take(self, shape, dtype):
        bpe = 4 if dtype == F32 else 2
        free = 1
        for s in shape[1:]:
            free *= s
        nb = free * bpe
        off = self.off
        self.off = (self.off + nb + 31) & ~31
        v = self.t[:, off:off + nb].bitcast(dtype)
        if len(shape) == 3:
            v = v.rearrange("p (a b) -> p a b", a=shape[1])
        elif len(shape) == 4:
            v = v.rearrange("p (a b c) -> p a b c", a=shape[1], b=shape[2])
        if shape[0] != 128:
            v = v[0:shape[0]]
        return v

    def reset(self):
        self.off = 0


def build_fused():
    nc = bass.Bass("TRN2", target_bir_lowering=False, debug=False)
    dxin = nc.dram_tensor("xin", [T_SEQ, D], F32, kind="ExternalInput").ap()
    dxout = nc.dram_tensor("xout", [T_SEQ, D], F32, kind="ExternalOutput").ap()
    dw = {}
    for w in ("wqT", "wqTs", "wkT", "wvT", "woT"):
        dw[w] = nc.dram_tensor(w, [D, D], F32, kind="ExternalInput").ap()
    dcols = nc.dram_tensor("cols", [128, 72], F32, kind="ExternalInput").ap()
    drw = nc.dram_tensor("rw", [D, 8], F32, kind="ExternalInput").ap()
    dw1 = nc.dram_tensor("w1", [8, D, 2816], F32, kind="ExternalInput").ap()
    dw2 = nc.dram_tensor("w2", [8, 1408, D], F32, kind="ExternalInput").ap()
    dwin = nc.dram_tensor("winT", [32, D, 128], F32, kind="ExternalInput").ap()
    dwout = nc.dram_tensor("woutT", [NG, 2048, 128], F32, kind="ExternalInput").ap()
    dwx = nc.dram_tensor("wxT", [2048, 96], F32, kind="ExternalInput").ap()
    dwdt = nc.dram_tensor("wdtT", [64, 2048], F32, kind="ExternalInput").ap()
    dalog = nc.dram_tensor("alog", [128, 256], F32, kind="ExternalInput").ap()
    dmcols = nc.dram_tensor("mcols", [128, 144], F32, kind="ExternalInput").ap()
    dgw = nc.dram_tensor("gw", [D, 8], F32, kind="ExternalInput").ap()
    dgb = nc.dram_tensor("gb", [1, 8], F32, kind="ExternalInput").ap()
    dew = nc.dram_tensor("ew", [8, D, D], F32, kind="ExternalInput").ap()
    deb = nc.dram_tensor("eb", [8, D], F32, kind="ExternalInput").ap()
    dconst = {}
    for cn, sh, dt_ in (("c_ident_f32", [128, 128], F32), ("c_ident_bf", [128, 128], BF16),
                        ("c_ones_row", [1, 128], F32), ("c_ones_col", [128, 1], F32),
                        ("c_pm", [128, 1], F32), ("c_revidx", [128, 8], F32),
                        ("c_sel", [8, 1024], F32), ("c_selb", [8, 1024], BF16),
                        ("c_sel16b", [16, 2048], BF16)):
        dconst[cn] = nc.dram_tensor(cn, sh, dt_, kind="ExternalInput").ap()

    es = ExitStack()
    with es:
        p = Prog(nc, es)
        X = p.sbuf("X", [128, NG, T_SEQ], F32)
        T2 = p.sbuf("T2", [128, 1024], F32)
        ident = p.sbuf("ident", [128, 128], F32)
        ident_bf = p.sbuf("ident_bf", [128, 128], BF16)
        ones_row = p.sbuf("ones_row", [1, 128], F32)
        ones_col = p.sbuf("ones_col", [128, 1], F32)
        pm = p.sbuf("pm", [128, 1], F32)
        revidx = p.sbuf("revidx", [128, 8], F32)
        selb = p.sbuf("selb", [8, 1024], BF16)
        cols = p.sbuf("scols", [128, 72], F32)
        ssgn = p.sbuf("ssgn", [128, 8], F32)
        gbs = p.sbuf("gbs", [1, 8], F32)
        r1 = p.sbuf("r1", [1, 512], F32)
        r2 = p.sbuf("r2", [1, 512], F32)
        r3 = p.sbuf("r3", [1, 512], F32)
        lg = p.sbuf("lg", [128, 16, 8], F32)
        nm = p.sbuf("nm", [128, 1], F32)
        rsum = p.sbuf("rsum", [128, 1], F32)
        m1 = p.sbuf("m1", [128, 1], F32)
        sc1 = p.sbuf("sc1", [128, 8], F32)
        sc2 = p.sbuf("sc2", [128, 8], F32)
        AR = Arena(p, "ARENA", 132608)
        import os as _os
        _dbg = _os.environ.get("ARDBG")

        P6 = p.psum("P6", [128, 3072], F32)
        PTf = p.psum("PTf", [128, 512], F32)
        PTb = p.psum("PTb", [128, 1024], BF16)

        ld = p.sem("ld")
        ldg = p.sem("ldg")
        ldw = [p.sem("ldw0"), p.sem("ldw1")]
        w1s = p.sem("w1s")
        w2s = p.sem("w2s")
        lwx = p.sem("lwx")
        lwin = [p.sem("lwin0"), p.sem("lwin1")]
        lwout = [p.sem("lwout0"), p.sem("lwout1")]
        ews2 = [p.sem("ews0"), p.sem("ews1")]
        pe = p.sem("pe")
        ve = p.sv
        se = p.ss
        sg = p.sg
        st2 = [p.sem("stx0"), p.sem("stx1")]
        ldx2 = [p.sem("ldx0"), p.sem("ldx1")]

        bcol = lambda i, go: cols[:, 8 * i + go:8 * i + go + 1]
        CONVW, CONVB, BDT, DP, LNG, LNB = 0, 64, 80, 96, 112, 120
        EPS1r = cols[0:1, 64:65]

        def barrier():
            targets = [pe.now, ve.now, se.now, ld.now, w1s.now, w2s.now, lwx.now,
                       ldw[0].now, ldw[1].now, lwin[0].now, lwin[1].now,
                       lwout[0].now, ews2[0].now, ews2[1].now, ldg.now]
            targets = [t for t in targets if t[1] > 0]
            for eng in ("sync", "gpsimd", "tensor", "vector", "scalar"):
                p.on(eng, lambda e: None, wait=list(targets), n=0)

        for cn, t in (("c_ident_f32", ident), ("c_ident_bf", ident_bf),
                      ("c_ones_row", ones_row), ("c_ones_col", ones_col),
                      ("c_pm", pm), ("c_revidx", revidx),
                      ("c_selb", selb)):
            p.dma("sync", t[:], dconst[cn], ld)
        p.dma("sync", cols[:], dcols, ld)
        p.dma("sync", gbs[:], dgb, ld)
        consts_ld = ld.now
        p.on("vector", lambda e: e.tensor_scalar(ssgn[:], cols[:, 40:48], pm[:], None, ALU.mult),
             wait=[consts_ld], n=1)

        xt2l = [AR.take([128, 1024], F32), AR.take([128, 1024], F32)]
        AR.reset()
        tr_done = {}
        cp_done = {}
        for tt in range(16):
            xt = xt2l[tt % 2]
            wl = [consts_ld] if tt < 2 else [(pe, tr_done[tt - 2])]
            p.dma("sync", xt[:], dxin[tt * 128:(tt + 1) * 128, :], ldx2[tt % 2], wait=wl)
            ldv = ldx2[tt % 2].now
            for half in range(2):
                def trb(e, tt=tt, half=half, xt=xt):
                    ins = None
                    for g4 in range(4):
                        g = half * 4 + g4
                        ins = e.transpose(PTf[:, g4 * 128:(g4 + 1) * 128],
                                          xt[:, g * 128:(g + 1) * 128], ident[:])
                    return ins
                w = [ldv]
                if tt > 0 or half > 0:
                    w.append((ve, cp_done.get((tt, half - 1), cp_done.get((tt - 1, 1), 0))))
                p.on("tensor", trb, wait=w, inc=[(pe, 1)])
                def cpb(e, tt=tt, half=half):
                    ins = None
                    for g4 in range(4):
                        g = half * 4 + g4
                        ins = e.tensor_copy(X[:, g, tt * 128:(tt + 1) * 128],
                                            PTf[:, g4 * 128:(g4 + 1) * 128])
                    return ins
                p.on("vector", cpb, wait=[pe.now], n=4)
                cp_done[(tt, half)] = ve.n
            tr_done[tt] = pe.n
        x_loaded = ve.now

        def lnchunk(src_g, eps_ap, tc, pa0, gate_wait):
            def mmu(e):
                ins = None
                for g in range(NG):
                    ins = e.matmul(P6[0:1, pa0:pa0 + tc], ones_col[:], src_g(g),
                                   start=(g == 0), stop=(g == NG - 1))
                return ins
            p.on("tensor", mmu, wait=gate_wait, inc=[(pe, 1)])
            p.on("vector", lambda e: e.tensor_scalar(r1[0:1, 0:tc], P6[0:1, pa0:pa0 + tc],
                                                     1.0 / D, None, ALU.mult),
                 wait=[pe.now], n=1)
            mu_ev = ve.now
            sq_done = 0
            for g in range(NG):
                w = list(gate_wait) if g < 2 else [(pe, sq_done)]
                p.on("scalar", lambda e, g=g: e.activation(
                    T2[:, (g % 2) * tc:(g % 2) * tc + tc], src_g(g), AF.Square),
                    wait=w + [mu_ev], n=1)
                p.on("tensor", lambda e, g=g: e.matmul(
                    P6[0:1, pa0 + tc:pa0 + 2 * tc], ones_col[:],
                    T2[:, (g % 2) * tc:(g % 2) * tc + tc],
                    start=(g == 0), stop=(g == NG - 1)),
                     wait=[se.now, mu_ev], inc=[(pe, 1)])
                if g >= 1:
                    sq_done = pe.n - 1
            p.on("vector", lambda e: e.tensor_scalar(r2[0:1, 0:tc], P6[0:1, pa0 + tc:pa0 + 2 * tc],
                                                     1.0 / D, None, ALU.mult),
                 wait=[pe.now], n=1)
            p.on("vector", lambda e: e.tensor_tensor(r3[0:1, 0:tc], r1[0:1, 0:tc], r1[0:1, 0:tc],
                                                     ALU.mult), n=1)
            p.on("vector", lambda e: e.tensor_tensor(r3[0:1, 0:tc], r2[0:1, 0:tc], r3[0:1, 0:tc],
                                                     ALU.subtract), n=1)
            p.on("scalar", lambda e: e.activation(r2[0:1, 0:tc], r3[0:1, 0:tc], AF.Sqrt, bias=eps_ap),
                 wait=[ve.now], n=1)
            p.on("vector", lambda e: e.reciprocal(r3[0:1, 0:tc], r2[0:1, 0:tc]), wait=[se.now], n=1)
            p.on("vector", lambda e: e.scalar_tensor_tensor(r1[0:1, 0:tc], r1[0:1, 0:tc], -1.0,
                                                            r3[0:1, 0:tc], ALU.mult, ALU.mult), n=1)
            rows_done = ve.now
            def mbc(e):
                e.matmul(P6[:, pa0:pa0 + tc], ones_row[:], r3[0:1, 0:tc], start=True, stop=True)
                return e.matmul(P6[:, pa0 + tc:pa0 + 2 * tc], ones_row[:], r1[0:1, 0:tc],
                                start=True, stop=True)
            p.on("tensor", mbc, wait=[rows_done], inc=[(pe, 1)])
            return pe.now

        NCB = 16
        Hg = AR.take([128, NG, 512], BF16)
        wbufH = [AR.take([128, 4, D], BF16), AR.take([128, 4, D], BF16)]
        Qg = AR.take([128, NG, 512], BF16)
        Qs = AR.take([128, NG, 512], BF16)
        Kg = AR.take([128, NG, 512], BF16)
        Vg = AR.take([128, NG, 512], BF16)
        e1a = AR.take([128, 512], F32)
        t1a = AR.take([128, 512], F32)
        swp = AR.take([128, 512], F32)
        Vp = AR.take([128, NCB, 65], BF16)
        Pb = AR.take([128, NCB, 512], BF16)
        o_s = AR.take([64, 512], F32)
        rec_s = AR.take([1, 512], F32)
        oatt = AR.take([128, NG, T_SEQ], BF16)

        slot_free = [ve.n] * 2
        slot_free_se = [se.n] * 2
        slot_i = 0
        PROJ = (("q", "wqT", Qg, 0), ("qs", "wqTs", Qs, 6), ("k", "wkT", Kg, 1), ("v", "wvT", Vg, 2))
        last_exp_for_st = {0: 0, 1: 0}
        last_pv_for_p = {0: 0, 1: 0}
        last_oevac = {0: 0, 1: 0}
        last_scatter = 0
        last_pv_all = 0
        wstep_mm_done = {}
        ln_free_a = 0

        for g4 in range(4):
            tg = slice(512 * g4, 512 * (g4 + 1))
            bc1 = lnchunk(lambda g, tg=tg: X[:, g, tg], EPS1r, 512, 0,
                          [x_loaded, (ve, ln_free_a), (ve, max(slot_free)),
                           (se, max(slot_free_se))])
            def hbt(e, tg=tg):
                ins = None
                for g in range(NG):
                    e.tensor_tensor(e1a[:], X[:, g, tg], P6[:, 0:512], ALU.mult)
                    ins = e.tensor_tensor(Hg[:, g, :], e1a[:], P6[:, 512:1024], ALU.add)
                return ins
            p.on("vector", hbt, wait=[bc1], n=16)
            ln_free_a = ve.n
            h_all = ve.now

            for pi, (nmp, wname, out_t, bidx) in enumerate(PROJ):
                ws = g4 * 4 + pi
                for half in range(2):
                    hstep = 2 * ws + half
                    dma_w = [(pe, wstep_mm_done.get(hstep - 2, 0))] if hstep >= 2 else [consts_ld]
                    for j in range(4):
                        jj = half * 4 + j
                        p.dma("gpsimd", wbufH[hstep % 2][:, j, :],
                              dw[wname][jj * 128:(jj + 1) * 128, :], ldw[hstep % 2],
                              wait=dma_w)
                w_ld = [ldw[0].now, ldw[1].now]
                for go in range(NG):
                    r = slot_i % 2
                    slot_i += 1
                    pv_view = P6[:, 512 * r:512 * (r + 1)]
                    def mproj(e, ws=ws, go=go, pv=pv_view):
                        ins = None
                        for j in range(NG):
                            wb = wbufH[(2 * ws + j // 4) % 2]
                            ins = e.matmul(pv, wb[:, j % 4, go * 128:(go + 1) * 128],
                                           Hg[:, j, :], start=(j == 0), stop=(j == NG - 1))
                        return ins
                    p.on("tensor", mproj,
                         wait=[h_all] + w_ld + [(ve, slot_free[r]), (se, slot_free_se[r])],
                         inc=[(pe, 1)])
                    wstep_mm_done[2 * ws] = pe.n
                    wstep_mm_done[2 * ws + 1] = pe.n
                    mm_done = pe.now
                    if nmp == "v":
                        p.on("scalar", lambda e, go=go, pv=pv_view: e.activation(
                            Vg[:, go, :], pv, AF.Identity, bias=bcol(2, go), scale=1.0),
                            wait=[mm_done], n=1)
                        slot_free_se[r] = se.n
                    else:
                        p.on("scalar", lambda e, go=go, pv=pv_view, b=bidx: e.activation(
                            e1a[:], pv, AF.Identity, bias=bcol(b, go), scale=1.0),
                            wait=[mm_done, ve.now], n=1)
                        slot_free_se[r] = se.n
                        p.on("scalar", lambda e, go=go: e.activation(
                            t1a[:], e1a[:], AF.Identity, bias=0.0, scale=cols[:, 32 + go:33 + go]),
                            n=1)
                        p.on("vector", lambda e: e.stream_shuffle(swp[:], e1a[:], SHUF_SWAP),
                             wait=[se.now], n=1)
                        p.on("vector", lambda e, out=out_t, go=go: e.scalar_tensor_tensor(
                            out[:, go, :], swp[:], ssgn[:, go:go + 1], t1a[:], ALU.mult, ALU.add),
                            n=1)
                    slot_free[r] = ve.n
            qkv_done = (ve.now, se.now)

            for hl in range(4):
                hh = g4 * 4 + hl
                go_out = hh // 2
                r0 = (hh % 2) * 64
                p.on("vector", lambda e: e.memset(Vp[:], 1.0),
                     wait=[qkv_done[0], (pe, last_pv_all)], n=1)
                def vtr(e, hl=hl):
                    ins = None
                    for g in range(NG):
                        ins = e.transpose(PTb[:, 128 * g:128 * (g + 1)],
                                          Vg[:, g, hl * 128:(hl + 1) * 128],
                                          ident_bf[:])
                    return ins
                p.on("tensor", vtr, wait=[qkv_done[1], (ve, ve.n)], inc=[(pe, 1)])
                def vcp(e):
                    ins = None
                    for g in range(NG):
                        e.tensor_copy(Vp[:, 2 * g, 0:64], PTb[:, 128 * g:128 * g + 64])
                        ins = e.tensor_copy(Vp[:, 2 * g + 1, 0:64], PTb[:, 128 * g + 64:128 * (g + 1)])
                    return ins
                p.on("vector", vcp, wait=[pe.now], n=16)
                vp_ready = ve.n

                for qc in range(4):
                    gqc = hh * 4 + qc
                    for cbk in range(NCB):
                        stp = cbk % 2
                        stv = P6[:, 1024 + 512 * stp: 1024 + 512 * (stp + 1)]
                        def smm(e, cbk=cbk, qc=qc, hl=hl, stv=stv):
                            ins = None
                            b = 64 * (cbk % 2)
                            hs = slice(hl * 128, (hl + 1) * 128)
                            stat = Kg[b:b + 64, cbk // 2, hs]
                            for j in range(4):
                                cbq = 4 * qc + j
                                if cbq % 2 == cbk % 2:
                                    mov = Qg[b:b + 64, cbq // 2, hs]
                                elif cbk % 2 == 0:
                                    mov = Qs[0:64, cbq // 2, hs]
                                else:
                                    mov = Qs[64:128, (cbq // 2 + 7) % 8, hs]
                                ins = e.matmul(stv[:, j * 128:(j + 1) * 128],
                                               stat, mov, start=True, stop=True)
                            return ins
                        p.on("tensor", smm,
                             wait=[qkv_done[0], (se, last_exp_for_st[stp])], inc=[(pe, 1)])
                        p.on("scalar", lambda e, cbk=cbk, stv=stv, Pb=Pb: e.activation(
                            Pb[:, cbk, :], stv, AF.Exp, bias=0.0, scale=0.125),
                            wait=[pe.now, (pe, last_pv_for_p[0])], n=1)
                        last_exp_for_st[stp] = se.n
                    exps_done = se.now
                    pslot = gqc % 2
                    ov = P6[0:65, 2048 + 512 * pslot: 2048 + 512 * (pslot + 1)]
                    def pvm(e, ov=ov, Pb=Pb):
                        ins = None
                        for cbk in range(NCB):
                            ins = e.matmul(ov, Vp[:, cbk, :], Pb[:, cbk, :],
                                           start=(cbk == 0), stop=(cbk == NCB - 1))
                        return ins
                    p.on("tensor", pvm,
                         wait=[exps_done, (ve, last_oevac[pslot]), (ve, vp_ready)],
                         inc=[(pe, 1)])
                    last_pv_for_p[0] = pe.n
                    last_pv_all = pe.n
                    pv_done = pe.now
                    p.on("vector", lambda e, ov=ov: e.reciprocal(rec_s[:], ov[64:65, :]),
                         wait=[pv_done], n=1)
                    p.on("vector", lambda e, ov=ov: e.tensor_copy(o_s[:], ov[0:64, :]), n=1)
                    oev = ve.now
                    last_oevac[pslot] = ve.n
                    bcv = PTf[0:64, :]
                    p.on("tensor", lambda e: e.matmul(bcv, ones_row[0:1, 0:64], rec_s[:],
                                                      start=True, stop=True),
                         wait=[oev, (ve, last_scatter)], inc=[(pe, 1)])
                    bc_done = pe.now
                    for j in range(4):
                        cbq = 4 * qc + j
                        def sct(e, j=j, cbq=cbq, go_out=go_out, r0=r0):
                            return e.tensor_tensor(
                                oatt[r0:r0 + 64, go_out, cbq:2048:16],
                                o_s[:, j * 128:(j + 1) * 128],
                                bcv[:, j * 128:(j + 1) * 128], ALU.mult)
                        p.on("vector", sct, wait=[bc_done], n=1)
                    last_scatter = ve.n
        attn_done = ve.now

        for half in range(2):
            hstep = 32 + half
            for j in range(4):
                jj = half * 4 + j
                p.dma("gpsimd", wbufH[hstep % 2][:, j, :],
                      dw["woT"][jj * 128:(jj + 1) * 128, :], ldw[hstep % 2],
                      wait=[(pe, wstep_mm_done[hstep - 2])])
        wo_ld = [ldw[0].now, ldw[1].now]
        for go in range(NG):
            for c in range(4):
                r = slot_i % 2
                slot_i += 1
                pv_view = P6[:, 512 * r:512 * (r + 1)]
                def mwo(e, go=go, c=c, pv=pv_view):
                    ins = None
                    for j in range(NG):
                        wb = wbufH[j // 4]
                        ins = e.matmul(pv, wb[:, j % 4, go * 128:(go + 1) * 128],
                                       oatt[:, j, 512 * c:512 * (c + 1)],
                                       start=(j == 0), stop=(j == NG - 1))
                    return ins
                p.on("tensor", mwo,
                     wait=[attn_done] + wo_ld + [(ve, slot_free[r]), (se, slot_free_se[r])],
                     inc=[(pe, 1)])
                p.on("vector", lambda e, go=go, c=c, pv=pv_view: e.scalar_tensor_tensor(
                    X[:, go, 512 * c:512 * (c + 1)], pv, bcol(3, go),
                    X[:, go, 512 * c:512 * (c + 1)], ALU.add, ALU.add),
                    wait=[pe.now], n=1)
                slot_free[r] = ve.n

        if _dbg: print("attn arena:", AR.off)
        barrier()
        AR.reset()

        NF = 11
        H = AR.take([128, NG, T_SEQ], BF16)
        w1b = AR.take([128, NG, 2816], BF16)
        w2b = AR.take([128, NF, D], BF16)
        rwb = AR.take([128, NG, 8], BF16)
        lp = AR.take([128, 16, 8], F32)
        lpT = AR.take([8, T_SEQ], BF16)
        silu_s = AR.take([128, 512], F32)
        g_all = AR.take([128, NF, 512], BF16)

        e1m = lambda: T2[:, 0:512]
        for j in range(NG):
            p.dma("gpsimd", rwb[:, j, :], drw[j * 128:(j + 1) * 128, :], ldg)
        rw_ld = ldg.now

        ln_free = 0
        H_done = {}
        for c in range(4):
            cs = slice(512 * c, 512 * (c + 1))
            bc = lnchunk(lambda g, cs=cs: X[:, g, cs], EPS1r, 512, 0,
                         [(ve, ln_free)] if c > 0 else [ve.now])
            def hbt0(e, cs=cs):
                ins = None
                for g in range(NG):
                    e.tensor_tensor(e1m(), X[:, g, cs], P6[:, 0:512], ALU.mult)
                    ins = e.tensor_tensor(H[:, g, cs], e1m(), P6[:, 512:1024], ALU.add)
                return ins
            p.on("vector", hbt0, wait=[bc], n=16)
            ln_free = ve.n
        h_all0 = ve.now

        rt_free = 0
        for tc in range(16):
            ts = slice(tc * 128, (tc + 1) * 128)
            def mrt(e, ts=ts):
                ins = None
                for j in range(NG):
                    ins = e.matmul(P6[:, 1024:1032], H[:, j, ts], rwb[:, j, :],
                                   start=(j == 0), stop=(j == NG - 1))
                return ins
            p.on("tensor", mrt, wait=[h_all0, rw_ld, (ve, rt_free)], inc=[(pe, 1)])
            pl_done = pe.now
            t = lg[:, tc, :]
            pr_ = lp[:, tc, :]
            p.on("vector", lambda e, t=t: e.tensor_copy(t, P6[:, 1024:1032]), wait=[pl_done], n=1)
            rt_free = ve.n
            p.on("vector", lambda e, t=t: e.reduce_max(nm[:], t, axis=AX.X, negate=True), n=1)
            p.on("scalar", lambda e, t=t: e.activation(t, t, AF.Exp, bias=nm[:], scale=1.0),
                 wait=[ve.now], n=1)
            p.on("vector", lambda e, t=t: e.reduce_sum(rsum[:], t, axis=AX.X), wait=[se.now], n=1)
            p.on("vector", lambda e: e.reciprocal(rsum[:], rsum[:]), n=1)
            p.on("vector", lambda e, t=t: e.tensor_scalar(t, t, rsum[:], None, ALU.mult), n=1)
            p.on("vector", lambda e, t=t: e.reduce_max(m1[:], t, axis=AX.X), n=1)
            p.on("vector", lambda e, t=t: e.tensor_scalar(sc1[:], t, m1[:], None, ALU.is_ge), n=1)
            p.on("vector", lambda e: e.tensor_tensor(sc1[:], sc1[:], revidx[:], ALU.mult), n=1)
            p.on("vector", lambda e: e.reduce_max(m1[:], sc1[:], axis=AX.X), n=1)
            p.on("vector", lambda e: e.tensor_scalar(sc1[:], sc1[:], m1[:], None, ALU.is_ge), n=1)
            p.on("vector", lambda e, t=t: e.scalar_tensor_tensor(sc2[:], sc1[:], -1e9, t,
                                                                 ALU.mult, ALU.add), n=1)
            p.on("vector", lambda e: e.reduce_max(m1[:], sc2[:], axis=AX.X), n=1)
            p.on("vector", lambda e: e.tensor_scalar(sc2[:], sc2[:], m1[:], None, ALU.is_ge), n=1)
            p.on("vector", lambda e: e.tensor_tensor(sc2[:], sc2[:], revidx[:], ALU.mult), n=1)
            p.on("vector", lambda e: e.reduce_max(m1[:], sc2[:], axis=AX.X), n=1)
            p.on("vector", lambda e: e.tensor_scalar(sc2[:], sc2[:], m1[:], None, ALU.is_ge), n=1)
            p.on("vector", lambda e: e.tensor_tensor(sc1[:], sc1[:], sc2[:], ALU.add), n=1)
            p.on("vector", lambda e, t=t, pr_=pr_: e.tensor_tensor(pr_, t, sc1[:], ALU.mult), n=1)
        lp_done = ve.now

        lpt_cp = 0
        for tc in range(16):
            p.on("tensor", lambda e, tc=tc: e.transpose(PTf[0:8, 0:128], lp[:, tc, :], ident[:]),
                 wait=[lp_done, (ve, lpt_cp)], inc=[(pe, 1)])
            p.on("vector", lambda e, tc=tc: e.tensor_copy(lpT[:, tc * 128:(tc + 1) * 128],
                                                          PTf[0:8, 0:128]),
                 wait=[pe.now], n=1)
            lpt_cp = ve.n
        lpt_done = ve.now

        w1_free = 0
        w2_free = 0
        lpb_free = 0
        y_free_se = 0
        y_free_ve = 0
        g_free = {}
        out_free = {0: 0, 1: 0}
        for ei in range(8):
            for j in range(NG):
                p.dma("gpsimd", w1b[:, j, :], dw1[ei, j * 128:(j + 1) * 128, :], w1s,
                      wait=[(pe, w1_free)] if ei > 0 else [lp_done])
            for f in range(NF):
                p.dma("gpsimd", w2b[:, f, :], dw2[ei, f * 128:(f + 1) * 128, :], w2s,
                      wait=[(pe, w2_free)] if ei > 0 else [lp_done])
            w1_ld = w1s.now
            w2_ld = w2s.now
            for c in range(4):
                cs = slice(c * 512, (c + 1) * 512)
                p.on("tensor", lambda e, ei=ei, cs=cs: e.matmul(
                    P6[:, 2048:2560], selb[:, ei * 128:(ei + 1) * 128], lpT[:, cs],
                    start=True, stop=True),
                    wait=[lpt_done, (ve, lpb_free)], inc=[(pe, 1)])
                lpb_done = pe.now
                for f in range(NF):
                    def my(e, f=f, cs=cs):
                        for j in range(NG):
                            e.matmul(P6[:, 0:512], w1b[:, j, f * 128:(f + 1) * 128],
                                     H[:, j, cs], start=(j == 0), stop=(j == NG - 1))
                        ins = None
                        for j in range(NG):
                            ins = e.matmul(P6[:, 512:1024],
                                           w1b[:, j, 1408 + f * 128:1408 + f * 128 + 128],
                                           H[:, j, cs], start=(j == 0), stop=(j == NG - 1))
                        return ins
                    p.on("tensor", my, wait=[w1_ld, (se, y_free_se), (ve, y_free_ve),
                                             (pe, g_free.get(f, 0))],
                         inc=[(pe, 1)])
                    y_done = pe.now
                    p.on("scalar", lambda e: e.activation(silu_s[:], P6[:, 0:512], AF.Silu),
                         wait=[y_done], n=1)
                    y_free_se = se.n
                    p.on("vector", lambda e, f=f: e.tensor_tensor(
                        g_all[:, f, :], silu_s[:], P6[:, 512:1024], ALU.mult),
                        wait=[se.now], n=1)
                    y_free_ve = ve.n
                    p.on("vector", lambda e, f=f: e.tensor_tensor(
                        g_all[:, f, :], g_all[:, f, :], P6[:, 2048:2560], ALU.mult),
                        wait=[lpb_done], n=1)
                g_done = ve.now
                lpb_free = ve.n
                for go in range(NG):
                    ov = P6[:, 1024 + (go % 2) * 512:1024 + (go % 2) * 512 + 512]
                    def mo(e, go=go, ov=ov):
                        ins = None
                        for f in range(NF):
                            ins = e.matmul(ov, w2b[:, f, go * 128:(go + 1) * 128],
                                           g_all[:, f, :],
                                           start=(f == 0), stop=(f == NF - 1))
                        return ins
                    p.on("tensor", mo, wait=[g_done, w2_ld, (ve, out_free[go % 2])],
                         inc=[(pe, 1)])
                    if go == NG - 1:
                        for f in range(NF):
                            g_free[f] = pe.n
                    p.on("vector", lambda e, go=go, cs=cs, ov=ov: e.tensor_tensor(
                        X[:, go, cs], ov, X[:, go, cs], ALU.add),
                        wait=[pe.now], n=1)
                    out_free[go % 2] = ve.n
            w1_free = pe.n
            w2_free = pe.n

        if _dbg: print("moe0 arena:", AR.off)
        barrier()
        AR.reset()

        B_NI, B_NS, B_TC = 16, 16, 256
        B_NCH = T_SEQ // B_TC
        U1 = AR.take([128, B_NI, B_TC], F32)
        ln1 = U1[:, 0:NG, :]
        h_bfm = U1[:, 8:12, :].bitcast(BF16).rearrange("p a b -> p (a b)").rearrange(
            "p (a b) -> p a b", b=B_TC)
        u_s = U1
        SCR = AR.take([128, B_NI, 260], F32)
        tailb = AR.take([128, B_NI, 4], F32)
        sz = AR.take([128, B_NI, B_TC], BF16)
        dt_s = AR.take([128, B_NI, B_TC], F32)
        u_bf = AR.take([128, B_NI, B_TC], BF16)
        h_n = AR.take([128, B_NI, B_TC], F32)
        dbu = U1
        y_acc = AR.take([128, B_NI, B_TC], F32)
        state = AR.take([128, 256], F32)
        dA_t = AR.take([128, 2, B_TC], F32)
        B_s = AR.take([128, 2, B_TC], F32)
        winb = AR.take([128, 2, 8, 128], BF16)
        woutb = AR.take([128, 2, 16, 128], BF16)
        wxb = AR.take([128, B_NI, 96], BF16)
        wdtb = AR.take([64, 2048], BF16)
        dtr_bf = AR.take([64, B_TC], BF16)
        Br = AR.take([16, B_TC], BF16)
        Cr = AR.take([16, B_TC], BF16)
        sel16 = AR.take([16, 2048], BF16)
        A_all = AR.take([128, 256], F32)
        mcols = AR.take([128, 144], F32)

        ccol = lambda base, i: mcols[:, base + i:base + i + 1]
        EPS2r = mcols[0:1, 129:130]
        ONEc = mcols[:, 130:131]
        p.dma("sync", sel16[:], dconst["c_sel16b"], ld)
        p.dma("sync", mcols[:], dmcols, ld)
        p.dma("sync", A_all[:], dalog, ld)
        mld = ld.now
        p.on("scalar", lambda e: e.activation(A_all[:], A_all[:], AF.Exp), wait=[mld], n=1)
        p.on("vector", lambda e: e.tensor_scalar(A_all[:], A_all[:], -1.0, None, ALU.mult),
             wait=[se.now], n=1)

        for i in range(B_NI):
            p.dma("gpsimd", wxb[:, i, :], dwx[i * 128:(i + 1) * 128, :], lwx)
        p.dma("gpsimd", wdtb[:], dwdt, lwx)
        wx_ld = lwx.now
        p.on("vector", lambda e: e.memset(tailb[:], 0.0), n=1)
        p.on("vector", lambda e: e.memset(state[:], 0.0), n=1)

        ln_free = ve.n
        win_mm = {}
        wout_mm = {}
        win_step = 0
        wout_step = 0
        pb_free = {0: 0, 1: 0}
        pb_free_se = {0: 0, 1: 0}
        pc_free = 0
        pd_free_ve = 0
        pd_free_se = 0
        PBv = lambda s: P6[:, 512 + 512 * s:512 + 512 * s + B_TC]
        for c in range(B_NCH):
            cs = slice(c * B_TC, (c + 1) * B_TC)
            bc1 = lnchunk(lambda g, cs=cs: X[:, g, cs], EPS1r, B_TC, 0,
                          [(ve, ln_free)])
            def l1w(e, cs=cs):
                ins = None
                for g in range(NG):
                    e.tensor_tensor(T2[:, 0:B_TC], X[:, g, cs], P6[:, 0:B_TC], ALU.mult)
                    e.tensor_tensor(ln1[:, g], T2[:, 0:B_TC], P6[:, 256:256 + B_TC], ALU.add)
                    ins = e.tensor_tensor(X[:, g, cs], ln1[:, g], X[:, g, cs], ALU.add)
                return ins
            p.on("vector", l1w, wait=[bc1], n=24)
            ln_free = ve.n
            bc2 = lnchunk(lambda g: ln1[:, g], EPS2r, B_TC, 0, [(ve, ln_free)])
            aff_se = {}
            for g in range(NG):
                w2_ = [bc2] if g < 2 else [bc2, (se, aff_se[g - 2])]
                p.on("vector", lambda e, g=g: e.tensor_tensor(
                    T2[:, 0:B_TC] if g % 2 == 0 else T2[:, B_TC:2 * B_TC],
                    ln1[:, g], P6[:, 0:B_TC], ALU.mult), wait=w2_, n=1)
                p.on("vector", lambda e, g=g: e.tensor_tensor(
                    T2[:, 0:B_TC] if g % 2 == 0 else T2[:, B_TC:2 * B_TC],
                    T2[:, 0:B_TC] if g % 2 == 0 else T2[:, B_TC:2 * B_TC],
                    P6[:, 256:256 + B_TC], ALU.add), n=1)
                p.on("scalar", lambda e, g=g: e.activation(
                    h_bfm[:, g, :], T2[:, 0:B_TC] if g % 2 == 0 else T2[:, B_TC:2 * B_TC],
                    AF.Identity, bias=ccol(LNB, g), scale=ccol(LNG, g)),
                    wait=[ve.now], n=1)
                aff_se[g] = se.n
            ln_free = ve.n
            h_done = se.now

            for ei in range(32):
                slot = win_step % 2
                wsem = lwin[slot]
                p.dma("gpsimd", winb[:, slot, :, :],
                      dwin[ei, :, :].rearrange("(j p) k -> p j k", p=128),
                      wsem, wait=[(pe, win_mm.get(win_step - 2, 0))] if win_step >= 2 else [consts_ld])
                win_ld = wsem.now
                pslot = ei % 2
                def mxz2(e, slot=slot, pslot=pslot):
                    ins = None
                    for j in range(NG):
                        ins = e.matmul(PBv(pslot), winb[:, slot, j, :], h_bfm[:, j, :],
                                       start=(j == 0), stop=(j == NG - 1))
                    return ins
                p.on("tensor", mxz2,
                     wait=[h_done, win_ld, (ve, pb_free[pslot]), (se, pb_free_se[pslot])],
                     inc=[(pe, 1)])
                win_mm[win_step] = pe.n
                win_step += 1
                mmd = pe.now
                if ei < 16:
                    p.on("vector", lambda e, ei=ei, pslot=pslot: e.tensor_copy(
                        SCR[:, ei, 4:4 + B_TC], PBv(pslot)), wait=[mmd], n=1)
                    pb_free[pslot] = ve.n
                else:
                    p.on("scalar", lambda e, ei=ei, pslot=pslot: e.activation(
                        sz[:, ei - 16, :], PBv(pslot), AF.Silu), wait=[mmd], n=1)
                    pb_free_se[pslot] = se.n

            for i in range(B_NI):
                p.on("vector", lambda e, i=i: e.tensor_copy(SCR[:, i, 0:4], tailb[:, i, :]), n=1)
            conv_par = {0: 0, 1: 0}
            conv_entry = ve.n
            restores_done = ve.now
            for i in range(B_NI):
                cv = lambda i=i: T2[:, (i % 2) * B_TC:(i % 2) * B_TC + B_TC]
                p.on("scalar", lambda e, i=i: e.activation(
                    cv(i), SCR[:, i, 1:1 + B_TC], AF.Identity,
                    bias=ccol(CONVB, i), scale=ccol(CONVW, i * 4 + 0)),
                    wait=[(ve, conv_par[i % 2] or conv_entry), restores_done], n=1)
                for j in (1, 2, 3):
                    p.on("vector", lambda e, i=i, j=j: e.scalar_tensor_tensor(
                        cv(i), SCR[:, i, 1 + j:1 + j + B_TC], ccol(CONVW, i * 4 + j),
                        cv(i), ALU.mult, ALU.add), wait=[se.now], n=1)
                p.on("scalar", lambda e, i=i: e.activation(u_s[:, i, :], cv(i), AF.Silu),
                     wait=[(ve, ve.n), (pe, win_mm[win_step - 1])], n=1)
                p.on("vector", lambda e, i=i: e.tensor_copy(u_bf[:, i, :], u_s[:, i, :]),
                     wait=[se.now], n=1)
                p.on("vector", lambda e, i=i: e.tensor_copy(tailb[:, i, :], SCR[:, i, B_TC:B_TC + 4]), n=1)
                conv_par[i % 2] = ve.n
            u_done = (ve.now, se.now)

            def mxd(e):
                for i in range(B_NI):
                    e.matmul(P6[0:80, 2048:2048 + B_TC], wxb[:, i, 0:80], u_bf[:, i, :],
                             start=(i == 0), stop=(i == B_NI - 1))
                ins = None
                for i in range(B_NI):
                    ins = e.matmul(P6[0:16, 2304:2304 + B_TC], wxb[:, i, 80:96], u_bf[:, i, :],
                                   start=(i == 0), stop=(i == B_NI - 1))
                return ins
            p.on("tensor", mxd, wait=[u_done[0], wx_ld, (ve, pd_free_ve), (se, pd_free_se)],
                 inc=[(pe, 1)])
            xd_done = pe.now
            p.on("scalar", lambda e: e.activation(dtr_bf[:], P6[0:64, 2048:2048 + B_TC], AF.Identity),
                 wait=[xd_done], n=1)
            pd_free_se = se.n
            p.on("vector", lambda e: e.tensor_copy(Br[:], P6[64:80, 2048:2048 + B_TC]),
                 wait=[xd_done, (se, pd_free_se)], n=1)
            p.on("vector", lambda e: e.tensor_copy(Cr[:], P6[0:16, 2304:2304 + B_TC]), n=1)
            pd_free_ve = ve.n
            xd_ev = (ve.now, se.now)

            for i in range(B_NI):
                pslot = i % 2
                p.on("tensor", lambda e, i=i, pslot=pslot: e.matmul(
                    PBv(pslot), wdtb[:, i * 128:(i + 1) * 128], dtr_bf[:],
                    start=True, stop=True),
                    wait=[xd_ev[1], (ve, pb_free[pslot]), (se, pb_free_se[pslot])],
                    inc=[(pe, 1)])
                mmd = pe.now
                p.on("scalar", lambda e, i=i, pslot=pslot: e.activation(
                    T2[:, 0:B_TC], PBv(pslot), AF.Abs, bias=ccol(BDT, i)),
                    wait=[mmd, (ve, ve.n)], n=1)
                abs_done = se.n
                p.on("scalar", lambda e: e.activation(T2[:, 0:B_TC], T2[:, 0:B_TC], AF.Exp,
                                                      scale=-1.0), n=1)
                p.on("scalar", lambda e: e.activation(T2[:, 0:B_TC], T2[:, 0:B_TC], AF.Ln,
                                                      bias=ONEc), n=1)
                pb_free_se[pslot] = se.n
                p.on("vector", lambda e, i=i, pslot=pslot: e.tensor_scalar(
                    T2[:, B_TC:2 * B_TC], PBv(pslot), ccol(BDT, i), 0.0,
                    ALU.add, ALU.max), wait=[mmd, (se, abs_done)], n=1)
                pb_free[pslot] = ve.n
                p.on("vector", lambda e, i=i: e.tensor_tensor(
                    dt_s[:, i, :], T2[:, B_TC:2 * B_TC], T2[:, 0:B_TC], ALU.add),
                    wait=[se.now], n=1)
            dt_done = ve.now

            p.on("gpsimd", lambda e: e.tensor_tensor(
                SCR[:, :, 0:B_TC], dt_s[:], u_s[:], ALU.mult), wait=[dt_done], n=1)
            du_done = sg.now
            for i in range(B_NI):
                p.on("vector", lambda e, i=i: e.tensor_scalar(
                    y_acc[:, i, :], u_s[:, i, :], ccol(DP, i), None, ALU.mult),
                    wait=[du_done] if i == 0 else [], n=1)
            yinit_done = ve.n
            scan_par = {0: 0, 1: 0}
            bs_free = {0: 0, 1: 0}

            pc_freep = {0: pc_free, 1: pc_free}
            scans_done = ve.n
            dbu_g = 0
            for n_ in range(B_NS):
                pcb = 1536 + 1024 * (n_ % 2)
                def mbcn(e, n_=n_, pcb=pcb):
                    e.matmul(P6[:, pcb:pcb + B_TC], sel16[:, n_ * 128:(n_ + 1) * 128], Br[:],
                             start=True, stop=True)
                    return e.matmul(P6[:, pcb + 256:pcb + 256 + B_TC],
                                    sel16[:, n_ * 128:(n_ + 1) * 128],
                                    Cr[:], start=True, stop=True)
                p.on("tensor", mbcn, wait=[xd_ev[0], (ve, pc_freep[n_ % 2])], inc=[(pe, 1)])
                bc_done = pe.now
                p.on("scalar", lambda e, pcb=pcb, n_=n_: e.activation(
                    B_s[:, n_ % 2, :], P6[:, pcb:pcb + B_TC], AF.Identity),
                    wait=[bc_done, (sg, bs_free[n_ % 2])], n=1)
                bsc = se.now
                p.on("gpsimd", lambda e, n_=n_: e.tensor_tensor(
                    dbu[:], SCR[:, :, 0:B_TC],
                    B_s[:, n_ % 2, :].unsqueeze(1).broadcast_to([128, B_NI, B_TC]),
                    ALU.mult),
                    wait=[bsc, (ve, scans_done), (ve, yinit_done)], n=1)
                bs_free[n_ % 2] = sg.n
                dbu_g = sg.n
                for i in range(B_NI):
                    sl = i % 2
                    p.on("scalar", lambda e, i=i, n_=n_, sl=sl: e.activation(
                        dA_t[:, sl, :], dt_s[:, i, :], AF.Exp,
                        scale=A_all[:, i * 16 + n_:i * 16 + n_ + 1]),
                        wait=[(ve, scan_par[sl]), (ve, dt_done[1])], n=1)
                    p.on("vector", lambda e, i=i, n_=n_, sl=sl: e.tensor_tensor_scan(
                        h_n[:, i, :], dA_t[:, sl, :], dbu[:, i, :],
                        state[:, n_ * 16 + i:n_ * 16 + i + 1], ALU.mult, ALU.add),
                        wait=[se.now, (sg, dbu_g)], n=1)
                    scan_par[sl] = ve.n
                scans_done = ve.n
                p.on("vector", lambda e, n_=n_: e.tensor_copy(
                    state[:, n_ * 16:(n_ + 1) * 16], h_n[:, :, B_TC - 1]), n=1)
                p.on("vector", lambda e, pcb=pcb: e.tensor_tensor(
                    h_n[:], h_n[:],
                    P6[:, pcb + 256:pcb + 256 + B_TC].unsqueeze(1).broadcast_to(
                        [128, B_NI, B_TC]),
                    ALU.mult), n=1)
                p.on("vector", lambda e: e.tensor_tensor(y_acc[:], y_acc[:], h_n[:], ALU.add),
                     n=1)
                pc_freep[n_ % 2] = ve.n
            pc_free = ve.n

            for i in range(B_NI):
                p.on("vector", lambda e, i=i: e.tensor_tensor(
                    u_bf[:, i, :], y_acc[:, i, :], sz[:, i, :], ALU.mult), n=1)
            yg_done = ve.now

            for go in range(NG):
                wslot = wout_step % 2
                p.dma("gpsimd", woutb[:, wslot, :, :],
                      dwout[go, :, :].rearrange("(i p) k -> p i k", p=128),
                      lwout[wslot], wait=[(pe, wout_mm.get(wout_step - 2, 0))] if wout_step >= 2 else [consts_ld])
                wo_ldm = lwout[wslot].now
                pslot = go % 2
                def mwom(e, pslot=pslot, wslot=wslot):
                    ins = None
                    for i in range(B_NI):
                        ins = e.matmul(PBv(pslot), woutb[:, wslot, i, :], u_bf[:, i, :],
                                       start=(i == 0), stop=(i == B_NI - 1))
                    return ins
                p.on("tensor", mwom,
                     wait=[yg_done, wo_ldm, (ve, pb_free[pslot]), (se, pb_free_se[pslot])],
                     inc=[(pe, 1)])
                wout_mm[wout_step] = pe.n
                wout_step += 1
                p.on("vector", lambda e, go=go, cs=cs, pslot=pslot: e.tensor_tensor(
                    X[:, go, cs], PBv(pslot), X[:, go, cs], ALU.add),
                    wait=[pe.now], n=1)
                pb_free[pslot] = ve.n

        if _dbg: print("mamba arena:", AR.off)
        barrier()
        AR.reset()

        H1 = AR.take([128, NG, T_SEQ], BF16)
        ewb2m = [AR.take([128, NG, D], BF16), AR.take([128, NG, D], BF16)]
        gwb = AR.take([128, NG, 8], BF16)
        xe2 = [AR.take([128, NG, 512], BF16), AR.take([128, NG, 512], BF16)]
        cbm = AR.take([128, 16, 8], F32)
        cbT = AR.take([8, T_SEQ], BF16)
        ebs = AR.take([8, D], BF16)

        p.dma("gpsimd", ebs[:], deb, ldg)
        eb_ld = ldg.now
        for j in range(NG):
            p.dma("gpsimd", gwb[:, j, :], dgw[j * 128:(j + 1) * 128, :], ldg)
        gw_ld = ldg.now

        ln_free = ve.n
        for c in range(4):
            cs = slice(512 * c, 512 * (c + 1))
            bc = lnchunk(lambda g, cs=cs: X[:, g, cs], EPS1r, 512, 0, [(ve, ln_free)])
            def hbt1(e, cs=cs):
                ins = None
                for g in range(NG):
                    e.tensor_tensor(T2[:, 0:512], X[:, g, cs], P6[:, 0:512], ALU.mult)
                    ins = e.tensor_tensor(H1[:, g, cs], T2[:, 0:512], P6[:, 512:1024], ALU.add)
                return ins
            p.on("vector", hbt1, wait=[bc], n=16)
            ln_free = ve.n
        h_all1 = ve.now

        p.on("tensor", lambda e: e.matmul(P6[:, 1032:1040], ones_row[:], gbs[:],
                                          start=True, stop=True),
             wait=[consts_ld, h_all1], inc=[(pe, 1)])
        gbb_done = pe.now
        rt_free = 0
        for tc in range(16):
            ts = slice(tc * 128, (tc + 1) * 128)
            def mrt1(e, ts=ts):
                ins = None
                for j in range(NG):
                    ins = e.matmul(P6[:, 1024:1032], H1[:, j, ts], gwb[:, j, :],
                                   start=(j == 0), stop=(j == NG - 1))
                return ins
            p.on("tensor", mrt1, wait=[h_all1, gw_ld, (ve, rt_free)], inc=[(pe, 1)])
            pl_done = pe.now
            t = lg[:, tc, :]
            pr_ = cbm[:, tc, :]
            p.on("vector", lambda e, t=t: e.tensor_copy(t, P6[:, 1024:1032]),
                 wait=[pl_done, gbb_done], n=1)
            p.on("vector", lambda e, t=t: e.tensor_tensor(t, t, P6[:, 1032:1040], ALU.add), n=1)
            rt_free = ve.n
            p.on("vector", lambda e, t=t: e.tensor_scalar_max(t, t, 1e-4), n=1)
            p.on("vector", lambda e, t=t: e.reduce_max(nm[:], t, axis=AX.X, negate=True), n=1)
            p.on("scalar", lambda e, t=t: e.activation(t, t, AF.Exp, bias=nm[:], scale=1.0),
                 wait=[ve.now], n=1)
            p.on("vector", lambda e, t=t: e.reduce_sum(rsum[:], t, axis=AX.X), wait=[se.now], n=1)
            p.on("vector", lambda e: e.reciprocal(rsum[:], rsum[:]), n=1)
            p.on("vector", lambda e, t=t: e.tensor_scalar(t, t, rsum[:], None, ALU.mult), n=1)
            p.on("vector", lambda e, t=t: e.reduce_max(m1[:], t, axis=AX.X), n=1)
            p.on("vector", lambda e, t=t: e.tensor_scalar(sc1[:], t, m1[:], None, ALU.is_ge), n=1)
            p.on("vector", lambda e: e.tensor_tensor(sc1[:], sc1[:], revidx[:], ALU.mult), n=1)
            p.on("vector", lambda e: e.reduce_max(m1[:], sc1[:], axis=AX.X), n=1)
            p.on("vector", lambda e: e.tensor_scalar(sc1[:], sc1[:], m1[:], None, ALU.is_ge), n=1)
            p.on("vector", lambda e, t=t: e.scalar_tensor_tensor(sc2[:], sc1[:], -1e9, t,
                                                                 ALU.mult, ALU.add), n=1)
            p.on("vector", lambda e: e.reduce_max(m1[:], sc2[:], axis=AX.X), n=1)
            p.on("vector", lambda e: e.tensor_scalar(sc2[:], sc2[:], m1[:], None, ALU.is_ge), n=1)
            p.on("vector", lambda e: e.tensor_tensor(sc2[:], sc2[:], revidx[:], ALU.mult), n=1)
            p.on("vector", lambda e: e.reduce_max(m1[:], sc2[:], axis=AX.X), n=1)
            p.on("vector", lambda e: e.tensor_scalar(sc2[:], sc2[:], m1[:], None, ALU.is_ge), n=1)
            p.on("vector", lambda e: e.tensor_tensor(sc1[:], sc1[:], sc2[:], ALU.add), n=1)
            p.on("vector", lambda e, t=t: e.tensor_scalar_max(sc2[:], t, 0.1), n=1)
            p.on("vector", lambda e: e.tensor_tensor(sc2[:], sc2[:], sc1[:], ALU.mult), n=1)
            p.on("vector", lambda e: e.reduce_sum(m1[:], sc2[:], axis=AX.X), n=1)
            p.on("vector", lambda e: e.reciprocal(m1[:], m1[:]), n=1)
            p.on("vector", lambda e, pr_=pr_: e.tensor_scalar(pr_, sc2[:], m1[:], 0.5,
                                                              ALU.mult, ALU.mult), n=1)
        cb_done = ve.now

        cbt_cp = 0
        for tc in range(16):
            p.on("tensor", lambda e, tc=tc: e.transpose(PTf[0:8, 0:128], cbm[:, tc, :], ident[:]),
                 wait=[cb_done, (ve, cbt_cp)], inc=[(pe, 1)])
            p.on("vector", lambda e, tc=tc: e.tensor_copy(cbT[:, tc * 128:(tc + 1) * 128],
                                                          PTf[0:8, 0:128]),
                 wait=[pe.now], n=1)
            cbt_cp = ve.n
        cbt_done = ve.now

        out_free = {0: 0, 1: 0}
        for c in range(4):
            cs = slice(c * 512, (c + 1) * 512)
            pslot = c % 2
            for go in range(NG):
                p.on("tensor", lambda e, cs=cs, go=go, pslot=pslot: e.matmul(
                    P6[:, pslot * 512:pslot * 512 + 512],
                    ebs[:, go * 128:(go + 1) * 128], cbT[:, cs],
                    start=True, stop=True),
                    wait=[cbt_done, eb_ld, (ve, out_free[pslot])], inc=[(pe, 1)])
                p.on("vector", lambda e, cs=cs, go=go, pslot=pslot: e.tensor_tensor(
                    X[:, go, cs], P6[:, pslot * 512:pslot * 512 + 512], X[:, go, cs], ALU.add),
                    wait=[pe.now], n=1)
                out_free[pslot] = ve.n
                pslot = (pslot + 1) % 2

        ew_mm_last = {}
        xe_free_pe = {0: 0, 1: 0}
        plb_free = {0: 0, 1: 0}
        for ei in range(8):
            ewb = ewb2m[ei % 2]
            for j in range(NG):
                p.dma("gpsimd", ewb[:, j, :], dew[ei, j * 128:(j + 1) * 128, :], ews2[ei % 2],
                      wait=[(pe, ew_mm_last[ei - 2])] if ei >= 2 else [cb_done])
            ew_ld = ews2[ei % 2].now
            for c in range(4):
                cs = slice(c * 512, (c + 1) * 512)
                cpar = c % 2
                xe = xe2[cpar]
                plv = P6[:, 1024 + 512 * 0:1536] if True else None
                plo = 1024 if cpar == 0 else 2560
                p.on("tensor", lambda e, ei=ei, cs=cs, plo=plo: e.matmul(
                    P6[:, plo:plo + 512], selb[:, ei * 128:(ei + 1) * 128], cbT[:, cs],
                    start=True, stop=True),
                    wait=[cbt_done, (ve, plb_free[cpar])], inc=[(pe, 1)])
                plb_done = pe.now
                def mxe(e, cs=cs, xe=xe, plo=plo):
                    ins = None
                    for g in range(NG):
                        ins = e.tensor_tensor(xe[:, g, :], H1[:, g, cs], P6[:, plo:plo + 512],
                                              ALU.mult)
                    return ins
                p.on("vector", mxe, wait=[plb_done, (pe, xe_free_pe[cpar])], n=8)
                xe_done = ve.now
                plb_free[cpar] = ve.n
                for go in range(NG):
                    pslot = go % 2
                    def mex(e, go=go, pslot=pslot, xe=xe, ewb=ewb):
                        ins = None
                        for j in range(NG):
                            ins = e.matmul(P6[:, pslot * 512:pslot * 512 + 512],
                                           ewb[:, j, go * 128:(go + 1) * 128], xe[:, j, :],
                                           start=(j == 0), stop=(j == NG - 1))
                        return ins
                    p.on("tensor", mex, wait=[xe_done, ew_ld, (ve, out_free[pslot])],
                         inc=[(pe, 1)])
                    if go == NG - 1:
                        xe_free_pe[cpar] = pe.n
                    p.on("vector", lambda e, go=go, cs=cs, pslot=pslot: e.tensor_tensor(
                        X[:, go, cs], P6[:, pslot * 512:pslot * 512 + 512], X[:, go, cs],
                        ALU.add),
                        wait=[pe.now], n=1)
                    out_free[pslot] = ve.n
            ew_mm_last[ei] = pe.n
        res_done = ve.now

        xst2 = [T2, AR.take([128, 1024], F32)]
        st_dma = {}
        st_cp = {}
        for tt in range(16):
            xt = xst2[tt % 2]
            for half in range(2):
                def trs(e, tt=tt, half=half):
                    ins = None
                    for g4 in range(4):
                        g = half * 4 + g4
                        ins = e.transpose(PTf[:, g4 * 128:(g4 + 1) * 128],
                                          X[:, g, tt * 128:(tt + 1) * 128], ident[:])
                    return ins
                w = [res_done]
                if (tt, half) != (0, 0):
                    w.append((ve, st_cp.get((tt, half - 1), st_cp.get((tt - 1, 1), 0))))
                if tt >= 2:
                    w.append(st_dma[tt - 2])
                p.on("tensor", trs, wait=w, inc=[(pe, 1)])
                def cps(e, tt=tt, half=half, xt=xt):
                    ins = None
                    for g4 in range(4):
                        g = half * 4 + g4
                        ins = e.tensor_copy(xt[:, g * 128:(g + 1) * 128],
                                            PTf[:, g4 * 128:(g4 + 1) * 128])
                    return ins
                p.on("vector", cps, wait=[pe.now], n=4)
                st_cp[(tt, half)] = ve.n
            p.dma("sync", dxout[tt * 128:(tt + 1) * 128, :], xt[:, 0:1024], st2[tt % 2],
                  wait=[(ve, st_cp[(tt, 1)])])
            st_dma[tt] = st2[tt % 2].now
        p.on("sync", lambda e: None, wait=[st2[0].now, st2[1].now])
        p.finalize()
    return nc


def host_fused(inputs, core):
    cs = make_consts()
    x = np.ascontiguousarray(np.asarray(inputs["L0_attn_x"][core], dtype=np.float32))
    cos_row = np.asarray(inputs["L0_attn_cos"])[T_SEQ]
    sin_row = np.asarray(inputs["L0_attn_sin"])[T_SEQ]
    ccolh = np.tile(cos_row, 16).reshape(8, 128).T.astype(np.float32)
    scolh = np.tile(sin_row, 16).reshape(8, 128).T.astype(np.float32)
    colsh = np.zeros((128, 72), np.float32)
    for i, bn in enumerate(("bq", "bk", "bv", "bo")):
        colsh[:, 8 * i:8 * (i + 1)] = np.asarray(inputs["L0_attn_" + bn]).reshape(8, 128).T
    colsh[:, 32:40] = ccolh
    colsh[:, 40:48] = scolh
    colsh[:, 48:56] = np.roll(np.asarray(inputs["L0_attn_bq"]), -64).reshape(8, 128).T
    colsh[:, 64] = 1e-5
    wqT = np.ascontiguousarray(np.asarray(inputs["L0_attn_wq"]).T)
    W_in = np.asarray(inputs["L1_attn_W_in"]).astype(np.float32)
    W_out = np.asarray(inputs["L1_attn_W_out"]).astype(np.float32)
    W_x = np.asarray(inputs["L1_attn_W_x"]).astype(np.float32)
    W_dt = np.asarray(inputs["L1_attn_W_dt"]).astype(np.float32)
    winT_t = np.ascontiguousarray(W_in.T.reshape(1024, 32, 128).transpose(1, 0, 2))
    woutT_t = np.ascontiguousarray(W_out.T.reshape(2048, 8, 128).transpose(1, 0, 2))
    conv_w = np.asarray(inputs["L1_attn_conv_w"]).astype(np.float32)
    mcolsh = np.zeros((128, 144), np.float32)
    cw = conv_w[:, 0, :].reshape(16, 128, 4)
    for i in range(16):
        for j in range(4):
            mcolsh[:, i * 4 + j] = cw[i, :, j]
    mcolsh[:, 64:80] = np.asarray(inputs["L1_attn_conv_b"]).reshape(16, 128).T
    mcolsh[:, 80:96] = np.asarray(inputs["L1_attn_b_dt"]).reshape(16, 128).T
    mcolsh[:, 96:112] = np.asarray(inputs["L1_attn_Dp"]).reshape(16, 128).T
    mcolsh[:, 112:120] = np.asarray(inputs["L1_attn_ln_g"]).reshape(8, 128).T
    mcolsh[:, 120:128] = np.asarray(inputs["L1_attn_ln_b"]).reshape(8, 128).T
    mcolsh[:, 128] = 1e-5
    mcolsh[:, 129] = 1e-6
    mcolsh[:, 130] = 1.0
    alog = np.asarray(inputs["L1_attn_A_log"]).astype(np.float32)
    alog_t = np.zeros((128, 256), np.float32)
    for i in range(16):
        for n in range(16):
            alog_t[:, i * 16 + n] = alog[i * 128:(i + 1) * 128, n]
    sel16h = np.concatenate([np.tile(np.eye(16, dtype=np.float32)[:, n:n + 1], (1, 128))
                             for n in range(16)], axis=1).astype(ml_dtypes.bfloat16)
    selh = np.concatenate([np.tile(np.eye(8, dtype=np.float32)[:, e:e + 1], (1, 128))
                           for e in range(8)], axis=1)
    m = {
        "xin": x,
        "wqT": wqT,
        "wqTs": np.ascontiguousarray(np.roll(wqT, -64, axis=1)),
        "wkT": np.ascontiguousarray(np.asarray(inputs["L0_attn_wk"]).T),
        "wvT": np.ascontiguousarray(np.asarray(inputs["L0_attn_wv"]).T),
        "woT": np.ascontiguousarray(np.asarray(inputs["L0_attn_wo"]).T),
        "cols": colsh,
        "rw": np.asarray(inputs["L0_ffn_router_weight"]).astype(np.float32),
        "w1": np.asarray(inputs["L0_ffn_w1"]).astype(np.float32),
        "w2": np.asarray(inputs["L0_ffn_w2"]).astype(np.float32),
        "winT": winT_t,
        "woutT": woutT_t,
        "wxT": np.ascontiguousarray(W_x.T),
        "wdtT": np.ascontiguousarray(W_dt.T),
        "alog": alog_t,
        "mcols": mcolsh,
        "gw": np.asarray(inputs["L1_ffn_gate_W"]).astype(np.float32),
        "gb": np.asarray(inputs["L1_ffn_gate_b"]).astype(np.float32).reshape(1, 8),
        "ew": np.asarray(inputs["L1_ffn_expert_W"]).astype(np.float32),
        "eb": np.asarray(inputs["L1_ffn_expert_b"]).astype(np.float32),
        "c_ident_f32": cs["c_ident_f32"],
        "c_ident_bf": cs["c_ident_bf"],
        "c_ones_row": cs["c_ones_row"],
        "c_ones_col": cs["c_ones_col"],
        "c_pm": cs["c_pm"],
        "c_revidx": cs["c_revidx"],
        "c_sel": selh,
        "c_selb": selh.astype(ml_dtypes.bfloat16),
        "c_sel16b": sel16h,
    }
    return m


_CACHE = {}


def kernel(**inputs):
    if "nc" not in _CACHE:
        _CACHE["nc"] = build_fused()
    nc = _CACHE["nc"]
    in_maps = [host_fused(inputs, c) for c in range(8)]
    res = run_bass_kernel_spmd(nc, in_maps, core_ids=list(range(8)))
    return np.stack([res.results[c]["xout"] for c in range(8)])


# version 13 (final)
# speedup vs baseline: 1.0826x; 1.0011x over previous
import numpy as np
from contextlib import ExitStack
import concourse.bass as bass
import concourse.mybir as mybir

F32 = mybir.dt.float32
BF16 = mybir.dt.bfloat16
AF = mybir.ActivationFunctionType
ALU = mybir.AluOpType
AX = mybir.AxisListType

ENGINES = ("sync", "gpsimd", "tensor", "vector", "scalar")

_AUTOSYNC_OPS = frozenset((
    "tensor_tensor", "tensor_add", "tensor_sub", "tensor_mul", "tensor_max",
    "tensor_scalar", "tensor_single_scalar", "tensor_scalar_add",
    "tensor_scalar_sub", "tensor_scalar_mul", "tensor_scalar_max",
    "tensor_scalar_min", "tensor_relu", "tensor_copy", "tensor_reduce",
    "reduce_max", "reduce_sum", "scalar_tensor_tensor", "tensor_tensor_scan",
    "memset", "memzero", "reciprocal", "stream_shuffle", "select",
    "copy_predicated", "transpose", "bn_stats", "bn_aggr", "max", "max_index",
    "activation", "sign", "square", "sqrt", "mul", "copy", "add", "iota",
    "affine_select", "partition_broadcast", "partition_all_reduce",
))


class Sem:
    __slots__ = ("h", "n", "name")

    def __init__(self, h, name):
        self.h = h
        self.n = 0
        self.name = name

    @property
    def now(self):
        return (self, self.n)


class AutoSyncEngine:

    def __init__(self, eng, sem):
        self._eng = eng
        self._sem = sem
        self._n = 0

    def wait_ge(self, sem, v):
        return self._eng.wait_ge(sem, v)

    def __getattr__(self, name):
        attr = getattr(self._eng, name)
        if name not in _AUTOSYNC_OPS:
            return attr

        def wrapped(*a, **k):
            if self._n > 0:
                self._eng.wait_ge(self._sem, self._n)
            ins = attr(*a, **k)
            ins.then_inc(self._sem, 1)
            self._n += 1
            return ins

        return wrapped


class Prog:
    AUTO = ("vector", "scalar", "gpsimd")

    def __init__(self, nc, es: ExitStack):
        self.nc = nc
        self.es = es
        self.steps = {e: [] for e in ENGINES}
        self.waited = {e: {} for e in ENGINES}
        self.sv = self.sem("auto_v")
        self.ss = self.sem("auto_s")
        self.sg = self.sem("auto_g")
        self._auto = {"vector": self.sv, "scalar": self.ss, "gpsimd": self.sg}

    def sem(self, name):
        return Sem(self.es.enter_context(self.nc.semaphore(name)), name)

    def sbuf(self, name, shape, dtype):
        return self.es.enter_context(self.nc.sbuf_tensor(name, shape, dtype))

    def psum(self, name, shape, dtype=F32):
        return self.es.enter_context(self.nc.psum_tensor(name, shape, dtype))

    def on(self, engine, fn, wait=(), inc=(), n=1):
        w = []
        seen = self.waited[engine]
        for s, v in wait:
            if v > 0 and seen.get(id(s), -1) < v:
                w.append((s, v))
                seen[id(s)] = v
        if engine in self._auto and not inc:
            s = self._auto[engine]
            s.n += n
            self.steps[engine].append((w, fn, (), s.n))
            return s.n
        for s, k in inc:
            s.n += k
        self.steps[engine].append((w, fn, list(inc), None))
        return None

    def dma(self, engine, out, in_, sem, wait=()):
        self.on(engine, lambda e, o=out, i=in_: e.dma_start(out=o, in_=i),
                wait=wait, inc=[(sem, 16)])

    def replay(self, engine, eng):
        if engine in self._auto:
            proxy = AutoSyncEngine(eng, self._auto[engine].h)
            for w, fn, incs, after in self.steps[engine]:
                for s, v in w:
                    eng.wait_ge(s.h, v)
                ins = fn(proxy)
                if after is None:
                    for s, k in incs:
                        ins = ins.then_inc(s.h, k)
                else:
                    assert proxy._n == after, (
                        f"{engine} step declared wrong n: have {proxy._n}, expected {after}")
            return
        for w, fn, incs, _ in self.steps[engine]:
            for s, v in w:
                eng.wait_ge(s.h, v)
            ins = fn(eng)
            for s, k in incs:
                assert ins is not None, f"step on {engine} has inc but returned None"
                ins = ins.then_inc(s.h, k)

    def finalize(self):
        with self.nc.Block() as block:
            @block.sync
            def _(eng):
                self.replay("sync", eng)

            @block.gpsimd
            def _(eng):
                self.replay("gpsimd", eng)

            @block.tensor
            def _(eng):
                self.replay("tensor", eng)

            @block.vector
            def _(eng):
                self.replay("vector", eng)

            @block.scalar
            def _(eng):
                self.replay("scalar", eng)


def make_consts():
    ident = np.eye(128, dtype=np.float32)
    import ml_dtypes
    consts = {
        "c_ident_f32": ident,
        "c_ident_bf": ident.astype(ml_dtypes.bfloat16),
        "c_ident2": np.vstack([np.eye(64), np.eye(64)]).astype(ml_dtypes.bfloat16),
        "c_ones_row": np.ones((1, 128), dtype=np.float32),
        "c_ones_col": np.ones((128, 1), dtype=np.float32),
        "c_pm": np.tile(np.array([[-1.0], [1.0]], np.float32), (64, 1)),
        "c_revidx": np.tile(np.arange(8, 0, -1, dtype=np.float32)[None, :], (128, 1)),
    }
    return consts


SHUF_SWAP = [i ^ 1 for i in range(32)]
import ml_dtypes
from concourse.bass_utils import run_bass_kernel_spmd

U8 = mybir.dt.uint8
T_SEQ = 2048
D = 1024
NG = 8


class Arena:
    def __init__(self, p, name, nbytes):
        self.t = p.sbuf(name, [128, nbytes], U8)
        self.off = 0

    def take(self, shape, dtype):
        bpe = 4 if dtype == F32 else 2
        free = 1
        for s in shape[1:]:
            free *= s
        nb = free * bpe
        off = self.off
        self.off = (self.off + nb + 31) & ~31
        v = self.t[:, off:off + nb].bitcast(dtype)
        if len(shape) == 3:
            v = v.rearrange("p (a b) -> p a b", a=shape[1])
        elif len(shape) == 4:
            v = v.rearrange("p (a b c) -> p a b c", a=shape[1], b=shape[2])
        if shape[0] != 128:
            v = v[0:shape[0]]
        return v

    def reset(self):
        self.off = 0


def build_fused():
    nc = bass.Bass("TRN2", target_bir_lowering=False, debug=False)
    dxin = nc.dram_tensor("xin", [T_SEQ, D], F32, kind="ExternalInput").ap()
    dxout = nc.dram_tensor("xout", [T_SEQ, D], F32, kind="ExternalOutput").ap()
    dw = {}
    for w in ("wqT", "wqTs", "wkT", "wvT", "woT"):
        dw[w] = nc.dram_tensor(w, [D, D], F32, kind="ExternalInput").ap()
    dcols = nc.dram_tensor("cols", [128, 72], F32, kind="ExternalInput").ap()
    drw = nc.dram_tensor("rw", [D, 8], F32, kind="ExternalInput").ap()
    dw1 = nc.dram_tensor("w1", [8, D, 2816], F32, kind="ExternalInput").ap()
    dw2 = nc.dram_tensor("w2", [8, 1408, D], F32, kind="ExternalInput").ap()
    dwin = nc.dram_tensor("winT", [32, D, 128], F32, kind="ExternalInput").ap()
    dwout = nc.dram_tensor("woutT", [NG, 2048, 128], F32, kind="ExternalInput").ap()
    dwx = nc.dram_tensor("wxT", [2048, 96], F32, kind="ExternalInput").ap()
    dwdt = nc.dram_tensor("wdtT", [64, 2048], F32, kind="ExternalInput").ap()
    dalog = nc.dram_tensor("alog", [128, 256], F32, kind="ExternalInput").ap()
    dmcols = nc.dram_tensor("mcols", [128, 144], F32, kind="ExternalInput").ap()
    dgw = nc.dram_tensor("gw", [D, 8], F32, kind="ExternalInput").ap()
    dgb = nc.dram_tensor("gb", [1, 8], F32, kind="ExternalInput").ap()
    dew = nc.dram_tensor("ew", [8, D, D], F32, kind="ExternalInput").ap()
    deb = nc.dram_tensor("eb", [8, D], F32, kind="ExternalInput").ap()
    dconst = {}
    for cn, sh, dt_ in (("c_ident_f32", [128, 128], F32), ("c_ident_bf", [128, 128], BF16),
                        ("c_ones_row", [1, 128], F32), ("c_ones_col", [128, 1], F32),
                        ("c_pm", [128, 1], F32), ("c_revidx", [128, 8], F32),
                        ("c_sel", [8, 1024], F32), ("c_selb", [8, 1024], BF16),
                        ("c_sel16b", [16, 2048], BF16)):
        dconst[cn] = nc.dram_tensor(cn, sh, dt_, kind="ExternalInput").ap()

    es = ExitStack()
    with es:
        p = Prog(nc, es)
        X = p.sbuf("X", [128, NG, T_SEQ], F32)
        T2 = p.sbuf("T2", [128, 1024], F32)
        ident = p.sbuf("ident", [128, 128], F32)
        ident_bf = p.sbuf("ident_bf", [128, 128], BF16)
        ones_row = p.sbuf("ones_row", [1, 128], F32)
        ones_col = p.sbuf("ones_col", [128, 1], F32)
        pm = p.sbuf("pm", [128, 1], F32)
        revidx = p.sbuf("revidx", [128, 8], F32)
        selb = p.sbuf("selb", [8, 1024], BF16)
        cols = p.sbuf("scols", [128, 72], F32)
        ssgn = p.sbuf("ssgn", [128, 8], F32)
        gbs = p.sbuf("gbs", [1, 8], F32)
        r1 = p.sbuf("r1", [1, 512], F32)
        r2 = p.sbuf("r2", [1, 512], F32)
        r3 = p.sbuf("r3", [1, 512], F32)
        lg = p.sbuf("lg", [128, 16, 8], F32)
        nm = p.sbuf("nm", [128, 1], F32)
        rsum = p.sbuf("rsum", [128, 1], F32)
        m1 = p.sbuf("m1", [128, 1], F32)
        sc1 = p.sbuf("sc1", [128, 8], F32)
        sc2 = p.sbuf("sc2", [128, 8], F32)
        AR = Arena(p, "ARENA", 132608)
        import os as _os
        _dbg = _os.environ.get("ARDBG")

        P6 = p.psum("P6", [128, 3072], F32)
        PTf = p.psum("PTf", [128, 512], F32)
        PTb = p.psum("PTb", [128, 1024], BF16)

        ld = p.sem("ld")
        ldg = p.sem("ldg")
        ldw = [p.sem("ldw0"), p.sem("ldw1")]
        w1s = p.sem("w1s")
        w2s = p.sem("w2s")
        lwx = p.sem("lwx")
        lwin = [p.sem("lwin0"), p.sem("lwin1")]
        lwout = [p.sem("lwout0"), p.sem("lwout1")]
        ews2 = [p.sem("ews0"), p.sem("ews1")]
        pe = p.sem("pe")
        ve = p.sv
        se = p.ss
        sg = p.sg
        st2 = [p.sem("stx0"), p.sem("stx1")]
        ldx2 = [p.sem("ldx0"), p.sem("ldx1")]

        bcol = lambda i, go: cols[:, 8 * i + go:8 * i + go + 1]
        CONVW, CONVB, BDT, DP, LNG, LNB = 0, 64, 80, 96, 112, 120
        EPS1r = cols[0:1, 64:65]

        def barrier():
            targets = [pe.now, ve.now, se.now, ld.now, w1s.now, w2s.now, lwx.now,
                       ldw[0].now, ldw[1].now, lwin[0].now, lwin[1].now,
                       lwout[0].now, ews2[0].now, ews2[1].now, ldg.now]
            targets = [t for t in targets if t[1] > 0]
            for eng in ("sync", "gpsimd", "tensor", "vector", "scalar"):
                p.on(eng, lambda e: None, wait=list(targets), n=0)

        for cn, t in (("c_ident_f32", ident), ("c_ident_bf", ident_bf),
                      ("c_ones_row", ones_row), ("c_ones_col", ones_col),
                      ("c_pm", pm), ("c_revidx", revidx),
                      ("c_selb", selb)):
            p.dma("sync", t[:], dconst[cn], ld)
        p.dma("sync", cols[:], dcols, ld)
        p.dma("sync", gbs[:], dgb, ld)
        consts_ld = ld.now
        p.on("vector", lambda e: e.tensor_scalar(ssgn[:], cols[:, 40:48], pm[:], None, ALU.mult),
             wait=[consts_ld], n=1)

        xt2l = [AR.take([128, 1024], F32), AR.take([128, 1024], F32)]
        AR.reset()
        tr_done = {}
        cp_done = {}
        for tt in range(16):
            xt = xt2l[tt % 2]
            wl = [consts_ld] if tt < 2 else [(pe, tr_done[tt - 2])]
            p.dma("sync", xt[:], dxin[tt * 128:(tt + 1) * 128, :], ldx2[tt % 2], wait=wl)
            ldv = ldx2[tt % 2].now
            for half in range(2):
                def trb(e, tt=tt, half=half, xt=xt):
                    ins = None
                    for g4 in range(4):
                        g = half * 4 + g4
                        ins = e.transpose(PTf[:, g4 * 128:(g4 + 1) * 128],
                                          xt[:, g * 128:(g + 1) * 128], ident[:])
                    return ins
                w = [ldv]
                if tt > 0 or half > 0:
                    w.append((ve, cp_done.get((tt, half - 1), cp_done.get((tt - 1, 1), 0))))
                p.on("tensor", trb, wait=w, inc=[(pe, 1)])
                def cpb(e, tt=tt, half=half):
                    ins = None
                    for g4 in range(4):
                        g = half * 4 + g4
                        ins = e.tensor_copy(X[:, g, tt * 128:(tt + 1) * 128],
                                            PTf[:, g4 * 128:(g4 + 1) * 128])
                    return ins
                p.on("vector", cpb, wait=[pe.now], n=4)
                cp_done[(tt, half)] = ve.n
            tr_done[tt] = pe.n
        x_loaded = ve.now

        def lnchunk(src_g, eps_ap, tc, pa0, gate_wait):
            def mmu(e):
                ins = None
                for g in range(NG):
                    ins = e.matmul(P6[0:1, pa0:pa0 + tc], ones_col[:], src_g(g),
                                   start=(g == 0), stop=(g == NG - 1))
                return ins
            p.on("tensor", mmu, wait=gate_wait, inc=[(pe, 1)])
            p.on("vector", lambda e: e.tensor_scalar(r1[0:1, 0:tc], P6[0:1, pa0:pa0 + tc],
                                                     1.0 / D, None, ALU.mult),
                 wait=[pe.now], n=1)
            mu_ev = ve.now
            sq_done = 0
            for g in range(NG):
                w = list(gate_wait) if g < 2 else [(pe, sq_done)]
                p.on("scalar", lambda e, g=g: e.activation(
                    T2[:, (g % 2) * tc:(g % 2) * tc + tc], src_g(g), AF.Square),
                    wait=w + [mu_ev], n=1)
                p.on("tensor", lambda e, g=g: e.matmul(
                    P6[0:1, pa0 + tc:pa0 + 2 * tc], ones_col[:],
                    T2[:, (g % 2) * tc:(g % 2) * tc + tc],
                    start=(g == 0), stop=(g == NG - 1)),
                     wait=[se.now, mu_ev], inc=[(pe, 1)])
                if g >= 1:
                    sq_done = pe.n - 1
            p.on("vector", lambda e: e.tensor_scalar(r2[0:1, 0:tc], P6[0:1, pa0 + tc:pa0 + 2 * tc],
                                                     1.0 / D, None, ALU.mult),
                 wait=[pe.now], n=1)
            p.on("vector", lambda e: e.tensor_tensor(r3[0:1, 0:tc], r1[0:1, 0:tc], r1[0:1, 0:tc],
                                                     ALU.mult), n=1)
            p.on("vector", lambda e: e.tensor_tensor(r3[0:1, 0:tc], r2[0:1, 0:tc], r3[0:1, 0:tc],
                                                     ALU.subtract), n=1)
            p.on("scalar", lambda e: e.activation(r2[0:1, 0:tc], r3[0:1, 0:tc], AF.Sqrt, bias=eps_ap),
                 wait=[ve.now], n=1)
            p.on("vector", lambda e: e.reciprocal(r3[0:1, 0:tc], r2[0:1, 0:tc]), wait=[se.now], n=1)
            p.on("vector", lambda e: e.scalar_tensor_tensor(r1[0:1, 0:tc], r1[0:1, 0:tc], -1.0,
                                                            r3[0:1, 0:tc], ALU.mult, ALU.mult), n=1)
            rows_done = ve.now
            def mbc(e):
                e.matmul(P6[:, pa0:pa0 + tc], ones_row[:], r3[0:1, 0:tc], start=True, stop=True)
                return e.matmul(P6[:, pa0 + tc:pa0 + 2 * tc], ones_row[:], r1[0:1, 0:tc],
                                start=True, stop=True)
            p.on("tensor", mbc, wait=[rows_done], inc=[(pe, 1)])
            return pe.now

        NCB = 16
        Hg = AR.take([128, NG, 512], BF16)
        wbufH = [AR.take([128, 4, D], BF16), AR.take([128, 4, D], BF16)]
        Qg = AR.take([128, NG, 512], BF16)
        Qs = AR.take([128, NG, 512], BF16)
        Kg = AR.take([128, NG, 512], BF16)
        Vg = AR.take([128, NG, 512], BF16)
        e1a = AR.take([128, 512], F32)
        t1a = AR.take([128, 512], F32)
        swp = AR.take([128, 512], F32)
        Vp = AR.take([128, NCB, 65], BF16)
        Pb = AR.take([128, NCB, 512], BF16)
        o_s = AR.take([64, 512], F32)
        rec_s = AR.take([1, 512], F32)
        oatt = AR.take([128, NG, T_SEQ], BF16)

        slot_free = [ve.n] * 2
        slot_free_se = [se.n] * 2
        slot_i = 0
        PROJ = (("q", "wqT", Qg, 0), ("qs", "wqTs", Qs, 6), ("k", "wkT", Kg, 1), ("v", "wvT", Vg, 2))
        last_exp_for_st = {0: 0, 1: 0}
        last_pv_for_p = {0: 0, 1: 0}
        last_oevac = {0: 0, 1: 0}
        last_scatter = 0
        last_pv_all = 0
        wstep_mm_done = {}
        ln_free_a = 0

        for g4 in range(4):
            tg = slice(512 * g4, 512 * (g4 + 1))
            bc1 = lnchunk(lambda g, tg=tg: X[:, g, tg], EPS1r, 512, 0,
                          [x_loaded, (ve, ln_free_a), (ve, max(slot_free)),
                           (se, max(slot_free_se))])
            def hbt(e, tg=tg):
                ins = None
                for g in range(NG):
                    e.tensor_tensor(e1a[:], X[:, g, tg], P6[:, 0:512], ALU.mult)
                    ins = e.tensor_tensor(Hg[:, g, :], e1a[:], P6[:, 512:1024], ALU.add)
                return ins
            p.on("vector", hbt, wait=[bc1], n=16)
            ln_free_a = ve.n
            h_all = ve.now

            for pi, (nmp, wname, out_t, bidx) in enumerate(PROJ):
                ws = g4 * 4 + pi
                for half in range(2):
                    hstep = 2 * ws + half
                    dma_w = [(pe, wstep_mm_done.get(hstep - 2, 0))] if hstep >= 2 else [consts_ld]
                    for j in range(4):
                        jj = half * 4 + j
                        p.dma("gpsimd", wbufH[hstep % 2][:, j, :],
                              dw[wname][jj * 128:(jj + 1) * 128, :], ldw[hstep % 2],
                              wait=dma_w)
                w_ld = [ldw[0].now, ldw[1].now]
                for go in range(NG):
                    r = slot_i % 2
                    slot_i += 1
                    pv_view = P6[:, 512 * r:512 * (r + 1)]
                    def mproj(e, ws=ws, go=go, pv=pv_view):
                        ins = None
                        for j in range(NG):
                            wb = wbufH[(2 * ws + j // 4) % 2]
                            ins = e.matmul(pv, wb[:, j % 4, go * 128:(go + 1) * 128],
                                           Hg[:, j, :], start=(j == 0), stop=(j == NG - 1))
                        return ins
                    p.on("tensor", mproj,
                         wait=[h_all] + w_ld + [(ve, slot_free[r]), (se, slot_free_se[r])],
                         inc=[(pe, 1)])
                    wstep_mm_done[2 * ws] = pe.n
                    wstep_mm_done[2 * ws + 1] = pe.n
                    mm_done = pe.now
                    if nmp == "v":
                        p.on("scalar", lambda e, go=go, pv=pv_view: e.activation(
                            Vg[:, go, :], pv, AF.Identity, bias=bcol(2, go), scale=1.0),
                            wait=[mm_done], n=1)
                        slot_free_se[r] = se.n
                    else:
                        p.on("scalar", lambda e, go=go, pv=pv_view, b=bidx: e.activation(
                            e1a[:], pv, AF.Identity, bias=bcol(b, go), scale=1.0),
                            wait=[mm_done, ve.now], n=1)
                        slot_free_se[r] = se.n
                        p.on("scalar", lambda e, go=go: e.activation(
                            t1a[:], e1a[:], AF.Identity, bias=0.0, scale=cols[:, 32 + go:33 + go]),
                            n=1)
                        p.on("vector", lambda e: e.stream_shuffle(swp[:], e1a[:], SHUF_SWAP),
                             wait=[se.now], n=1)
                        p.on("vector", lambda e, out=out_t, go=go: e.scalar_tensor_tensor(
                            out[:, go, :], swp[:], ssgn[:, go:go + 1], t1a[:], ALU.mult, ALU.add),
                            n=1)
                    slot_free[r] = ve.n
            qkv_done = (ve.now, se.now)

            for hl in range(4):
                hh = g4 * 4 + hl
                go_out = hh // 2
                r0 = (hh % 2) * 64
                p.on("vector", lambda e: e.memset(Vp[:], 1.0),
                     wait=[qkv_done[0], (pe, last_pv_all)], n=1)
                def vtr(e, hl=hl):
                    ins = None
                    for g in range(NG):
                        ins = e.transpose(PTb[:, 128 * g:128 * (g + 1)],
                                          Vg[:, g, hl * 128:(hl + 1) * 128],
                                          ident_bf[:])
                    return ins
                p.on("tensor", vtr, wait=[qkv_done[1], (ve, ve.n)], inc=[(pe, 1)])
                def vcp(e):
                    ins = None
                    for g in range(NG):
                        e.tensor_copy(Vp[:, 2 * g, 0:64], PTb[:, 128 * g:128 * g + 64])
                        ins = e.tensor_copy(Vp[:, 2 * g + 1, 0:64], PTb[:, 128 * g + 64:128 * (g + 1)])
                    return ins
                p.on("vector", vcp, wait=[pe.now], n=16)
                vp_ready = ve.n

                for qc in range(4):
                    gqc = hh * 4 + qc
                    for cbk in range(NCB):
                        stp = cbk % 2
                        stv = P6[:, 1024 + 512 * stp: 1024 + 512 * (stp + 1)]
                        def smm(e, cbk=cbk, qc=qc, hl=hl, stv=stv):
                            ins = None
                            b = 64 * (cbk % 2)
                            hs = slice(hl * 128, (hl + 1) * 128)
                            stat = Kg[b:b + 64, cbk // 2, hs]
                            for j in range(4):
                                cbq = 4 * qc + j
                                if cbq % 2 == cbk % 2:
                                    mov = Qg[b:b + 64, cbq // 2, hs]
                                elif cbk % 2 == 0:
                                    mov = Qs[0:64, cbq // 2, hs]
                                else:
                                    mov = Qs[64:128, (cbq // 2 + 7) % 8, hs]
                                ins = e.matmul(stv[:, j * 128:(j + 1) * 128],
                                               stat, mov, start=True, stop=True)
                            return ins
                        p.on("tensor", smm,
                             wait=[qkv_done[0], (se, last_exp_for_st[stp])], inc=[(pe, 1)])
                        p.on("scalar", lambda e, cbk=cbk, stv=stv, Pb=Pb: e.activation(
                            Pb[:, cbk, :], stv, AF.Exp, bias=0.0, scale=0.125),
                            wait=[pe.now, (pe, last_pv_for_p[0])], n=1)
                        last_exp_for_st[stp] = se.n
                    exps_done = se.now
                    pslot = gqc % 2
                    ov = P6[0:65, 2048 + 512 * pslot: 2048 + 512 * (pslot + 1)]
                    def pvm(e, ov=ov, Pb=Pb):
                        ins = None
                        for cbk in range(NCB):
                            ins = e.matmul(ov, Vp[:, cbk, :], Pb[:, cbk, :],
                                           start=(cbk == 0), stop=(cbk == NCB - 1))
                        return ins
                    p.on("tensor", pvm,
                         wait=[exps_done, (ve, last_oevac[pslot]), (ve, vp_ready)],
                         inc=[(pe, 1)])
                    last_pv_for_p[0] = pe.n
                    last_pv_all = pe.n
                    pv_done = pe.now
                    p.on("vector", lambda e, ov=ov: e.reciprocal(rec_s[:], ov[64:65, :]),
                         wait=[pv_done], n=1)
                    p.on("vector", lambda e, ov=ov: e.tensor_copy(o_s[:], ov[0:64, :]), n=1)
                    oev = ve.now
                    last_oevac[pslot] = ve.n
                    bcv = PTf[0:64, :]
                    p.on("tensor", lambda e: e.matmul(bcv, ones_row[0:1, 0:64], rec_s[:],
                                                      start=True, stop=True),
                         wait=[oev, (ve, last_scatter)], inc=[(pe, 1)])
                    bc_done = pe.now
                    for j in range(4):
                        cbq = 4 * qc + j
                        def sct(e, j=j, cbq=cbq, go_out=go_out, r0=r0):
                            return e.tensor_tensor(
                                oatt[r0:r0 + 64, go_out, cbq:2048:16],
                                o_s[:, j * 128:(j + 1) * 128],
                                bcv[:, j * 128:(j + 1) * 128], ALU.mult)
                        p.on("vector", sct, wait=[bc_done], n=1)
                    last_scatter = ve.n
        attn_done = ve.now

        for half in range(2):
            hstep = 32 + half
            for j in range(4):
                jj = half * 4 + j
                p.dma("gpsimd", wbufH[hstep % 2][:, j, :],
                      dw["woT"][jj * 128:(jj + 1) * 128, :], ldw[hstep % 2],
                      wait=[(pe, wstep_mm_done[hstep - 2])])
        wo_ld = [ldw[0].now, ldw[1].now]
        for go in range(NG):
            for c in range(4):
                r = slot_i % 2
                slot_i += 1
                pv_view = P6[:, 512 * r:512 * (r + 1)]
                def mwo(e, go=go, c=c, pv=pv_view):
                    ins = None
                    for j in range(NG):
                        wb = wbufH[j // 4]
                        ins = e.matmul(pv, wb[:, j % 4, go * 128:(go + 1) * 128],
                                       oatt[:, j, 512 * c:512 * (c + 1)],
                                       start=(j == 0), stop=(j == NG - 1))
                    return ins
                p.on("tensor", mwo,
                     wait=[attn_done] + wo_ld + [(ve, slot_free[r]), (se, slot_free_se[r])],
                     inc=[(pe, 1)])
                p.on("vector", lambda e, go=go, c=c, pv=pv_view: e.scalar_tensor_tensor(
                    X[:, go, 512 * c:512 * (c + 1)], pv, bcol(3, go),
                    X[:, go, 512 * c:512 * (c + 1)], ALU.add, ALU.add),
                    wait=[pe.now], n=1)
                slot_free[r] = ve.n

        if _dbg: print("attn arena:", AR.off)
        barrier()
        AR.reset()

        NF = 11
        H = AR.take([128, NG, T_SEQ], BF16)
        w1b = AR.take([128, NG, 2816], BF16)
        w2b = AR.take([128, NF, D], BF16)
        rwb = AR.take([128, NG, 8], BF16)
        lp = AR.take([128, 16, 8], F32)
        lpT = AR.take([8, T_SEQ], BF16)
        silu_s = AR.take([128, 512], F32)
        g_all = AR.take([128, NF, 512], BF16)

        e1m = lambda: T2[:, 0:512]
        for j in range(NG):
            p.dma("gpsimd", rwb[:, j, :], drw[j * 128:(j + 1) * 128, :], ldg)
        rw_ld = ldg.now

        ln_free = 0
        H_done = {}
        for c in range(4):
            cs = slice(512 * c, 512 * (c + 1))
            bc = lnchunk(lambda g, cs=cs: X[:, g, cs], EPS1r, 512, 0,
                         [(ve, ln_free)] if c > 0 else [ve.now])
            def hbt0(e, cs=cs):
                ins = None
                for g in range(NG):
                    e.tensor_tensor(e1m(), X[:, g, cs], P6[:, 0:512], ALU.mult)
                    ins = e.tensor_tensor(H[:, g, cs], e1m(), P6[:, 512:1024], ALU.add)
                return ins
            p.on("vector", hbt0, wait=[bc], n=16)
            ln_free = ve.n
        h_all0 = ve.now

        rt_free = 0
        for tc in range(16):
            ts = slice(tc * 128, (tc + 1) * 128)
            def mrt(e, ts=ts):
                ins = None
                for j in range(NG):
                    ins = e.matmul(P6[:, 1024:1032], H[:, j, ts], rwb[:, j, :],
                                   start=(j == 0), stop=(j == NG - 1))
                return ins
            p.on("tensor", mrt, wait=[h_all0, rw_ld, (ve, rt_free)], inc=[(pe, 1)])
            pl_done = pe.now
            t = lg[:, tc, :]
            pr_ = lp[:, tc, :]
            p.on("vector", lambda e, t=t: e.tensor_copy(t, P6[:, 1024:1032]), wait=[pl_done], n=1)
            rt_free = ve.n
            p.on("vector", lambda e, t=t: e.reduce_max(nm[:], t, axis=AX.X, negate=True), n=1)
            p.on("scalar", lambda e, t=t: e.activation(t, t, AF.Exp, bias=nm[:], scale=1.0),
                 wait=[ve.now], n=1)
            p.on("vector", lambda e, t=t: e.reduce_sum(rsum[:], t, axis=AX.X), wait=[se.now], n=1)
            p.on("vector", lambda e: e.reciprocal(rsum[:], rsum[:]), n=1)
            p.on("vector", lambda e, t=t: e.tensor_scalar(t, t, rsum[:], None, ALU.mult), n=1)
            p.on("vector", lambda e, t=t: e.reduce_max(m1[:], t, axis=AX.X), n=1)
            p.on("vector", lambda e, t=t: e.tensor_scalar(sc1[:], t, m1[:], None, ALU.is_ge), n=1)
            p.on("vector", lambda e: e.tensor_tensor(sc1[:], sc1[:], revidx[:], ALU.mult), n=1)
            p.on("vector", lambda e: e.reduce_max(m1[:], sc1[:], axis=AX.X), n=1)
            p.on("vector", lambda e: e.tensor_scalar(sc1[:], sc1[:], m1[:], None, ALU.is_ge), n=1)
            p.on("vector", lambda e, t=t: e.scalar_tensor_tensor(sc2[:], sc1[:], -1e9, t,
                                                                 ALU.mult, ALU.add), n=1)
            p.on("vector", lambda e: e.reduce_max(m1[:], sc2[:], axis=AX.X), n=1)
            p.on("vector", lambda e: e.tensor_scalar(sc2[:], sc2[:], m1[:], None, ALU.is_ge), n=1)
            p.on("vector", lambda e: e.tensor_tensor(sc2[:], sc2[:], revidx[:], ALU.mult), n=1)
            p.on("vector", lambda e: e.reduce_max(m1[:], sc2[:], axis=AX.X), n=1)
            p.on("vector", lambda e: e.tensor_scalar(sc2[:], sc2[:], m1[:], None, ALU.is_ge), n=1)
            p.on("vector", lambda e: e.tensor_tensor(sc1[:], sc1[:], sc2[:], ALU.add), n=1)
            p.on("vector", lambda e, t=t, pr_=pr_: e.tensor_tensor(pr_, t, sc1[:], ALU.mult), n=1)
        lp_done = ve.now

        lpt_cp = 0
        for tc in range(16):
            p.on("tensor", lambda e, tc=tc: e.transpose(PTf[0:8, 0:128], lp[:, tc, :], ident[:]),
                 wait=[lp_done, (ve, lpt_cp)], inc=[(pe, 1)])
            p.on("vector", lambda e, tc=tc: e.tensor_copy(lpT[:, tc * 128:(tc + 1) * 128],
                                                          PTf[0:8, 0:128]),
                 wait=[pe.now], n=1)
            lpt_cp = ve.n
        lpt_done = ve.now

        w1_free = 0
        w2_free = 0
        lpb_free = 0
        y_free_se = 0
        y_free_ve = 0
        g_free = {}
        out_free = {0: 0, 1: 0}
        for ei in range(8):
            for j in range(NG):
                p.dma("gpsimd", w1b[:, j, :], dw1[ei, j * 128:(j + 1) * 128, :], w1s,
                      wait=[(pe, w1_free)] if ei > 0 else [lp_done])
            for f in range(NF):
                p.dma("gpsimd", w2b[:, f, :], dw2[ei, f * 128:(f + 1) * 128, :], w2s,
                      wait=[(pe, w2_free)] if ei > 0 else [lp_done])
            w1_ld = w1s.now
            w2_ld = w2s.now
            for c in range(4):
                cs = slice(c * 512, (c + 1) * 512)
                p.on("tensor", lambda e, ei=ei, cs=cs: e.matmul(
                    P6[:, 2048:2560], selb[:, ei * 128:(ei + 1) * 128], lpT[:, cs],
                    start=True, stop=True),
                    wait=[lpt_done, (ve, lpb_free)], inc=[(pe, 1)])
                lpb_done = pe.now
                for f in range(NF):
                    def my(e, f=f, cs=cs):
                        for j in range(NG):
                            e.matmul(P6[:, 0:512], w1b[:, j, f * 128:(f + 1) * 128],
                                     H[:, j, cs], start=(j == 0), stop=(j == NG - 1))
                        ins = None
                        for j in range(NG):
                            ins = e.matmul(P6[:, 512:1024],
                                           w1b[:, j, 1408 + f * 128:1408 + f * 128 + 128],
                                           H[:, j, cs], start=(j == 0), stop=(j == NG - 1))
                        return ins
                    p.on("tensor", my, wait=[w1_ld, (se, y_free_se), (ve, y_free_ve),
                                             (pe, g_free.get(f, 0))],
                         inc=[(pe, 1)])
                    y_done = pe.now
                    p.on("scalar", lambda e: e.activation(silu_s[:], P6[:, 0:512], AF.Silu),
                         wait=[y_done], n=1)
                    y_free_se = se.n
                    p.on("vector", lambda e, f=f: e.tensor_tensor(
                        g_all[:, f, :], silu_s[:], P6[:, 512:1024], ALU.mult),
                        wait=[se.now], n=1)
                    y_free_ve = ve.n
                    p.on("vector", lambda e, f=f: e.tensor_tensor(
                        g_all[:, f, :], g_all[:, f, :], P6[:, 2048:2560], ALU.mult),
                        wait=[lpb_done], n=1)
                g_done = ve.now
                lpb_free = ve.n
                for go in range(NG):
                    ov = P6[:, 1024 + (go % 2) * 512:1024 + (go % 2) * 512 + 512]
                    def mo(e, go=go, ov=ov):
                        ins = None
                        for f in range(NF):
                            ins = e.matmul(ov, w2b[:, f, go * 128:(go + 1) * 128],
                                           g_all[:, f, :],
                                           start=(f == 0), stop=(f == NF - 1))
                        return ins
                    p.on("tensor", mo, wait=[g_done, w2_ld, (ve, out_free[go % 2])],
                         inc=[(pe, 1)])
                    if go == NG - 1:
                        for f in range(NF):
                            g_free[f] = pe.n
                    p.on("vector", lambda e, go=go, cs=cs, ov=ov: e.tensor_tensor(
                        X[:, go, cs], ov, X[:, go, cs], ALU.add),
                        wait=[pe.now], n=1)
                    out_free[go % 2] = ve.n
            w1_free = pe.n
            w2_free = pe.n

        if _dbg: print("moe0 arena:", AR.off)
        barrier()
        AR.reset()

        B_NI, B_NS, B_TC = 16, 16, 256
        B_NCH = T_SEQ // B_TC
        U1 = AR.take([128, B_NI, B_TC], F32)
        ln1 = U1[:, 0:NG, :]
        h_bfm = U1[:, 8:12, :].bitcast(BF16).rearrange("p a b -> p (a b)").rearrange(
            "p (a b) -> p a b", b=B_TC)
        u_s = U1
        SCR = AR.take([128, B_NI, 260], F32)
        tailb = AR.take([128, B_NI, 4], F32)
        sz = AR.take([128, B_NI, B_TC], BF16)
        dt_s = AR.take([128, B_NI, B_TC], F32)
        u_bf = AR.take([128, B_NI, B_TC], BF16)
        h_n = AR.take([128, B_NI, B_TC], F32)
        dbu = U1
        y_acc = AR.take([128, B_NI, B_TC], F32)
        state = AR.take([128, 256], F32)
        dA_t = AR.take([128, 2, B_TC], F32)
        B_s = AR.take([128, 2, B_TC], F32)
        winb = AR.take([128, 2, 8, 128], BF16)
        woutb = AR.take([128, 2, 16, 128], BF16)
        wxb = AR.take([128, B_NI, 96], BF16)
        wdtb = AR.take([64, 2048], BF16)
        dtr_bf = AR.take([64, B_TC], BF16)
        Br = AR.take([16, B_TC], BF16)
        Cr = AR.take([16, B_TC], BF16)
        sel16 = AR.take([16, 2048], BF16)
        A_all = AR.take([128, 256], F32)
        mcols = AR.take([128, 144], F32)

        ccol = lambda base, i: mcols[:, base + i:base + i + 1]
        EPS2r = mcols[0:1, 129:130]
        ONEc = mcols[:, 130:131]
        p.dma("sync", sel16[:], dconst["c_sel16b"], ld)
        p.dma("sync", mcols[:], dmcols, ld)
        p.dma("sync", A_all[:], dalog, ld)
        mld = ld.now
        p.on("scalar", lambda e: e.activation(A_all[:], A_all[:], AF.Exp), wait=[mld], n=1)
        p.on("vector", lambda e: e.tensor_scalar(A_all[:], A_all[:], -1.0, None, ALU.mult),
             wait=[se.now], n=1)

        for i in range(B_NI):
            p.dma("gpsimd", wxb[:, i, :], dwx[i * 128:(i + 1) * 128, :], lwx)
        p.dma("gpsimd", wdtb[:], dwdt, lwx)
        wx_ld = lwx.now
        p.on("vector", lambda e: e.memset(tailb[:], 0.0), n=1)
        p.on("vector", lambda e: e.memset(state[:], 0.0), n=1)

        ln_free = ve.n
        win_mm = {}
        wout_mm = {}
        win_step = 0
        wout_step = 0
        pb_free = {0: 0, 1: 0}
        pb_free_se = {0: 0, 1: 0}
        pc_free = 0
        pd_free_ve = 0
        pd_free_se = 0
        PBv = lambda s: P6[:, 512 + 512 * s:512 + 512 * s + B_TC]
        for c in range(B_NCH):
            cs = slice(c * B_TC, (c + 1) * B_TC)
            bc1 = lnchunk(lambda g, cs=cs: X[:, g, cs], EPS1r, B_TC, 0,
                          [(ve, ln_free)])
            def l1w(e, cs=cs):
                ins = None
                for g in range(NG):
                    e.tensor_tensor(T2[:, 0:B_TC], X[:, g, cs], P6[:, 0:B_TC], ALU.mult)
                    e.tensor_tensor(ln1[:, g], T2[:, 0:B_TC], P6[:, 256:256 + B_TC], ALU.add)
                    ins = e.tensor_tensor(X[:, g, cs], ln1[:, g], X[:, g, cs], ALU.add)
                return ins
            p.on("vector", l1w, wait=[bc1], n=24)
            ln_free = ve.n
            bc2 = lnchunk(lambda g: ln1[:, g], EPS2r, B_TC, 0, [(ve, ln_free)])
            aff_se = {}
            for g in range(NG):
                w2_ = [bc2] if g < 2 else [bc2, (se, aff_se[g - 2])]
                p.on("vector", lambda e, g=g: e.tensor_tensor(
                    T2[:, 0:B_TC] if g % 2 == 0 else T2[:, B_TC:2 * B_TC],
                    ln1[:, g], P6[:, 0:B_TC], ALU.mult), wait=w2_, n=1)
                p.on("vector", lambda e, g=g: e.tensor_tensor(
                    T2[:, 0:B_TC] if g % 2 == 0 else T2[:, B_TC:2 * B_TC],
                    T2[:, 0:B_TC] if g % 2 == 0 else T2[:, B_TC:2 * B_TC],
                    P6[:, 256:256 + B_TC], ALU.add), n=1)
                p.on("scalar", lambda e, g=g: e.activation(
                    h_bfm[:, g, :], T2[:, 0:B_TC] if g % 2 == 0 else T2[:, B_TC:2 * B_TC],
                    AF.Identity, bias=ccol(LNB, g), scale=ccol(LNG, g)),
                    wait=[ve.now], n=1)
                aff_se[g] = se.n
            ln_free = ve.n
            h_done = se.now

            for ei in range(32):
                slot = win_step % 2
                wsem = lwin[slot]
                p.dma("gpsimd", winb[:, slot, :, :],
                      dwin[ei, :, :].rearrange("(j p) k -> p j k", p=128),
                      wsem, wait=[(pe, win_mm.get(win_step - 2, 0))] if win_step >= 2 else [consts_ld])
                win_ld = wsem.now
                pslot = ei % 2
                def mxz2(e, slot=slot, pslot=pslot):
                    ins = None
                    for j in range(NG):
                        ins = e.matmul(PBv(pslot), winb[:, slot, j, :], h_bfm[:, j, :],
                                       start=(j == 0), stop=(j == NG - 1))
                    return ins
                p.on("tensor", mxz2,
                     wait=[h_done, win_ld, (ve, pb_free[pslot]), (se, pb_free_se[pslot])],
                     inc=[(pe, 1)])
                win_mm[win_step] = pe.n
                win_step += 1
                mmd = pe.now
                if ei < 16:
                    p.on("vector", lambda e, ei=ei, pslot=pslot: e.tensor_copy(
                        SCR[:, ei, 4:4 + B_TC], PBv(pslot)), wait=[mmd], n=1)
                    pb_free[pslot] = ve.n
                else:
                    p.on("scalar", lambda e, ei=ei, pslot=pslot: e.activation(
                        sz[:, ei - 16, :], PBv(pslot), AF.Silu), wait=[mmd], n=1)
                    pb_free_se[pslot] = se.n

            for i in range(B_NI):
                p.on("vector", lambda e, i=i: e.tensor_copy(SCR[:, i, 0:4], tailb[:, i, :]), n=1)
            conv_par = {0: 0, 1: 0}
            conv_entry = ve.n
            restores_done = ve.now
            for i in range(B_NI):
                cv = lambda i=i: T2[:, (i % 2) * B_TC:(i % 2) * B_TC + B_TC]
                p.on("scalar", lambda e, i=i: e.activation(
                    cv(i), SCR[:, i, 1:1 + B_TC], AF.Identity,
                    bias=ccol(CONVB, i), scale=ccol(CONVW, i * 4 + 0)),
                    wait=[(ve, conv_par[i % 2] or conv_entry), restores_done], n=1)
                for j in (1, 2, 3):
                    p.on("vector", lambda e, i=i, j=j: e.scalar_tensor_tensor(
                        cv(i), SCR[:, i, 1 + j:1 + j + B_TC], ccol(CONVW, i * 4 + j),
                        cv(i), ALU.mult, ALU.add), wait=[se.now], n=1)
                p.on("scalar", lambda e, i=i: e.activation(u_s[:, i, :], cv(i), AF.Silu),
                     wait=[(ve, ve.n), (pe, win_mm[win_step - 1])], n=1)
                p.on("vector", lambda e, i=i: e.tensor_copy(u_bf[:, i, :], u_s[:, i, :]),
                     wait=[se.now], n=1)
                p.on("vector", lambda e, i=i: e.tensor_copy(tailb[:, i, :], SCR[:, i, B_TC:B_TC + 4]), n=1)
                conv_par[i % 2] = ve.n
            u_done = (ve.now, se.now)

            def mxd(e):
                for i in range(B_NI):
                    e.matmul(P6[0:80, 2048:2048 + B_TC], wxb[:, i, 0:80], u_bf[:, i, :],
                             start=(i == 0), stop=(i == B_NI - 1))
                ins = None
                for i in range(B_NI):
                    ins = e.matmul(P6[0:16, 2304:2304 + B_TC], wxb[:, i, 80:96], u_bf[:, i, :],
                                   start=(i == 0), stop=(i == B_NI - 1))
                return ins
            p.on("tensor", mxd, wait=[u_done[0], wx_ld, (ve, pd_free_ve), (se, pd_free_se)],
                 inc=[(pe, 1)])
            xd_done = pe.now
            p.on("scalar", lambda e: e.activation(dtr_bf[:], P6[0:64, 2048:2048 + B_TC], AF.Identity),
                 wait=[xd_done], n=1)
            pd_free_se = se.n
            p.on("vector", lambda e: e.tensor_copy(Br[:], P6[64:80, 2048:2048 + B_TC]),
                 wait=[xd_done, (se, pd_free_se)], n=1)
            p.on("vector", lambda e: e.tensor_copy(Cr[:], P6[0:16, 2304:2304 + B_TC]), n=1)
            pd_free_ve = ve.n
            xd_ev = (ve.now, se.now)

            for i in range(B_NI):
                pslot = i % 2
                p.on("tensor", lambda e, i=i, pslot=pslot: e.matmul(
                    PBv(pslot), wdtb[:, i * 128:(i + 1) * 128], dtr_bf[:],
                    start=True, stop=True),
                    wait=[xd_ev[1], (ve, pb_free[pslot]), (se, pb_free_se[pslot])],
                    inc=[(pe, 1)])
                mmd = pe.now
                p.on("scalar", lambda e, i=i, pslot=pslot: e.activation(
                    T2[:, 0:B_TC], PBv(pslot), AF.Abs, bias=ccol(BDT, i)),
                    wait=[mmd, (ve, ve.n)], n=1)
                abs_done = se.n
                p.on("scalar", lambda e: e.activation(T2[:, 0:B_TC], T2[:, 0:B_TC], AF.Exp,
                                                      scale=-1.0), n=1)
                p.on("scalar", lambda e: e.activation(T2[:, 0:B_TC], T2[:, 0:B_TC], AF.Ln,
                                                      bias=ONEc), n=1)
                pb_free_se[pslot] = se.n
                p.on("vector", lambda e, i=i, pslot=pslot: e.tensor_scalar(
                    T2[:, B_TC:2 * B_TC], PBv(pslot), ccol(BDT, i), 0.0,
                    ALU.add, ALU.max), wait=[mmd, (se, abs_done)], n=1)
                pb_free[pslot] = ve.n
                p.on("vector", lambda e, i=i: e.tensor_tensor(
                    dt_s[:, i, :], T2[:, B_TC:2 * B_TC], T2[:, 0:B_TC], ALU.add),
                    wait=[se.now], n=1)
            dt_done = ve.now

            p.on("gpsimd", lambda e: e.tensor_tensor(
                SCR[:, :, 0:B_TC], dt_s[:], u_s[:], ALU.mult), wait=[dt_done], n=1)
            du_done = sg.now
            for i in range(B_NI):
                p.on("vector", lambda e, i=i: e.tensor_scalar(
                    y_acc[:, i, :], u_s[:, i, :], ccol(DP, i), None, ALU.mult),
                    wait=[du_done] if i == 0 else [], n=1)
            yinit_done = ve.n
            scan_par = {0: 0, 1: 0}
            bs_free = {0: 0, 1: 0}

            pc_freep = {0: pc_free, 1: pc_free}
            scans_done = ve.n
            scan_half = {0: ve.n, 1: ve.n}
            dbu_gh = {0: 0, 1: 0}
            for n_ in range(B_NS):
                pcb = 1536 + 1024 * (n_ % 2)
                def mbcn(e, n_=n_, pcb=pcb):
                    e.matmul(P6[:, pcb:pcb + B_TC], sel16[:, n_ * 128:(n_ + 1) * 128], Br[:],
                             start=True, stop=True)
                    return e.matmul(P6[:, pcb + 256:pcb + 256 + B_TC],
                                    sel16[:, n_ * 128:(n_ + 1) * 128],
                                    Cr[:], start=True, stop=True)
                p.on("tensor", mbcn, wait=[xd_ev[0], (ve, pc_freep[n_ % 2])], inc=[(pe, 1)])
                bc_done = pe.now
                p.on("scalar", lambda e, pcb=pcb, n_=n_: e.activation(
                    B_s[:, n_ % 2, :], P6[:, pcb:pcb + B_TC], AF.Identity),
                    wait=[bc_done, (sg, bs_free[n_ % 2])], n=1)
                bsc = se.now
                hc = B_NI // 2
                for h in range(2):
                    p.on("gpsimd", lambda e, n_=n_, h=h, hc=hc: e.tensor_tensor(
                        dbu[:, h * hc:(h + 1) * hc, :], SCR[:, h * hc:(h + 1) * hc, 0:B_TC],
                        B_s[:, n_ % 2, :].unsqueeze(1).broadcast_to([128, hc, B_TC]),
                        ALU.mult),
                        wait=[bsc, (ve, scan_half[h]), (ve, yinit_done)], n=1)
                    dbu_gh[h] = sg.n
                bs_free[n_ % 2] = sg.n
                for i in range(B_NI):
                    sl = i % 2
                    p.on("scalar", lambda e, i=i, n_=n_, sl=sl: e.activation(
                        dA_t[:, sl, :], dt_s[:, i, :], AF.Exp,
                        scale=A_all[:, i * 16 + n_:i * 16 + n_ + 1]),
                        wait=[(ve, scan_par[sl]), (ve, dt_done[1])], n=1)
                    p.on("vector", lambda e, i=i, n_=n_, sl=sl: e.tensor_tensor_scan(
                        h_n[:, i, :], dA_t[:, sl, :], dbu[:, i, :],
                        state[:, n_ * 16 + i:n_ * 16 + i + 1], ALU.mult, ALU.add),
                        wait=[se.now, (sg, dbu_gh[i // hc])], n=1)
                    scan_par[sl] = ve.n
                    if i == hc - 1:
                        scan_half[0] = ve.n
                scans_done = ve.n
                scan_half[1] = ve.n
                p.on("vector", lambda e, n_=n_: e.tensor_copy(
                    state[:, n_ * 16:(n_ + 1) * 16], h_n[:, :, B_TC - 1]), n=1)
                p.on("vector", lambda e, pcb=pcb: e.tensor_tensor(
                    h_n[:], h_n[:],
                    P6[:, pcb + 256:pcb + 256 + B_TC].unsqueeze(1).broadcast_to(
                        [128, B_NI, B_TC]),
                    ALU.mult), n=1)
                p.on("vector", lambda e: e.tensor_tensor(y_acc[:], y_acc[:], h_n[:], ALU.add),
                     n=1)
                pc_freep[n_ % 2] = ve.n
            pc_free = ve.n

            for i in range(B_NI):
                p.on("vector", lambda e, i=i: e.tensor_tensor(
                    u_bf[:, i, :], y_acc[:, i, :], sz[:, i, :], ALU.mult), n=1)
            yg_done = ve.now

            for go in range(NG):
                wslot = wout_step % 2
                p.dma("gpsimd", woutb[:, wslot, :, :],
                      dwout[go, :, :].rearrange("(i p) k -> p i k", p=128),
                      lwout[wslot], wait=[(pe, wout_mm.get(wout_step - 2, 0))] if wout_step >= 2 else [consts_ld])
                wo_ldm = lwout[wslot].now
                pslot = go % 2
                def mwom(e, pslot=pslot, wslot=wslot):
                    ins = None
                    for i in range(B_NI):
                        ins = e.matmul(PBv(pslot), woutb[:, wslot, i, :], u_bf[:, i, :],
                                       start=(i == 0), stop=(i == B_NI - 1))
                    return ins
                p.on("tensor", mwom,
                     wait=[yg_done, wo_ldm, (ve, pb_free[pslot]), (se, pb_free_se[pslot])],
                     inc=[(pe, 1)])
                wout_mm[wout_step] = pe.n
                wout_step += 1
                p.on("vector", lambda e, go=go, cs=cs, pslot=pslot: e.tensor_tensor(
                    X[:, go, cs], PBv(pslot), X[:, go, cs], ALU.add),
                    wait=[pe.now], n=1)
                pb_free[pslot] = ve.n

        if _dbg: print("mamba arena:", AR.off)
        barrier()
        AR.reset()

        H1 = AR.take([128, NG, T_SEQ], BF16)
        ewb2m = [AR.take([128, NG, D], BF16), AR.take([128, NG, D], BF16)]
        gwb = AR.take([128, NG, 8], BF16)
        xe2 = [AR.take([128, NG, 512], BF16), AR.take([128, NG, 512], BF16)]
        cbm = AR.take([128, 16, 8], F32)
        cbT = AR.take([8, T_SEQ], BF16)
        ebs = AR.take([8, D], BF16)

        p.dma("gpsimd", ebs[:], deb, ldg)
        eb_ld = ldg.now
        for j in range(NG):
            p.dma("gpsimd", gwb[:, j, :], dgw[j * 128:(j + 1) * 128, :], ldg)
        gw_ld = ldg.now

        ln_free = ve.n
        for c in range(4):
            cs = slice(512 * c, 512 * (c + 1))
            bc = lnchunk(lambda g, cs=cs: X[:, g, cs], EPS1r, 512, 0, [(ve, ln_free)])
            def hbt1(e, cs=cs):
                ins = None
                for g in range(NG):
                    e.tensor_tensor(T2[:, 0:512], X[:, g, cs], P6[:, 0:512], ALU.mult)
                    ins = e.tensor_tensor(H1[:, g, cs], T2[:, 0:512], P6[:, 512:1024], ALU.add)
                return ins
            p.on("vector", hbt1, wait=[bc], n=16)
            ln_free = ve.n
        h_all1 = ve.now

        p.on("tensor", lambda e: e.matmul(P6[:, 1032:1040], ones_row[:], gbs[:],
                                          start=True, stop=True),
             wait=[consts_ld, h_all1], inc=[(pe, 1)])
        gbb_done = pe.now
        rt_free = 0
        for tc in range(16):
            ts = slice(tc * 128, (tc + 1) * 128)
            def mrt1(e, ts=ts):
                ins = None
                for j in range(NG):
                    ins = e.matmul(P6[:, 1024:1032], H1[:, j, ts], gwb[:, j, :],
                                   start=(j == 0), stop=(j == NG - 1))
                return ins
            p.on("tensor", mrt1, wait=[h_all1, gw_ld, (ve, rt_free)], inc=[(pe, 1)])
            pl_done = pe.now
            t = lg[:, tc, :]
            pr_ = cbm[:, tc, :]
            p.on("vector", lambda e, t=t: e.tensor_copy(t, P6[:, 1024:1032]),
                 wait=[pl_done, gbb_done], n=1)
            p.on("vector", lambda e, t=t: e.tensor_tensor(t, t, P6[:, 1032:1040], ALU.add), n=1)
            rt_free = ve.n
            p.on("vector", lambda e, t=t: e.tensor_scalar_max(t, t, 1e-4), n=1)
            p.on("vector", lambda e, t=t: e.reduce_max(nm[:], t, axis=AX.X, negate=True), n=1)
            p.on("scalar", lambda e, t=t: e.activation(t, t, AF.Exp, bias=nm[:], scale=1.0),
                 wait=[ve.now], n=1)
            p.on("vector", lambda e, t=t: e.reduce_sum(rsum[:], t, axis=AX.X), wait=[se.now], n=1)
            p.on("vector", lambda e: e.reciprocal(rsum[:], rsum[:]), n=1)
            p.on("vector", lambda e, t=t: e.tensor_scalar(t, t, rsum[:], None, ALU.mult), n=1)
            p.on("vector", lambda e, t=t: e.reduce_max(m1[:], t, axis=AX.X), n=1)
            p.on("vector", lambda e, t=t: e.tensor_scalar(sc1[:], t, m1[:], None, ALU.is_ge), n=1)
            p.on("vector", lambda e: e.tensor_tensor(sc1[:], sc1[:], revidx[:], ALU.mult), n=1)
            p.on("vector", lambda e: e.reduce_max(m1[:], sc1[:], axis=AX.X), n=1)
            p.on("vector", lambda e: e.tensor_scalar(sc1[:], sc1[:], m1[:], None, ALU.is_ge), n=1)
            p.on("vector", lambda e, t=t: e.scalar_tensor_tensor(sc2[:], sc1[:], -1e9, t,
                                                                 ALU.mult, ALU.add), n=1)
            p.on("vector", lambda e: e.reduce_max(m1[:], sc2[:], axis=AX.X), n=1)
            p.on("vector", lambda e: e.tensor_scalar(sc2[:], sc2[:], m1[:], None, ALU.is_ge), n=1)
            p.on("vector", lambda e: e.tensor_tensor(sc2[:], sc2[:], revidx[:], ALU.mult), n=1)
            p.on("vector", lambda e: e.reduce_max(m1[:], sc2[:], axis=AX.X), n=1)
            p.on("vector", lambda e: e.tensor_scalar(sc2[:], sc2[:], m1[:], None, ALU.is_ge), n=1)
            p.on("vector", lambda e: e.tensor_tensor(sc1[:], sc1[:], sc2[:], ALU.add), n=1)
            p.on("vector", lambda e, t=t: e.tensor_scalar_max(sc2[:], t, 0.1), n=1)
            p.on("vector", lambda e: e.tensor_tensor(sc2[:], sc2[:], sc1[:], ALU.mult), n=1)
            p.on("vector", lambda e: e.reduce_sum(m1[:], sc2[:], axis=AX.X), n=1)
            p.on("vector", lambda e: e.reciprocal(m1[:], m1[:]), n=1)
            p.on("vector", lambda e, pr_=pr_: e.tensor_scalar(pr_, sc2[:], m1[:], 0.5,
                                                              ALU.mult, ALU.mult), n=1)
        cb_done = ve.now

        cbt_cp = 0
        for tc in range(16):
            p.on("tensor", lambda e, tc=tc: e.transpose(PTf[0:8, 0:128], cbm[:, tc, :], ident[:]),
                 wait=[cb_done, (ve, cbt_cp)], inc=[(pe, 1)])
            p.on("vector", lambda e, tc=tc: e.tensor_copy(cbT[:, tc * 128:(tc + 1) * 128],
                                                          PTf[0:8, 0:128]),
                 wait=[pe.now], n=1)
            cbt_cp = ve.n
        cbt_done = ve.now

        out_free = {0: 0, 1: 0}
        for c in range(4):
            cs = slice(c * 512, (c + 1) * 512)
            pslot = c % 2
            for go in range(NG):
                p.on("tensor", lambda e, cs=cs, go=go, pslot=pslot: e.matmul(
                    P6[:, pslot * 512:pslot * 512 + 512],
                    ebs[:, go * 128:(go + 1) * 128], cbT[:, cs],
                    start=True, stop=True),
                    wait=[cbt_done, eb_ld, (ve, out_free[pslot])], inc=[(pe, 1)])
                p.on("vector", lambda e, cs=cs, go=go, pslot=pslot: e.tensor_tensor(
                    X[:, go, cs], P6[:, pslot * 512:pslot * 512 + 512], X[:, go, cs], ALU.add),
                    wait=[pe.now], n=1)
                out_free[pslot] = ve.n
                pslot = (pslot + 1) % 2

        ew_mm_last = {}
        xe_free_pe = {0: 0, 1: 0}
        plb_free = {0: 0, 1: 0}
        for ei in range(8):
            ewb = ewb2m[ei % 2]
            for j in range(NG):
                p.dma("gpsimd", ewb[:, j, :], dew[ei, j * 128:(j + 1) * 128, :], ews2[ei % 2],
                      wait=[(pe, ew_mm_last[ei - 2])] if ei >= 2 else [cb_done])
            ew_ld = ews2[ei % 2].now
            for c in range(4):
                cs = slice(c * 512, (c + 1) * 512)
                cpar = c % 2
                xe = xe2[cpar]
                plv = P6[:, 1024 + 512 * 0:1536] if True else None
                plo = 1024 if cpar == 0 else 2560
                p.on("tensor", lambda e, ei=ei, cs=cs, plo=plo: e.matmul(
                    P6[:, plo:plo + 512], selb[:, ei * 128:(ei + 1) * 128], cbT[:, cs],
                    start=True, stop=True),
                    wait=[cbt_done, (ve, plb_free[cpar])], inc=[(pe, 1)])
                plb_done = pe.now
                def mxe(e, cs=cs, xe=xe, plo=plo):
                    ins = None
                    for g in range(NG):
                        ins = e.tensor_tensor(xe[:, g, :], H1[:, g, cs], P6[:, plo:plo + 512],
                                              ALU.mult)
                    return ins
                p.on("vector", mxe, wait=[plb_done, (pe, xe_free_pe[cpar])], n=8)
                xe_done = ve.now
                plb_free[cpar] = ve.n
                for go in range(NG):
                    pslot = go % 2
                    def mex(e, go=go, pslot=pslot, xe=xe, ewb=ewb):
                        ins = None
                        for j in range(NG):
                            ins = e.matmul(P6[:, pslot * 512:pslot * 512 + 512],
                                           ewb[:, j, go * 128:(go + 1) * 128], xe[:, j, :],
                                           start=(j == 0), stop=(j == NG - 1))
                        return ins
                    p.on("tensor", mex, wait=[xe_done, ew_ld, (ve, out_free[pslot])],
                         inc=[(pe, 1)])
                    if go == NG - 1:
                        xe_free_pe[cpar] = pe.n
                    p.on("vector", lambda e, go=go, cs=cs, pslot=pslot: e.tensor_tensor(
                        X[:, go, cs], P6[:, pslot * 512:pslot * 512 + 512], X[:, go, cs],
                        ALU.add),
                        wait=[pe.now], n=1)
                    out_free[pslot] = ve.n
            ew_mm_last[ei] = pe.n
        res_done = ve.now

        xst2 = [T2, AR.take([128, 1024], F32)]
        st_dma = {}
        st_cp = {}
        for tt in range(16):
            xt = xst2[tt % 2]
            for half in range(2):
                def trs(e, tt=tt, half=half):
                    ins = None
                    for g4 in range(4):
                        g = half * 4 + g4
                        ins = e.transpose(PTf[:, g4 * 128:(g4 + 1) * 128],
                                          X[:, g, tt * 128:(tt + 1) * 128], ident[:])
                    return ins
                w = [res_done]
                if (tt, half) != (0, 0):
                    w.append((ve, st_cp.get((tt, half - 1), st_cp.get((tt - 1, 1), 0))))
                if tt >= 2:
                    w.append(st_dma[tt - 2])
                p.on("tensor", trs, wait=w, inc=[(pe, 1)])
                def cps(e, tt=tt, half=half, xt=xt):
                    ins = None
                    for g4 in range(4):
                        g = half * 4 + g4
                        ins = e.tensor_copy(xt[:, g * 128:(g + 1) * 128],
                                            PTf[:, g4 * 128:(g4 + 1) * 128])
                    return ins
                p.on("vector", cps, wait=[pe.now], n=4)
                st_cp[(tt, half)] = ve.n
            p.dma("sync", dxout[tt * 128:(tt + 1) * 128, :], xt[:, 0:1024], st2[tt % 2],
                  wait=[(ve, st_cp[(tt, 1)])])
            st_dma[tt] = st2[tt % 2].now
        p.on("sync", lambda e: None, wait=[st2[0].now, st2[1].now])
        p.finalize()
    return nc


def host_fused(inputs, core):
    cs = make_consts()
    x = np.ascontiguousarray(np.asarray(inputs["L0_attn_x"][core], dtype=np.float32))
    cos_row = np.asarray(inputs["L0_attn_cos"])[T_SEQ]
    sin_row = np.asarray(inputs["L0_attn_sin"])[T_SEQ]
    ccolh = np.tile(cos_row, 16).reshape(8, 128).T.astype(np.float32)
    scolh = np.tile(sin_row, 16).reshape(8, 128).T.astype(np.float32)
    colsh = np.zeros((128, 72), np.float32)
    for i, bn in enumerate(("bq", "bk", "bv", "bo")):
        colsh[:, 8 * i:8 * (i + 1)] = np.asarray(inputs["L0_attn_" + bn]).reshape(8, 128).T
    colsh[:, 32:40] = ccolh
    colsh[:, 40:48] = scolh
    colsh[:, 48:56] = np.roll(np.asarray(inputs["L0_attn_bq"]), -64).reshape(8, 128).T
    colsh[:, 64] = 1e-5
    wqT = np.ascontiguousarray(np.asarray(inputs["L0_attn_wq"]).T)
    W_in = np.asarray(inputs["L1_attn_W_in"]).astype(np.float32)
    W_out = np.asarray(inputs["L1_attn_W_out"]).astype(np.float32)
    W_x = np.asarray(inputs["L1_attn_W_x"]).astype(np.float32)
    W_dt = np.asarray(inputs["L1_attn_W_dt"]).astype(np.float32)
    winT_t = np.ascontiguousarray(W_in.T.reshape(1024, 32, 128).transpose(1, 0, 2))
    woutT_t = np.ascontiguousarray(W_out.T.reshape(2048, 8, 128).transpose(1, 0, 2))
    conv_w = np.asarray(inputs["L1_attn_conv_w"]).astype(np.float32)
    mcolsh = np.zeros((128, 144), np.float32)
    cw = conv_w[:, 0, :].reshape(16, 128, 4)
    for i in range(16):
        for j in range(4):
            mcolsh[:, i * 4 + j] = cw[i, :, j]
    mcolsh[:, 64:80] = np.asarray(inputs["L1_attn_conv_b"]).reshape(16, 128).T
    mcolsh[:, 80:96] = np.asarray(inputs["L1_attn_b_dt"]).reshape(16, 128).T
    mcolsh[:, 96:112] = np.asarray(inputs["L1_attn_Dp"]).reshape(16, 128).T
    mcolsh[:, 112:120] = np.asarray(inputs["L1_attn_ln_g"]).reshape(8, 128).T
    mcolsh[:, 120:128] = np.asarray(inputs["L1_attn_ln_b"]).reshape(8, 128).T
    mcolsh[:, 128] = 1e-5
    mcolsh[:, 129] = 1e-6
    mcolsh[:, 130] = 1.0
    alog = np.asarray(inputs["L1_attn_A_log"]).astype(np.float32)
    alog_t = np.zeros((128, 256), np.float32)
    for i in range(16):
        for n in range(16):
            alog_t[:, i * 16 + n] = alog[i * 128:(i + 1) * 128, n]
    sel16h = np.concatenate([np.tile(np.eye(16, dtype=np.float32)[:, n:n + 1], (1, 128))
                             for n in range(16)], axis=1).astype(ml_dtypes.bfloat16)
    selh = np.concatenate([np.tile(np.eye(8, dtype=np.float32)[:, e:e + 1], (1, 128))
                           for e in range(8)], axis=1)
    m = {
        "xin": x,
        "wqT": wqT,
        "wqTs": np.ascontiguousarray(np.roll(wqT, -64, axis=1)),
        "wkT": np.ascontiguousarray(np.asarray(inputs["L0_attn_wk"]).T),
        "wvT": np.ascontiguousarray(np.asarray(inputs["L0_attn_wv"]).T),
        "woT": np.ascontiguousarray(np.asarray(inputs["L0_attn_wo"]).T),
        "cols": colsh,
        "rw": np.asarray(inputs["L0_ffn_router_weight"]).astype(np.float32),
        "w1": np.asarray(inputs["L0_ffn_w1"]).astype(np.float32),
        "w2": np.asarray(inputs["L0_ffn_w2"]).astype(np.float32),
        "winT": winT_t,
        "woutT": woutT_t,
        "wxT": np.ascontiguousarray(W_x.T),
        "wdtT": np.ascontiguousarray(W_dt.T),
        "alog": alog_t,
        "mcols": mcolsh,
        "gw": np.asarray(inputs["L1_ffn_gate_W"]).astype(np.float32),
        "gb": np.asarray(inputs["L1_ffn_gate_b"]).astype(np.float32).reshape(1, 8),
        "ew": np.asarray(inputs["L1_ffn_expert_W"]).astype(np.float32),
        "eb": np.asarray(inputs["L1_ffn_expert_b"]).astype(np.float32),
        "c_ident_f32": cs["c_ident_f32"],
        "c_ident_bf": cs["c_ident_bf"],
        "c_ones_row": cs["c_ones_row"],
        "c_ones_col": cs["c_ones_col"],
        "c_pm": cs["c_pm"],
        "c_revidx": cs["c_revidx"],
        "c_sel": selh,
        "c_selb": selh.astype(ml_dtypes.bfloat16),
        "c_sel16b": sel16h,
    }
    return m


    main()


_CACHE = {}


def kernel(**inputs):
    if "nc" not in _CACHE:
        _CACHE["nc"] = build_fused()
    nc = _CACHE["nc"]
    in_maps = [host_fused(inputs, c) for c in range(8)]
    res = run_bass_kernel_spmd(nc, in_maps, core_ids=list(range(8)))
    return np.stack([res.results[c]["xout"] for c in range(8)])
